# version 04
# baseline (speedup 1.0000x reference)
import numpy as np
from contextlib import ExitStack

import concourse.bass as bass
import concourse.mybir as mybir
from concourse.bass_utils import run_bass_kernel_spmd

F32 = mybir.dt.float32
BF16 = mybir.dt.bfloat16
AF = mybir.ActivationFunctionType
ALU = mybir.AluOpType
AX = mybir.AxisListType

D = 1024
T = 2048
FC = D // 128
CTX_M = 4096
CTX_D = 768
CC = CTX_D // 128
L0_HEADS = 8
L1_HEADS = 16
L1_HD = 64
I_FFN = 2816
IC = I_FFN // 128
EXPS = 30.0
L0_QK_SCALE = 1.5 * float(np.float32(128.0 ** -0.5))
L1_QK_SCALE = float(np.float32(L1_HD ** -0.5))
FP4_THR = [0.25, 0.75, 1.25, 1.75, 2.5, 3.5, 5.0]

CHUNK_FRAMES = 100
LOOKAHEAD = 50


def _l1_tmax(q):
    return max((q // CHUNK_FRAMES) * CHUNK_FRAMES + CHUNK_FRAMES - 1, q + LOOKAHEAD)


def l1_tile_classes():
    classes = {}
    partials = []
    for qtb in range(T // 512):
        q0 = qtb * 512
        for ktc in range(T // 128):
            k0 = ktc * 128
            if k0 + 127 <= _l1_tmax(q0):
                classes[(qtb, ktc)] = "full"
            elif k0 > _l1_tmax(q0 + 511):
                classes[(qtb, ktc)] = "zero"
            else:
                classes[(qtb, ktc)] = "partial"
                partials.append((qtb, ktc))
    return classes, partials


def l1_mask_array():
    _, partials = l1_tile_classes()
    q = np.arange(T)
    tmax = np.maximum((q // CHUNK_FRAMES) * CHUNK_FRAMES + CHUNK_FRAMES - 1,
                      q + LOOKAHEAD)
    out = np.zeros((len(partials), 128, 512), np.float32)
    for i, (qtb, ktc) in enumerate(partials):
        ks = ktc * 128 + np.arange(128)
        qs = qtb * 512 + np.arange(512)
        out[i] = (ks[:, None] <= tmax[qs][None, :]).astype(np.float32)
    return out



class Sem:
    def __init__(self, stack, nc, name):
        self.h = stack.enter_context(nc.semaphore(name))
        self.n = 0

    def inc(self, ins, k=1):
        ins.then_inc(self.h, k)
        self.n += k
        return self.n


class Ctx:
    def __init__(self, nc, stack, block):
        self.nc = nc
        self.stack = stack
        self.block = block
        self._sems = {}
        self._cvals = []

    def sem(self, name):
        if name not in self._sems:
            self._sems[name] = Sem(self.stack, self.nc, name)
        return self._sems[name]

    def sb(self, name, shape, dtype):
        return self.stack.enter_context(self.nc.sbuf_tensor(name, shape, dtype))

    def psum(self, name, shape, dtype=F32):
        return self.stack.enter_context(self.nc.psum_tensor(name, shape, dtype))

    def register_consts(self, vals):
        vals = sorted(set(float(v) for v in vals) - {0.0, 1.0})
        self._cvals = vals
        self.consts = self.sb("consts", [128, max(len(vals), 1)], F32)
        for j, v in enumerate(vals):
            self.nc.const_aps.register(F32, v, self.consts[:, j:j + 1])

    def qdma(self, q, sem, out, in_):
        if sem.n > 0:
            q.wait_ge(sem.h, sem.n)
        ins = q.dma_start(out=out, in_=in_)
        sem.inc(ins, 16)
        return ins

    def chain(self, eng, name="chn"):
        sem = self.sem(name)

        def step(ins):
            sem.inc(ins)
            eng.wait_ge(sem.h, sem.n)
            return ins
        return step

    def barrier(self):
        b = self.sem("bar")
        pending = [(sm.h, sm.n) for nm, sm in self._sems.items()
                   if nm != "bar" and sm.n > 0]
        engines = ("sync", "tensor", "vector", "scalar", "gpsimd")
        target = b.n + len(engines)
        for name in engines:
            dec = getattr(self.block, name)

            def _body(e, name=name, target=target):
                if name == "gpsimd":
                    for h, n in pending:
                        e.wait_ge(h, n)
                    e.memset(self.barcell[:, 2:3], 0.0).then_inc(b.h, 1)
                elif name == "vector":
                    e.memset(self.barcell[:, 0:1], 0.0).then_inc(b.h, 1)
                elif name == "scalar":
                    e.memzero(self.barcell[:, 1:2]).then_inc(b.h, 1)
                else:
                    e.nop().then_inc(b.h, 1)
                e.wait_ge(b.h, target)
            dec(_body)
        b.n = target


def setup_common(c: Ctx):
    c.onesb = c.sb("onesb", [128, 1], BF16)
    c.ones1f = c.sb("ones1f", [1, 128], F32)
    c.identb = c.sb("identb", [128, 128], BF16)
    c.identf = c.sb("identf", [128, 128], F32)
    c.scr_small = c.sb("scr_small", [128, 64], F32)
    c.barcell = c.sb("barcell", [1, 4], F32)

    @c.block.vector
    def _(v):
        if c._cvals:
            for j, val in enumerate(c._cvals):
                v.memset(c.consts[:, j:j + 1], val)
        v.memset(c.onesb[:], 1.0)
        v.memset(c.ones1f[:], 1.0)

    @c.block.gpsimd
    def _(g):
        st_ = c.chain(g, "chg")
        for ident in (c.identb, c.identf):
            st_(g.memset(ident[:], 0.0))
            st_(g.affine_select(out=ident[:], in_=ident[:],
                                compare_op=ALU.not_equal, fill=1.0,
                                base=0, pattern=[[-1, 128]],
                                channel_multiplier=1))
    c.barrier()



def emit_mod(c: Ctx, tag, wmod_d, c_d, modT):
    ld = c.sem(f"md{tag}")
    pe = c.sem(f"mp{tag}")
    ac = c.sem(f"ma{tag}")

    crow = c.sb(f"crow{tag}", [128, 8], F32)
    siluf = c.sb(f"siluf{tag}", [128, 8], F32)
    wmf = c.arena_f32[:, 0:6144]
    wm3 = wmod_d.rearrange("(f p) j -> p f j", p=128)

    @c.block.sync
    def _(sync):
        c.qdma(sync, ld, crow[:], c_d)

    @c.block.scalar
    def _(s, ldn=ld.n):
        s.wait_ge(ld.h, ldn)
        s.activation(siluf[:], crow[:], AF.Silu).then_inc(ac.h, 1)
    ac.n += 1

    pe0, ac0 = pe.n, ac.n
    for piece in range(8):
        @c.block.sync
        def _(sync, piece=piece, pe0=pe0):
            if piece >= 1:
                sync.wait_ge(pe.h, pe0 + piece)
            c.qdma(sync, ld, wmf.rearrange("p (f j) -> p f j", f=8),
                   wm3[:, :, piece * 768:(piece + 1) * 768])

        @c.block.tensor
        def _(t, piece=piece, ldn=ld.n, ac0=ac0):
            t.wait_ge(ld.h, ldn)
            t.wait_ge(ac.h, ac0)
            for jloc in range(6):
                jc = piece * 6 + jloc
                for fc in range(FC):
                    ins = t.matmul(
                        c.psX[:, jc:jc + 1],
                        wmf[:, fc * 768 + jloc * 128: fc * 768 + (jloc + 1) * 128],
                        siluf[:, fc:fc + 1],
                        start=(fc == 0), stop=(fc == FC - 1))
            ins.then_inc(pe.h, 1)
    pe.n = pe0 + 8

    @c.block.vector
    def _(v, pen=pe.n):
        v.wait_ge(pe.h, pen)
        st_ = c.chain(v)
        st_(v.tensor_copy(modT[:, 0:48], c.psX[:, 0:48]))
        for col in (16, 40):
            g = modT[:, col:col + 8]
            sgn = c.scr_small[:, 0:8]
            av = c.scr_small[:, 8:16]
            st_(v.tensor_scalar(sgn, g, 0.0, None, ALU.is_ge))
            st_(v.tensor_scalar(sgn, sgn, 2.0, -1.0, ALU.mult, ALU.add))
            st_(v.tensor_scalar(av, g, -1.0, None, ALU.mult))
            st_(v.tensor_tensor(av, av, g, ALU.max))
            st_(v.tensor_scalar(av, av, 0.25, None, ALU.max))
            st_(v.tensor_tensor(g, sgn, av, ALU.mult))
        st_(v.tensor_scalar(modT[:, 48:56], modT[:, 8:16], 1.0, None, ALU.add))
        st_(v.tensor_scalar(modT[:, 56:64], modT[:, 32:40], 1.0, None, ALU.add))
    c.barrier()



def emit_ln(c: Ctx, tag, x_src3, h_dst, modT, onec_col, s_col,
            h_f32_dram3=None, absmax_to=None):
    ld = c.sem(f"ln_ld{tag}")
    pe = c.sem(f"ln_pe{tag}")
    dv = c.sem(f"ln_dv{tag}")
    ac = c.sem(f"ln_ac{tag}")
    st = c.sem(f"ln_st{tag}")

    xf = c.arena_f32[:, 0:4096]
    hf = xf
    xbb = c.arena_bf[:, c.A_SCRATCH:c.A_SCRATCH + 4096]
    xsq = c.arena_bf[:, c.A_SCRATCH + 4096:c.A_SCRATCH + 8192]
    small = c.ln_small

    if absmax_to is not None:
        @c.block.vector
        def _(v):
            v.memset(absmax_to, 0.0)

    for g in range(4):
        @c.block.sync
        def _(sync, g=g):
            c.qdma(sync, ld, xf.rearrange("p (f t) -> p f t", f=8),
                   x_src3[:, :, g * 512:(g + 1) * 512])

        @c.block.vector
        def _(v, ldn=ld.n):
            v.wait_ge(ld.h, ldn)
            ins = v.tensor_copy(xbb, xf)
            dv.inc(ins)

        @c.block.scalar
        def _(s, dvn=dv.n):
            s.wait_ge(dv.h, dvn)
            ins = s.activation(xsq, xbb, AF.Square)
            ac.inc(ins)

        @c.block.tensor
        def _(t, acn=ac.n):
            t.wait_ge(ac.h, acn)
            for fc in range(FC):
                t.matmul(c.psS1[0:1, 0:512], c.onesb[:],
                         xbb[:, fc * 512:(fc + 1) * 512],
                         start=(fc == 0), stop=(fc == FC - 1))
            for fc in range(FC):
                ins = t.matmul(c.psS1[0:1, 512:1024], c.onesb[:],
                               xsq[:, fc * 512:(fc + 1) * 512],
                               start=(fc == 0), stop=(fc == FC - 1))
            pe.inc(ins)

        @c.block.vector
        def _(v, pen=pe.n):
            v.wait_ge(pe.h, pen)
            st_ = c.chain(v)
            m, varr, lv = (small[0:1, 0:512], small[0:1, 512:1024],
                           small[0:1, 1024:1536])
            st_(v.tensor_scalar(m, c.psS1[0:1, 0:512], 1.0 / D, None, ALU.mult))
            st_(v.tensor_scalar(lv, c.psS1[0:1, 512:1024], 1.0 / D, None, ALU.mult))
            st_(v.tensor_tensor(varr, m, m, ALU.mult))
            ins = v.tensor_tensor(varr, lv, varr, ALU.subtract)
            dv.inc(ins)

        @c.block.scalar
        def _(s, dvn=dv.n):
            s.wait_ge(dv.h, dvn)
            st_ = c.chain(s, "chs")
            varr, lv = small[0:1, 512:1024], small[0:1, 1024:1536]
            st_(s.activation(lv, varr, AF.Ln, bias=1e-5))
            ins = s.activation(varr, lv, AF.Exp, scale=-0.5)
            ac.inc(ins)

        @c.block.tensor
        def _(t, acn=ac.n):
            t.wait_ge(ac.h, acn)
            t.matmul(c.psX[:, 0:512], c.ones1f[:], small[0:1, 0:512],
                     start=True, stop=True)
            ins = t.matmul(c.psO[:, 0:512], c.ones1f[:], small[0:1, 512:1024],
                           start=True, stop=True)
            pe.inc(ins)

        @c.block.vector
        def _(v, pen=pe.n):
            v.wait_ge(pe.h, pen)
            st_ = c.chain(v)
            for fc in range(FC):
                hseg = hf[:, fc * 512:(fc + 1) * 512]
                st_(v.tensor_tensor(hseg, xf[:, fc * 512:(fc + 1) * 512],
                                    c.psX[:, 0:512], ALU.subtract))
                if fc < FC - 1:
                    st_(v.tensor_tensor(hseg, hseg, c.psO[:, 0:512], ALU.mult))
                else:
                    ins = v.tensor_tensor(hseg, hseg, c.psO[:, 0:512], ALU.mult)
            dv.inc(ins)

        @c.block.vector
        def _(v, g=g, dvn=dv.n):
            st_ = c.chain(v)
            for fc in range(FC):
                if h_dst is not None:
                    ins = v.tensor_scalar(
                        h_dst[:, fc * T + g * 512: fc * T + g * 512 + 512],
                        hf[:, fc * 512:(fc + 1) * 512],
                        modT[:, onec_col + fc:onec_col + fc + 1],
                        modT[:, s_col + fc:s_col + fc + 1],
                        ALU.mult, ALU.add)
                    if h_f32_dram3 is None and fc < FC - 1:
                        st_(ins)
                if h_f32_dram3 is not None:
                    hstage = c.arena_f32[:, 4096 + (fc % 2) * 512:
                                         4096 + (fc % 2) * 512 + 512]
                    tgt = st.n + 16 * (fc - 1)
                    if tgt > 0:
                        v.wait_ge(st.h, tgt)
                    st_(v.tensor_tensor(
                        hstage, hf[:, fc * 512:(fc + 1) * 512],
                        modT[:, onec_col + fc:onec_col + fc + 1]
                        .broadcast_to((128, 512)), ALU.mult))
                    ins = v.tensor_tensor(
                        hstage, hstage,
                        modT[:, s_col + fc:s_col + fc + 1]
                        .broadcast_to((128, 512)), ALU.add)
                    ins.then_inc(dv.h, 1)
                    if absmax_to is not None:
                        v.wait_ge(dv.h, dvn + fc + 1)
                        tm = c.scr_small[:, 16:17]
                        st_(v.tensor_reduce(tm, hstage, AX.X, ALU.max,
                                            apply_absolute_value=True))
                        st_(v.tensor_tensor(absmax_to, absmax_to, tm, ALU.max))
            if h_f32_dram3 is None:
                ac.inc(ins)

        if h_f32_dram3 is not None:
            dv.n += FC

            @c.block.gpsimd
            def _(gq, g=g, dv0=dv.n - FC, st0=st.n):
                for fc in range(FC):
                    hstage = c.arena_f32[:, 4096 + (fc % 2) * 512:
                                         4096 + (fc % 2) * 512 + 512]
                    gq.wait_ge(dv.h, dv0 + fc + 1)
                    c.qdma(gq, st, h_f32_dram3[:, fc, g * 512:(g + 1) * 512],
                           hstage)



        c.barrier()

    if h_f32_dram3 is not None:
        @c.block.gpsimd
        def _(gq):
            gq.wait_ge(st.h, st.n)
        c.barrier()

    if h_f32_dram3 is not None:
        @c.block.gpsimd
        def _(gq, stn=st.n):
            gq.wait_ge(st.h, stn)
        c.barrier()



def emit_l0_attn(c: Ctx, ctx_d, wq_d, wk_d, wv_d, wout_d, bout_d, x_in3, xs3,
                 modT):
    A = c.A
    ld = c.sem("a2ld")
    pe = c.sem("a2pe")
    ac = c.sem("a2ac")
    dv = c.sem("a2dv")
    st = c.sem("a2st")

    ctxb = c.arena_bf[:, A["CTX"]:A["CTX"] + CC * CTX_M]
    wqb = c.arena_bf[:, A["WQ"]:A["WQ"] + FC * 1024]
    wkb = c.arena_bf[:, A["WK"]:A["WK"] + CC * 1024]
    wvb = c.arena_bf[:, A["WV"]:A["WV"] + CC * 1024]
    h_bf = c.arena_bf[:, A["H"]:A["H"] + FC * T]
    ao = c.arena_bf[:, A["AO"]:A["AO"] + FC * T]
    qh = c.arena_bf[:, A["QH"]:A["QH"] + T]
    kh = c.arena_bf[:, A["KH"]:A["KH"] + CTX_M]
    vh = c.arena_bf[:, A["VH"]:A["VH"] + CTX_M]
    pt = c.arena_bf[:, A["P"]:A["P"] + 2048]
    stage0 = c.arena_f32[:, 0:512]
    stage1 = c.arena_f32[:, 512:1024]
    stage2 = c.arena_f32[:, 1024:1536]
    stage3 = c.arena_f32[:, 1536:2048]
    smalls = c.ln_small

    @c.block.gpsimd
    def _(g):
        c.qdma(g, ld, ctxb.rearrange("p (c m) -> p c m", c=CC),
               ctx_d.rearrange("(c p) m -> p c m", p=128))
        c.qdma(g, ld, wqb.rearrange("p (f o) -> p f o", f=FC),
               wq_d.rearrange("(f p) o -> p f o", p=128))
        c.qdma(g, ld, wkb.rearrange("p (f o) -> p f o", f=CC),
               wk_d.rearrange("(f p) o -> p f o", p=128))
        c.qdma(g, ld, wvb.rearrange("p (f o) -> p f o", f=CC),
               wv_d.rearrange("(f p) o -> p f o", p=128))

    @c.block.tensor
    def _(t, ldn=ld.n):
        t.wait_ge(ld.h, ldn)
    c.barrier()

    for head in range(L0_HEADS):
        pe0, ac0 = pe.n, ac.n
        slots = [c.psX, c.psO]

        @c.block.tensor
        def _(t, head=head, pe0=pe0, ac0=ac0):
            gi = 0
            for qtb in range(4):
                if gi >= 2:
                    t.wait_ge(ac.h, ac0 + gi - 1)
                for fc in range(FC):
                    ins = t.matmul(
                        slots[gi % 2][:, 0:512],
                        wqb[:, fc * 1024 + head * 128: fc * 1024 + head * 128 + 128],
                        h_bf[:, fc * T + qtb * 512: fc * T + qtb * 512 + 512],
                        start=(fc == 0), stop=(fc == FC - 1))
                ins.then_inc(pe.h, 1)
                gi += 1
            for kb in range(8):
                t.wait_ge(ac.h, ac0 + gi - 1)
                for cc in range(CC):
                    ins = t.matmul(
                        slots[gi % 2][:, 0:512],
                        wkb[:, cc * 1024 + head * 128: cc * 1024 + head * 128 + 128],
                        ctxb[:, cc * CTX_M + kb * 512: cc * CTX_M + kb * 512 + 512],
                        start=(cc == 0), stop=(cc == CC - 1))
                ins.then_inc(pe.h, 1)
                gi += 1
            for vg in range(8):
                t.wait_ge(ac.h, ac0 + gi - 1)
                for j in range(4):
                    ktc = vg * 4 + j
                    for cc in range(CC):
                        ins = t.matmul(
                            slots[gi % 2][:, j * 128:(j + 1) * 128],
                            ctxb[:, cc * CTX_M + ktc * 128: cc * CTX_M + ktc * 128 + 128],
                            wvb[:, cc * 1024 + head * 128: cc * 1024 + head * 128 + 128],
                            start=(cc == 0), stop=(cc == CC - 1))
                ins.then_inc(pe.h, 1)
                gi += 1
        pe.n = pe0 + 20

        @c.block.scalar
        def _(s, pe0=pe0):
            gi = 0
            for qtb in range(4):
                s.wait_ge(pe.h, pe0 + gi + 1)
                s.copy(qh[:, qtb * 512:(qtb + 1) * 512],
                       slots[gi % 2][:, 0:512]).then_inc(ac.h, 1)
                gi += 1
            for kb in range(8):
                s.wait_ge(pe.h, pe0 + gi + 1)
                s.copy(kh[:, kb * 512:(kb + 1) * 512],
                       slots[gi % 2][:, 0:512]).then_inc(ac.h, 1)
                gi += 1
            for vg in range(8):
                s.wait_ge(pe.h, pe0 + gi + 1)
                s.copy(vh[:, vg * 512:(vg + 1) * 512],
                       slots[gi % 2][:, 0:512]).then_inc(ac.h, 1)
                gi += 1
        ac.n = ac0 + 20
        c.barrier()

        for qtb in range(4):
            pe0, ac0, dv0 = pe.n, ac.n, dv.n

            @c.block.tensor
            def _(t, head=head, qtb=qtb, pe0=pe0, ac0=ac0, dv0=dv0):
                def scores(i):
                    slot = c.psS1 if i % 2 == 0 else c.psS2
                    if i >= 2:
                        t.wait_ge(ac.h, ac0 + i - 1)
                    for j in range(2):
                        ktc = i * 2 + j
                        ins = t.matmul(slot[:, j * 512:(j + 1) * 512],
                                       kh[:, ktc * 128:(ktc + 1) * 128],
                                       qh[:, qtb * 512:(qtb + 1) * 512],
                                       start=True, stop=True)
                    ins.then_inc(pe.h, 1)

                def pv(i):
                    t.wait_ge(ac.h, ac0 + i + 1)
                    pslot = pt[:, (i % 2) * 1024:(i % 2) * 1024 + 1024]
                    for j in range(2):
                        ktc = i * 2 + j
                        t.matmul(c.psO[:, 0:512],
                                 vh[:, ktc * 128:(ktc + 1) * 128],
                                 pslot[:, j * 512:(j + 1) * 512],
                                 start=(ktc == 0), stop=(ktc == 31))
                        ins = t.matmul(c.psSum[0:1, 0:512], c.onesb[:],
                                       pslot[:, j * 512:(j + 1) * 512],
                                       start=(ktc == 0), stop=(ktc == 31))
                    ins.then_inc(pe.h, 1)

                scores(0)
                scores(1)
                for i in range(14):
                    pv(i)
                    scores(i + 2)
                pv(14)
                pv(15)
                t.wait_ge(dv.h, dv0 + 1)
                t.matmul(c.psX[:, 0:512], c.ones1f[:],
                         smalls[0:1, 512:1024], start=True, stop=True
                         ).then_inc(pe.h, 1)

            pe.n = pe0 + 33

            @c.block.scalar
            def _(s, head=head, qtb=qtb, pe0=pe0, ac0=ac0):
                for i in range(16):
                    s.wait_ge(pe.h, pe0 + (i + 1 if i < 2 else 2 * i))
                    slot = c.psS1 if i % 2 == 0 else c.psS2
                    s.activation(pt[:, (i % 2) * 1024:(i % 2) * 1024 + 1024],
                                 slot[:, 0:1024], AF.Exp,
                                 scale=L0_QK_SCALE, bias=-EXPS
                                 ).then_inc(ac.h, 1)
                s.wait_ge(pe.h, pe0 + 32)
                s.copy(smalls[0:1, 0:512], c.psSum[0:1, 0:512]).then_inc(ac.h, 1)
                s.copy(stage0[:], c.psO[:, 0:512]).then_inc(ac.h, 1)
            ac.n = ac0 + 18

            @c.block.vector
            def _(v, head=head, qtb=qtb, ac0=ac0, pe0=pe0):
                v.wait_ge(ac.h, ac0 + 17)
                v.reciprocal(smalls[0:1, 512:1024], smalls[0:1, 0:512]
                             ).then_inc(dv.h, 1)
                v.wait_ge(ac.h, ac0 + 18)
                v.wait_ge(pe.h, pe0 + 33)
                v.tensor_tensor(
                    ao[:, head * T + qtb * 512: head * T + qtb * 512 + 512],
                    stage0[:], c.psX[:, 0:512], ALU.mult).then_inc(dv.h, 1)
            dv.n = dv0 + 2
            c.barrier()

    woutb = wqb
    bout_t = c.sb("bout_t", [128, 8], F32)
    gab = c.scr_small[:, 24:32]

    @c.block.gpsimd
    def _(g):
        c.qdma(g, ld, woutb.rearrange("p (f o) -> p f o", f=FC),
               wout_d.rearrange("(f p) o -> p f o", p=128))
        c.qdma(g, ld, bout_t[:], bout_d)

    @c.block.vector
    def _(v, ldn=ld.n):
        v.wait_ge(ld.h, ldn)
        v.tensor_tensor(gab, modT[:, 16:24], bout_t[:], ALU.mult).then_inc(dv.h, 1)
    dv.n += 1
    c.barrier()

    pe0, ac0, dv0, st0 = pe.n, ac.n, dv.n, st.n
    lds = c.sem("a2lds")
    ntile = 32

    def tile_oc_qtb(i):
        return i % 8, i // 8

    @c.block.sync
    def _(sync, dv0=dv0):
        for i in range(ntile):
            oc, qtb = tile_oc_qtb(i)
            if i >= 2:
                sync.wait_ge(dv.h, dv0 + i - 1)
            c.qdma(sync, lds, (stage1 if i % 2 == 0 else stage2),
                   x_in3[:, oc, qtb * 512:(qtb + 1) * 512])

    @c.block.tensor
    def _(t, pe0=pe0, dv0=dv0):
        for i in range(ntile):
            oc, qtb = tile_oc_qtb(i)
            slot = c.psS1 if i % 2 == 0 else c.psS2
            if i >= 2:
                t.wait_ge(dv.h, dv0 + i - 1)
            for fc in range(FC):
                ins = t.matmul(
                    slot[:, 0:512],
                    woutb[:, fc * 1024 + oc * 128: fc * 1024 + oc * 128 + 128],
                    ao[:, fc * T + qtb * 512: fc * T + qtb * 512 + 512],
                    start=(fc == 0), stop=(fc == FC - 1))
            ins.then_inc(pe.h, 1)
    pe.n = pe0 + ntile

    @c.block.vector
    def _(v, pe0=pe0, dv0=dv0, st0=st0):
        st_ = c.chain(v)
        for i in range(ntile):
            oc, qtb = tile_oc_qtb(i)
            slot = c.psS1 if i % 2 == 0 else c.psS2
            v.wait_ge(pe.h, pe0 + i + 1)
            v.wait_ge(lds.h, 16 * (i + 1))
            if i >= 1:
                v.wait_ge(st.h, st0 + 16 * i)
            st_(v.tensor_scalar(stage0[:], slot[:, 0:512],
                                modT[:, 16 + oc:16 + oc + 1],
                                gab[:, oc:oc + 1],
                                ALU.mult, ALU.add))
            v.tensor_tensor(stage3[:],
                            (stage1 if i % 2 == 0 else stage2),
                            stage0[:], ALU.add).then_inc(dv.h, 1)
    dv.n = dv0 + ntile

    @c.block.gpsimd
    def _(g, dv0=dv0, st0=st0):
        for i in range(ntile):
            oc, qtb = tile_oc_qtb(i)
            g.wait_ge(dv.h, dv0 + i + 1)
            c.qdma(g, st, xs3[:, oc, qtb * 512:(qtb + 1) * 512], stage3[:])
    c.barrier()



def _fp4_half(c, eng, st_, src, dst, utile, qtile, tmp, rs, hs, o, w,
              has_stt=True):
    b0, wb = o // 16, w // 16
    s_ = lambda a: a[:, o:o + w]
    s3 = lambda a: a[:, o:o + w].rearrange("p (n b) -> p n b", b=16)
    r3 = lambda a: (a[:, b0:b0 + wb].rearrange("p (n o) -> p n o", o=1)
                    .broadcast_to((128, wb, 16)))
    if has_stt:
        st_(eng.scalar_tensor_tensor(s_(utile), s_(src), -1.0, s_(src),
                                     ALU.mult, ALU.max))
    else:
        st_(eng.tensor_scalar(s_(tmp), s_(src), 0.0, None, ALU.is_ge))
        st_(eng.tensor_scalar(s_(tmp), s_(tmp), 2.0, -1.0, ALU.mult, ALU.add))
        st_(eng.tensor_tensor(s_(utile), s_(src), s_(tmp), ALU.mult))
    st_(eng.tensor_tensor(s3(utile), s3(utile), r3(rs), ALU.mult))
    st_(eng.tensor_scalar(s_(qtile), s_(utile), FP4_THR[0], None, ALU.is_gt))
    for thr in FP4_THR[1:4]:
        st_(eng.tensor_scalar(s_(tmp), s_(utile), thr, None, ALU.is_gt))
        st_(eng.tensor_tensor(s_(qtile), s_(qtile), s_(tmp), ALU.add))
    for thr, wt in ((FP4_THR[4], 2.0), (FP4_THR[5], 2.0), (FP4_THR[6], 4.0)):
        st_(eng.tensor_scalar(s_(tmp), s_(utile), thr, None, ALU.is_gt))
        if has_stt:
            st_(eng.scalar_tensor_tensor(s_(qtile), s_(tmp), wt, s_(qtile),
                                         ALU.mult, ALU.add))
        else:
            st_(eng.tensor_scalar(s_(tmp), s_(tmp), wt, None, ALU.mult))
            st_(eng.tensor_tensor(s_(qtile), s_(qtile), s_(tmp), ALU.add))
    st_(eng.tensor_scalar(s_(tmp), s_(src), 0.0, None, ALU.is_ge))
    st_(eng.tensor_scalar(s_(tmp), s_(tmp), 2.0, -1.0, ALU.mult, ALU.add))
    st_(eng.tensor_tensor(s_(qtile), s_(qtile), s_(tmp), ALU.mult))
    return eng.tensor_tensor(s3(dst), s3(qtile), r3(hs), ALU.mult)


def emit_fp4_quant_dual(c, src, dst, W, utile, qtile, tmp, qsm,
                        vwaits=(), gwaits=(), dv=None, gq=None):
    nb = W // 16
    hw = (W // 32) * 16
    bm = qsm[:, 0:nb]
    sc = qsm[:, nb:2 * nb]
    rs = qsm[:, 2 * nb:3 * nb]
    hs = qsm[:, 3 * nb:4 * nb]
    smalls_at = dv.n + 1
    gq_prev = gq.n

    @c.block.vector
    def _(v, vwaits=tuple(vwaits), smalls_at=smalls_at, gq_prev=gq_prev):
        st_ = c.chain(v)
        for h, n in vwaits:
            v.wait_ge(h, n)
        if gq_prev:
            v.wait_ge(gq.h, gq_prev)
        st_(v.tensor_reduce(bm, src.rearrange("p (n b) -> p n b", b=16),
                            AX.X, ALU.max, apply_absolute_value=True))
        st_(v.tensor_scalar(sc, bm, 1.0 / 6.0, 1e-12, ALU.mult, ALU.max))
        st_(v.tensor_scalar(hs, sc, 0.5, None, ALU.mult))
        v.reciprocal(rs, sc).then_inc(dv.h, 1)
        v.wait_ge(dv.h, smalls_at)
        ins = _fp4_half(c, v, st_, src, dst, utile, qtile, tmp, rs, hs, 0, hw)
        ins.then_inc(dv.h, 1)
    dv.n += 2

    @c.block.gpsimd
    def _(g, gwaits=tuple(gwaits), smalls_at=smalls_at):
        st_ = c.chain(g, "chgq")
        for h, n in gwaits:
            g.wait_ge(h, n)
        g.wait_ge(dv.h, smalls_at)
        ins = _fp4_half(c, g, st_, src, dst, utile, qtile, tmp, rs, hs,
                        hw, W - hw, has_stt=False)
        ins.then_inc(gq.h, 1)
    gq.n += 1



A3 = dict(WG=0, WU=22528, WD=45056, H2Q=67584, HQT=78848, H2QTM=79872,
          QS1=82688, QS2=85504, QS3=88320)


def emit_l0_ffn(c: Ctx, wg_d, wu_d, wd_d, xs3, h0scr3, modT):
    ld = c.sem("f0ld")
    pe = c.sem("f0pe")
    ac = c.sem("f0ac")
    dv = c.sem("f0dv")
    st = c.sem("f0st")

    wgf = c.arena_bf[:, A3["WG"]:A3["WG"] + FC * I_FFN]
    wuf = c.arena_bf[:, A3["WU"]:A3["WU"] + FC * I_FFN]
    wdf = c.arena_bf[:, A3["WD"]:A3["WD"] + IC * 1024]
    h2q = c.arena_bf[:, A3["H2Q"]:A3["H2Q"] + IC * 512]
    hqt = c.arena_bf[:, A3["HQT"]:A3["HQT"] + 1024]
    h2qtm = c.arena_bf[:, A3["H2QTM"]:A3["H2QTM"] + I_FFN]
    qs1 = c.arena_bf[:, A3["QS1"]:A3["QS1"] + I_FFN]
    qs2 = c.arena_bf[:, A3["QS2"]:A3["QS2"] + I_FFN]
    qs3 = c.arena_bf[:, A3["QS3"]:A3["QS3"] + 1792]
    hx = c.arena_f32[:, 0:1024]
    htm = c.arena_f32[:, 1024:2048]
    sg = c.arena_f32[:, 2048:2560]
    h2f = c.arena_f32[:, 2560:5376]
    qsm = c.arena_f32[:, 6416:6416 + 704]
    xw = c.arena_f32[:, 0:512]
    gq = c.sem("f0gq")

    tps = c.sem("f0tps")
    tpe = c.sem("f0tpe")
    wsrcs = (c.arena_f32[:, 0:2816], c.arena_f32[:, 2816:5632])
    chl = []
    for wi, (wd_src, wdst, nchunk, wwidth) in enumerate(
            ((wg_d, wgf, IC, 1024), (wu_d, wuf, IC, 1024), (wd_d, wdf, FC, I_FFN))):
        w3 = wd_src.rearrange("(n p) k -> p n k", p=128)
        for ch in range(nchunk):
            chl.append((w3, ch, wdst, wwidth, wi))
    prevw = {}
    for k, (w3, ch, wdst, wwidth, wi) in enumerate(chl):
        wsrc = wsrcs[k % 2]
        @c.block.sync
        def _(sync, w3=w3, ch=ch, wwidth=wwidth, wsrc=wsrc,
              pw=prevw.get(k % 2)):
            if pw is not None:
                sync.wait_ge(dv.h, pw[0])
                sync.wait_ge(gq.h, pw[1])
            c.qdma(sync, ld, wsrc[:, 0:wwidth], w3[:, ch, :])

        emit_fp4_quant_dual(c, wsrc[:, 0:wwidth], qs2[:, 0:wwidth], wwidth,
                            qs1[:, 0:wwidth], h2qtm[:, 0:wwidth],
                            h2q[:, 0:wwidth], qsm,
                            vwaits=[(ld.h, ld.n), (tpe.h, tpe.n)],
                            gwaits=[(ld.h, ld.n), (tpe.h, tpe.n)],
                            dv=dv, gq=gq)
        prevw[k % 2] = (dv.n, gq.n)

        ntr = wwidth // 128
        ngrp = (ntr + 3) // 4
        @c.block.tensor
        def _(t, ntr=ntr, ngrp=ngrp, dvn=dv.n, gqn=gq.n, tps0=tps.n,
              tpe0=tpe.n):
            t.wait_ge(dv.h, dvn)
            t.wait_ge(gq.h, gqn)
            for g4 in range(ngrp):
                if tps0 + g4 >= 1:
                    t.wait_ge(tpe.h, tps0 + g4)
                for j in range(min(4, ntr - g4 * 4)):
                    trc = g4 * 4 + j
                    ins = t.transpose(
                        c.psT[:, j * 128:(j + 1) * 128],
                        qs2[:, trc * 128:(trc + 1) * 128], c.identb[:])
                ins.then_inc(tps.h, 1)

        @c.block.scalar
        def _(s, ch=ch, wdst=wdst, ntr=ntr, ngrp=ngrp, wi=wi, tps0=tps.n):
            for g4 in range(ngrp):
                s.wait_ge(tps.h, tps0 + g4 + 1)
                nj = min(4, ntr - g4 * 4)
                for j in range(nj):
                    trc = g4 * 4 + j
                    stride = I_FFN if wi < 2 else 1024
                    dst = wdst[:, trc * stride + ch * 128:
                               trc * stride + ch * 128 + 128]
                    ins = s.copy(dst, c.psT[:, j * 128:(j + 1) * 128])
                ins.then_inc(tpe.h, 1)
        tps.n += ngrp
        tpe.n += ngrp
    c.barrier()

    IBW = [512, 512, 512, 512, 512, 256]
    for tb in range(4):
        for tci in range(4):
            tc = tb * 4 + tci
            @c.block.sync
            def _(sync, tc=tc, dvn=dv.n):
                sync.wait_ge(dv.h, dvn)
                c.qdma(sync, ld, hx.rearrange("p (f t) -> p f t", f=8),
                       h0scr3[:, :, tc * 128:(tc + 1) * 128])

            @c.block.tensor
            def _(t, ldn=ld.n, acn=ac.n):
                t.wait_ge(ld.h, ldn)
                t.wait_ge(ac.h, acn)
                for j in range(8):
                    slot = c.psX if j < 4 else c.psO
                    ins = t.transpose(slot[:, (j % 4) * 128:(j % 4) * 128 + 128],
                                      hx[:, j * 128:(j + 1) * 128], c.identf[:])
                ins.then_inc(pe.h, 1)
            pe.n += 1

            @c.block.scalar
            def _(s, pen=pe.n):
                s.wait_ge(pe.h, pen)
                st2 = c.chain(s, "chs")
                st2(s.copy(htm[:, 0:512], c.psX[:, 0:512]))
                s.copy(htm[:, 512:1024], c.psO[:, 0:512]).then_inc(ac.h, 1)
            ac.n += 1
            c.barrier()

            emit_fp4_quant_dual(c, htm[:, 0:1024], qs2[:, 0:1024], 1024,
                                qs1[:, 0:1024], h2qtm[:, 0:1024],
                                qs3[:, 0:1024], qsm,
                                vwaits=[(ac.h, ac.n)], gwaits=[(ac.h, ac.n)],
                                dv=dv, gq=gq)
            c.barrier()

            @c.block.tensor
            def _(t, dvn=dv.n, gqn=gq.n, tpe0=tpe.n):
                t.wait_ge(dv.h, dvn)
                t.wait_ge(gq.h, gqn)
                for g4 in range(2):
                    t.wait_ge(tpe.h, tpe0 + g4)
                    for j in range(4):
                        trc = g4 * 4 + j
                        ins = t.transpose(c.psT[:, j * 128:(j + 1) * 128],
                                          qs2[:, trc * 128:(trc + 1) * 128],
                                          c.identb[:])
                    ins.then_inc(tps.h, 1)
            tps.n += 2

            @c.block.scalar
            def _(s, tps0=tps.n - 2):
                for g4 in range(2):
                    s.wait_ge(tps.h, tps0 + g4 + 1)
                    for j in range(4):
                        trc = g4 * 4 + j
                        ins = s.copy(hqt[:, trc * 128:(trc + 1) * 128],
                                     c.psT[:, j * 128:(j + 1) * 128])
                    ins.then_inc(tpe.h, 1)
            tpe.n += 2
            c.barrier()

            off = 0
            for ib, ibw in enumerate(IBW):
                @c.block.tensor
                def _(t, ib=ib, ibw=ibw, off=off, tpen=tpe.n, dvn=dv.n):
                    t.wait_ge(tpe.h, tpen)
                    if ib >= 1:
                        t.wait_ge(dv.h, dvn)
                    for fc in range(FC):
                        t.matmul(c.psS2[:, 0:ibw],
                                 hqt[:, fc * 128:(fc + 1) * 128],
                                 wgf[:, fc * I_FFN + off: fc * I_FFN + off + ibw],
                                 start=(fc == 0), stop=(fc == FC - 1))
                    for fc in range(FC):
                        ins = t.matmul(c.psS2[:, 512:512 + ibw],
                                       hqt[:, fc * 128:(fc + 1) * 128],
                                       wuf[:, fc * I_FFN + off: fc * I_FFN + off + ibw],
                                       start=(fc == 0), stop=(fc == FC - 1))
                    ins.then_inc(pe.h, 1)
                pe.n += 1

                @c.block.scalar
                def _(s, ibw=ibw, pen=pe.n):
                    s.wait_ge(pe.h, pen)
                    s.activation(sg[:, 0:ibw], c.psS2[:, 0:ibw], AF.Silu
                                 ).then_inc(ac.h, 1)
                ac.n += 1

                @c.block.vector
                def _(v, ibw=ibw, off=off, acn=ac.n):
                    v.wait_ge(ac.h, acn)
                    v.tensor_tensor(h2f[:, off:off + ibw], sg[:, 0:ibw],
                                    c.psS2[:, 512:512 + ibw], ALU.mult
                                    ).then_inc(dv.h, 1)
                dv.n += 1
                off += ibw
                c.barrier()

            emit_fp4_quant_dual(c, h2f[:, 0:I_FFN], h2qtm[:, 0:I_FFN], I_FFN,
                                qs1[:, 0:I_FFN], qs2[:, 0:I_FFN],
                                h2qtm[:, 0:I_FFN], qsm,
                                vwaits=[(dv.h, dv.n)], gwaits=[(dv.h, dv.n)],
                                dv=dv, gq=gq)
            c.barrier()

            @c.block.tensor
            def _(t, dvn=dv.n, gqn=gq.n, tpe0=tpe.n):
                t.wait_ge(dv.h, dvn)
                t.wait_ge(gq.h, gqn)
                for g4 in range(6):
                    t.wait_ge(tpe.h, tpe0 + g4)
                    for j in range(min(4, 22 - g4 * 4)):
                        trc = g4 * 4 + j
                        ins = t.transpose(c.psT[:, j * 128:(j + 1) * 128],
                                          h2qtm[:, trc * 128:(trc + 1) * 128],
                                          c.identb[:])
                    ins.then_inc(tps.h, 1)
            tps.n += 6

            @c.block.scalar
            def _(s, tci=tci, tps0=tps.n - 6):
                for g4 in range(6):
                    s.wait_ge(tps.h, tps0 + g4 + 1)
                    for j in range(min(4, 22 - g4 * 4)):
                        trc = g4 * 4 + j
                        ins = s.copy(h2q[:, trc * 512 + tci * 128:
                                        trc * 512 + tci * 128 + 128],
                                     c.psT[:, j * 128:(j + 1) * 128])
                    ins.then_inc(tpe.h, 1)
            tpe.n += 6
            c.barrier()

        for oc in range(FC):
            @c.block.tensor
            def _(t, oc=oc, tpen=tpe.n, dvn=dv.n):
                t.wait_ge(tpe.h, tpen)
                if oc >= 2:
                    t.wait_ge(dv.h, dvn - 1)
                slot = c.psS1[:, 0:512] if oc % 2 == 0 else c.psS1[:, 512:1024]
                for ic in range(IC):
                    ins = t.matmul(slot,
                                   wdf[:, ic * 1024 + oc * 128:
                                       ic * 1024 + oc * 128 + 128],
                                   h2q[:, ic * 512:(ic + 1) * 512],
                                   start=(ic == 0), stop=(ic == IC - 1))
                ins.then_inc(pe.h, 1)
            pe.n += 1

            @c.block.sync
            def _(sync, tb=tb, oc=oc, dvn=dv.n):
                if oc >= 1:
                    sync.wait_ge(dv.h, dvn)
                c.qdma(sync, ld, xw[:, 0:512],
                       xs3[:, oc, tb * 512:(tb + 1) * 512])

            @c.block.vector
            def _(v, oc=oc, tb=tb, pen=pe.n, ldn=ld.n, stn=st.n):
                st_ = c.chain(v)
                v.wait_ge(pe.h, pen)
                v.wait_ge(ld.h, ldn)
                v.wait_ge(st.h, stn)
                slot = c.psS1[:, 0:512] if oc % 2 == 0 else c.psS1[:, 512:1024]
                esg = c.arena_f32[:, 5904:6416]
                st_(v.tensor_scalar(esg, slot,
                                    modT[:, 40 + oc:40 + oc + 1], None, ALU.mult))
                v.tensor_tensor(h2f[:, 0:512], xw[:, 0:512], esg,
                                ALU.add).then_inc(dv.h, 1)
            dv.n += 1

            @c.block.gpsimd
            def _(g, oc=oc, tb=tb, dvn=dv.n):
                g.wait_ge(dv.h, dvn)
                c.qdma(g, st, xs3[:, oc, tb * 512:(tb + 1) * 512],
                       h2f[:, 0:512])
            c.barrier()
    c.barrier()



A5 = dict(WQ=0, WK=8192, WV=16384, WO=24576, MASK=32768, QH=77824, KH=79872,
          VP=81920, VP2=82960, PM=84000)


def emit_l1_attn(c: Ctx, wq_d, wk_d, wv_d, wo_d, bq_d, bv_d, bo_d, mask_d,
                 n_par, xs3, modT):
    A = c.A
    ld = c.sem("a5ld")
    pe = c.sem("a5pe")
    ac = c.sem("a5ac")
    dv = c.sem("a5dv")
    st = c.sem("a5st")

    wqb = c.arena_bf[:, A5["WQ"]:A5["WQ"] + FC * 1024]
    wkb = c.arena_bf[:, A5["WK"]:A5["WK"] + FC * 1024]
    wvb = c.arena_bf[:, A5["WV"]:A5["WV"] + FC * 1024]
    wob = c.arena_bf[:, A5["WO"]:A5["WO"] + FC * 1024]
    maskb = c.arena_bf[:, A5["MASK"]:A5["MASK"] + n_par * 512]
    h_bf = c.arena_bf[:, A["H"]:A["H"] + FC * T]
    ao = c.arena_bf[:, A["AO"]:A["AO"] + FC * T]
    qh2 = c.arena_bf[:, A5["QH"]:A5["QH"] + T]
    kh2 = c.arena_bf[:, A5["KH"]:A5["KH"] + T]
    vps = [c.arena_bf[:, A5["VP"]:A5["VP"] + 16 * 65],
           c.arena_bf[:, A5["VP2"]:A5["VP2"] + 16 * 65]]
    ptm = c.arena_bf[:, A5["PM"]:A5["PM"] + 512]
    pt = c.arena_bf[:, A["P"]:A["P"] + 2048]
    stage0 = c.arena_f32[:, 0:512]
    stage1 = c.arena_f32[:, 512:1024]
    stage2 = c.arena_f32[:, 1024:1536]
    stage3 = c.arena_f32[:, 1536:2048]
    smalls = c.ln_small

    bq_t = c.sb("bq_t", [128, 8], F32)
    bv_t = c.sb("bv_t", [128, 8], F32)
    bo_t = c.sb("bo_t", [128, 8], F32)
    bqs = c.sb("bqs", [128, 8], F32)
    gab1 = c.scr_small[:, 40:48]

    classes, partials = l1_tile_classes()
    pidx = {tt_: i for i, tt_ in enumerate(partials)}

    @c.block.gpsimd
    def _(g):
        for wd, dst in ((wq_d, wqb), (wk_d, wkb), (wv_d, wvb), (wo_d, wob)):
            c.qdma(g, ld, dst.rearrange("p (f o) -> p f o", f=FC),
                   wd.rearrange("(f p) o -> p f o", p=128))
        c.qdma(g, ld, maskb.rearrange("p (n q) -> p n q", n=n_par),
               mask_d.rearrange("(n p) q -> p n q", p=128))

    lds = c.sem("a5lds")

    @c.block.sync
    def _(sync):
        c.qdma(sync, lds, bq_t[:], bq_d)
        c.qdma(sync, lds, bv_t[:], bv_d)
        c.qdma(sync, lds, bo_t[:], bo_d)

    @c.block.vector
    def _(v, ldn=ld.n, ldsn=lds.n):
        v.wait_ge(ld.h, ldn)
        v.wait_ge(lds.h, ldsn)
        st_ = c.chain(v)
        st_(v.tensor_scalar(bqs[:], bq_t[:], L1_QK_SCALE, None, ALU.mult))
        st_(v.tensor_tensor(gab1, modT[:, 16:24], bo_t[:], ALU.mult))
        st_(v.memset(vps[0].rearrange("p (n w) -> p n w", w=65)[:, :, 64:65], 1.0))
        ins = v.memset(vps[1].rearrange("p (n w) -> p n w", w=65)[:, :, 64:65], 1.0)
        dv.inc(ins)
    c.barrier()

    for hp in range(L1_HEADS // 2):
        pe0, ac0, dv0 = pe.n, ac.n, dv.n

        @c.block.tensor
        def _(t, hp=hp, pe0=pe0, dv0=dv0):
            for qtb in range(4):
                if qtb >= 1:
                    t.wait_ge(dv.h, dv0 + 2 * qtb - 1)
                for fc in range(FC):
                    ins = t.matmul(
                        c.psX[:, 0:512],
                        wqb[:, fc * 1024 + hp * 128: fc * 1024 + hp * 128 + 128],
                        h_bf[:, fc * T + qtb * 512: fc * T + qtb * 512 + 512],
                        start=(fc == 0), stop=(fc == FC - 1))
                ins.then_inc(pe.h, 1)
                if qtb >= 1:
                    t.wait_ge(dv.h, dv0 + 2 * qtb)
                for fc in range(FC):
                    ins = t.matmul(
                        c.psO[:, 0:512],
                        wkb[:, fc * 1024 + hp * 128: fc * 1024 + hp * 128 + 128],
                        h_bf[:, fc * T + qtb * 512: fc * T + qtb * 512 + 512],
                        start=(fc == 0), stop=(fc == FC - 1))
                ins.then_inc(pe.h, 1)
            for tg in range(4):
                if tg >= 1:
                    t.wait_ge(dv.h, dv0 + 8 + tg)
                for j in range(4):
                    tc = tg * 4 + j
                    for fc in range(FC):
                        ins = t.matmul(
                            c.psS1[:, j * 256:j * 256 + 128],
                            h_bf[:, fc * T + tc * 128: fc * T + tc * 128 + 128],
                            wvb[:, fc * 1024 + hp * 128: fc * 1024 + hp * 128 + 128],
                            start=(fc == 0), stop=(fc == FC - 1))
                ins.then_inc(pe.h, 1)

        @c.block.vector
        def _(v, hp=hp, pe0=pe0):
            st_ = c.chain(v)
            for qtb in range(4):
                v.wait_ge(pe.h, pe0 + 2 * qtb + 1)
                v.tensor_scalar(qh2[:, qtb * 512:(qtb + 1) * 512],
                                c.psX[:, 0:512], L1_QK_SCALE,
                                bqs[:, hp:hp + 1],
                                ALU.mult, ALU.add).then_inc(dv.h, 1)
                v.wait_ge(pe.h, pe0 + 2 * qtb + 2)
                v.tensor_copy(kh2[:, qtb * 512:(qtb + 1) * 512],
                              c.psO[:, 0:512]).then_inc(dv.h, 1)
            for tg in range(4):
                v.wait_ge(pe.h, pe0 + 8 + tg + 1)
                for j in range(4):
                    tc = tg * 4 + j
                    st_(v.tensor_copy(vps[0][:, tc * 65:tc * 65 + 64],
                                      c.psS1[:, j * 256:j * 256 + 64]))
                    ins = v.tensor_copy(vps[1][:, tc * 65:tc * 65 + 64],
                                        c.psS1[:, j * 256 + 64:j * 256 + 128])
                ins.then_inc(dv.h, 1)
        pe.n += 12
        dv.n += 12
        c.barrier()

        sc = c.sem("a5sc")
        pvs = c.sem("a5pv")
        bc = c.sem("a5bc")
        for sub in range(2):
          head = 2 * hp + sub
          hb = sub * 64
          vpp = vps[sub]
          for qtb in range(4):
            allowed = [k for k in range(16) if classes[(qtb, k)] != "zero"]
            groups = [allowed[i:i + 2] for i in range(0, len(allowed), 2)]
            ng = len(groups)
            ac0, dv0 = ac.n, dv.n
            sc0, pv0, bc0 = sc.n, pvs.n, bc.n

            @c.block.tensor
            def _(t, head=head, hb=hb, vpp=vpp, qtb=qtb, ac0=ac0, dv0=dv0,
                  sc0=sc0, pv0=pv0, groups=groups, ng=ng):
                def scores(i):
                    slot = c.psS1 if i % 2 == 0 else c.psS2
                    if i >= 2:
                        t.wait_ge(ac.h, ac0 + i - 1)
                    for j, ktc in enumerate(groups[i]):
                        ins = t.matmul(slot[:, j * 512:(j + 1) * 512],
                                       kh2[hb:hb + 64,
                                           ktc * 128:(ktc + 1) * 128],
                                       qh2[hb:hb + 64,
                                           qtb * 512:(qtb + 1) * 512],
                                       start=True, stop=True)
                    ins.then_inc(sc.h, 1)

                def pv(i):
                    t.wait_ge(dv.h, dv0 + i + 1)
                    pslot = pt[:, (i % 2) * 1024:(i % 2) * 1024 + 1024]
                    for j, ktc in enumerate(groups[i]):
                        first = (i == 0 and j == 0)
                        last = (i == ng - 1 and j == len(groups[i]) - 1)
                        ins = t.matmul(c.psO[0:65, 0:512],
                                       vpp[:, ktc * 65:(ktc + 1) * 65],
                                       pslot[:, j * 512:(j + 1) * 512],
                                       start=first, stop=last)
                    ins.then_inc(pvs.h, 1)

                scores(0)
                if ng > 1:
                    scores(1)
                for i in range(ng - 2):
                    pv(i)
                    scores(i + 2)
                if ng > 1:
                    pv(ng - 2)
                pv(ng - 1)
                t.wait_ge(dv.h, dv0 + ng + 1)
                t.matmul(c.psX[:, 0:512], c.ones1f[:],
                         smalls[0:1, 512:1024], start=True, stop=True
                         ).then_inc(bc.h, 1)
            sc.n = sc0 + ng
            pvs.n = pv0 + ng
            bc.n = bc0 + 1

            @c.block.scalar
            def _(s, qtb=qtb, ac0=ac0, sc0=sc0, pv0=pv0, groups=groups, ng=ng):
                for i in range(ng):
                    s.wait_ge(sc.h, sc0 + i + 1)
                    if i >= 2:
                        s.wait_ge(pvs.h, pv0 + i - 1)
                    slot = c.psS1 if i % 2 == 0 else c.psS2
                    w = 512 * len(groups[i])
                    s.activation(pt[:, (i % 2) * 1024:(i % 2) * 1024 + w],
                                 slot[:, 0:w], AF.Exp, bias=-EXPS
                                 ).then_inc(ac.h, 1)
                s.wait_ge(pvs.h, pv0 + ng)
                s.copy(stage0[0:65, :], c.psO[0:65, 0:512]).then_inc(ac.h, 1)
            ac.n = ac0 + ng + 1

            @c.block.vector
            def _(v, head=head, hb=hb, qtb=qtb, ac0=ac0, dv0=dv0, bc0=bc0,
                  groups=groups, ng=ng):
                st_ = c.chain(v)
                for i in range(ng):
                    v.wait_ge(ac.h, ac0 + i + 1)
                    for j, ktc in enumerate(groups[i]):
                        if classes[(qtb, ktc)] == "partial":
                            pi = pidx[(qtb, ktc)]
                            seg = pt[:, (i % 2) * 1024 + j * 512:
                                     (i % 2) * 1024 + j * 512 + 512]
                            st_(v.tensor_tensor(
                                ptm[:, 0:512], seg,
                                maskb[:, pi * 512:(pi + 1) * 512], ALU.mult))
                            st_(v.tensor_copy(seg, ptm[:, 0:512]))
                    ins = v.memset(c.barcell[:, 3:4], 0.0)
                    dv.inc(ins)
                    v.wait_ge(dv.h, dv.n)
                v.wait_ge(ac.h, ac0 + ng + 1)
                ins = v.reciprocal(smalls[0:1, 512:1024], stage0[64:65, 0:512])
                dv.inc(ins)
                v.wait_ge(bc.h, bc0 + 1)
                st_(v.tensor_tensor(stage1[0:64, :], stage0[0:64, :],
                                    c.psX[0:64, 0:512], ALU.mult))
                ins = v.tensor_scalar(
                    ao[hb:hb + 64,
                       (head // 2) * T + qtb * 512:
                       (head // 2) * T + qtb * 512 + 512],
                    stage1[0:64, :], 1.0,
                    bv_t[hb:hb + 64, head // 2:head // 2 + 1],
                    ALU.mult, ALU.add)
                dv.inc(ins)
            c.barrier()

    pe0, ac0, dv0, st0, ld0 = pe.n, ac.n, dv.n, st.n, ld.n
    lds0 = lds.n
    ntile = 32

    def tile_oc_qtb(i):
        return i % 8, i // 8

    @c.block.sync
    def _(sync, dv0=dv0, lds0=lds0):
        for i in range(ntile):
            oc, qtb = tile_oc_qtb(i)
            if i >= 2:
                sync.wait_ge(dv.h, dv0 + i - 1)
            c.qdma(sync, lds, (stage1 if i % 2 == 0 else stage2),
                   xs3[:, oc, qtb * 512:(qtb + 1) * 512])

    @c.block.tensor
    def _(t, pe0=pe0, dv0=dv0):
        for i in range(ntile):
            oc, qtb = tile_oc_qtb(i)
            slot = c.psS1 if i % 2 == 0 else c.psS2
            if i >= 2:
                t.wait_ge(dv.h, dv0 + i - 1)
            for fc in range(FC):
                ins = t.matmul(
                    slot[:, 0:512],
                    wob[:, fc * 1024 + oc * 128: fc * 1024 + oc * 128 + 128],
                    ao[:, fc * T + qtb * 512: fc * T + qtb * 512 + 512],
                    start=(fc == 0), stop=(fc == FC - 1))
            ins.then_inc(pe.h, 1)
    pe.n = pe0 + ntile

    @c.block.vector
    def _(v, pe0=pe0, dv0=dv0, st0=st0, lds0=lds0):
        st_ = c.chain(v)
        for i in range(ntile):
            oc, qtb = tile_oc_qtb(i)
            slot = c.psS1 if i % 2 == 0 else c.psS2
            v.wait_ge(pe.h, pe0 + i + 1)
            v.wait_ge(lds.h, lds0 + 16 * (i + 1))
            if i >= 1:
                v.wait_ge(st.h, st0 + 16 * i)
            st_(v.tensor_scalar(stage0[:, 0:512], slot[:, 0:512],
                                modT[:, 16 + oc:16 + oc + 1],
                                gab1[:, oc:oc + 1],
                                ALU.mult, ALU.add))
            v.tensor_tensor(stage3[:],
                            (stage1 if i % 2 == 0 else stage2),
                            stage0[:, 0:512], ALU.add).then_inc(dv.h, 1)
    dv.n = dv0 + ntile

    @c.block.gpsimd
    def _(g, dv0=dv0, st0=st0):
        for i in range(ntile):
            oc, qtb = tile_oc_qtb(i)
            g.wait_ge(dv.h, dv0 + i + 1)
            c.qdma(g, st, xs3[:, oc, qtb * 512:(qtb + 1) * 512], stage3[:])
    c.barrier()


LA_CONSTS = [L0_QK_SCALE, -EXPS, 1e-5, -0.5, L1_QK_SCALE]
A_MAP = dict(CTX=0, WQ=24576, WK=32768, WV=38912, H=45056, AO=61440,
             QH=77824, KH=79872, VH=83968, P=88064)


def build_launchA(upto="attn1", dbg=()):
    nc = bass.Bass("TRN2", target_bir_lowering=False)
    dram = {}

    def din(name, shape):
        dram[name] = nc.dram_tensor(name, shape, F32, kind="ExternalInput").ap()
        return dram[name]

    def dout(name, shape):
        dram[name] = nc.dram_tensor(name, shape, F32, kind="ExternalOutput").ap()
        return dram[name]

    x_d = din("x", [D, T])
    ctx_d = din("ctx", [CTX_D, CTX_M])
    wq_d = din("l0wq", [D, D])
    wk_d = din("l0wk", [CTX_D, D])
    wv_d = din("l0wv", [CTX_D, D])
    wout_d = din("l0wo", [D, D])
    bout_d = din("l0bout", [128, 8])
    c0_d = din("c0", [128, 8])
    wmod0_d = din("wmod0", [D, 6 * D])
    x1_d = dout("x1", [D, T])
    wg_d = din("l0wg", [I_FFN, D])
    wu_d = din("l0wu", [I_FFN, D])
    wd_d = din("l0wd", [D, I_FFN])
    _, _partials = l1_tile_classes()
    NPAR = len(_partials)
    l1wq_d = din("l1wq", [D, D])
    l1wk_d = din("l1wk", [D, D])
    l1wv_d = din("l1wv", [D, D])
    l1wo_d = din("l1wo", [D, D])
    l1bq_d = din("l1bq", [128, 8])
    l1bv_d = din("l1bv", [128, 8])
    l1bo_d = din("l1bo", [128, 8])
    l1mask_d = din("l1mask", [NPAR * 128, 512])
    c1_d = din("c1", [128, 8])
    wmod1_d = din("wmod1", [D, 6 * D])
    want_hb = any(n == "dbg_hb" for n, _ in dbg)
    h0scr_d = nc.dram_tensor("h0scr", [D, T], F32,
                             kind=("ExternalOutput" if want_hb else "Internal")).ap()
    if want_hb:
        dram["h0scr"] = h0scr_d
    full = (upto == "full")
    if full:
        h1_d = dout("h1", [D, T])
        lm_d = dout("lm", [1, 1])
        modg_d = dout("modg", [128, 8])

    dbg_outs = {}
    for name, shape in dbg:
        if name == "dbg_hb":
            continue
        dbg_outs[name] = dout(name, shape)

    with ExitStack() as stack:
        c = Ctx(nc, stack, None)
        c.register_consts(LA_CONSTS)
        c.arena_bf = c.sb("arena_bf", [128, 90112], BF16)[:]
        c.arena_f32 = c.sb("arena_f32", [128, 7168], F32)[:]
        c.ln_small = c.arena_f32[0:1, 5120:6656]
        c.psS1 = c.psum("psS1", [128, 1024])[:]
        c.psS2 = c.psum("psS2", [128, 1024])[:]
        c.psO = c.psum("psO", [128, 512])[:]
        c.psX = c.psum("psX", [128, 512])[:]
        c.psSum = c.psum("psSum", [1, 512])[:]
        c.psT = c.psum("psT", [128, 1024], BF16)[:]
        c.modT0 = c.sb("modT0", [128, 64], F32)[:]
        c.modT1 = c.sb("modT1", [128, 64], F32)[:]
        c.A = A_MAP
        c.A_SCRATCH = 0
        block = stack.enter_context(nc.Block())
        c.block = block

        setup_common(c)
        emit_mod(c, "0", wmod0_d, c0_d, c.modT0)
        x_in3 = x_d.rearrange("(f p) t -> p f t", p=128)
        xs3 = x1_d.rearrange("(f p) t -> p f t", p=128)
        emit_ln(c, "a0", x_in3, c.arena_bf[:, A_MAP["H"]:A_MAP["H"] + FC * T],
                c.modT0, 48, 0)
        if upto == "ln0":
            pass
        else:
            emit_l0_attn(c, ctx_d, wq_d, wk_d, wv_d, wout_d, bout_d,
                         x_in3, xs3, c.modT0)
        h0scr3 = h0scr_d.rearrange("(f p) t -> p f t", p=128)
        if upto in ("ffn0", "lnb0", "attn1", "full"):
            emit_ln(c, "b0", xs3, None, c.modT0, 56, 24, h_f32_dram3=h0scr3)
        if upto in ("ffn0", "attn1", "full"):
            emit_l0_ffn(c, wg_d, wu_d, wd_d, xs3, h0scr3, c.modT0)
        if upto in ("attn1", "full"):
            emit_mod(c, "1", wmod1_d, c1_d, c.modT1)
            emit_ln(c, "a1", xs3, c.arena_bf[:, A_MAP["H"]:A_MAP["H"] + FC * T],
                    c.modT1, 48, 0)
            emit_l1_attn(c, l1wq_d, l1wk_d, l1wv_d, l1wo_d, l1bq_d, l1bv_d,
                         l1bo_d, l1mask_d, NPAR, xs3, c.modT1)
        if full:
            am = c.scr_small[:, 18:19]
            h13 = h1_d.rearrange("(f p) t -> p f t", p=128)
            emit_ln(c, "b1", xs3, None, c.modT1, 56, 24,
                    h_f32_dram3=h13, absmax_to=am)
            fin2 = c.sem("fin2")

            @c.block.tensor
            def _(t):
                t.transpose(c.psX[0:1, 0:128], am, c.identf[:]
                            ).then_inc(fin2.h, 1)

            @c.block.vector
            def _(v):
                v.wait_ge(fin2.h, 1)
                v.tensor_reduce(c.scr_small[0:1, 20:21], c.psX[0:1, 0:128],
                                AX.X, ALU.max).then_inc(fin2.h, 1)

            @c.block.gpsimd
            def _(g):
                g.wait_ge(fin2.h, 2)
                c.qdma(g, c.sem("finst"), lm_d, c.scr_small[0:1, 20:21])
                c.qdma(g, c.sem("finst"), modg_d, c.modT1[:, 40:48])
            c.barrier()

        fin = c.sem("fin")

        @c.block.gpsimd
        def _(g):
            for name in dbg_outs:
                if name == "dbg_h":
                    c.qdma(g, fin, dbg_outs[name],
                           c.arena_bf[:, A_MAP["H"]:A_MAP["H"] + FC * T])
                elif name == "dbg_modT":
                    c.qdma(g, fin, dbg_outs[name], c.modT0)
                elif name == "dbg_ao":
                    c.qdma(g, fin, dbg_outs[name],
                           c.arena_bf[:, A_MAP["AO"]:A_MAP["AO"] + FC * T])
                elif name == "dbg_small":
                    c.qdma(g, fin, dbg_outs[name], c.ln_small)
            g.wait_ge(fin.h, fin.n)
        c.barrier()
    return nc



def _prep_common(inp):
    out = {}
    ctx = np.asarray(inp["L0_attn_context"], np.float32).reshape(CTX_M, CTX_D)
    out["ctx"] = np.ascontiguousarray(ctx.T)
    out["l0wq"] = np.ascontiguousarray(np.asarray(inp["L0_attn_Wq"], np.float32))
    out["l0wk"] = np.ascontiguousarray(np.asarray(inp["L0_attn_Wk"], np.float32))
    out["l0wv"] = np.ascontiguousarray(np.asarray(inp["L0_attn_Wv"], np.float32))
    out["l0wo"] = np.ascontiguousarray(np.asarray(inp["L0_attn_Wout"], np.float32))
    out["l0bout"] = np.ascontiguousarray(
        np.asarray(inp["L0_attn_bout"], np.float32).reshape(8, 128).T)
    out["wmod0"] = np.ascontiguousarray(np.asarray(inp["L0_wmod"], np.float32))
    out["l0wg"] = np.ascontiguousarray(np.asarray(inp["L0_ffn_w_gate"], np.float32))
    out["l0wu"] = np.ascontiguousarray(np.asarray(inp["L0_ffn_w_up"], np.float32))
    out["l0wd"] = np.ascontiguousarray(np.asarray(inp["L0_ffn_w_down"], np.float32))
    out["l1wq"] = np.ascontiguousarray(np.asarray(inp["L1_attn_Wq"], np.float32))
    out["l1wk"] = np.ascontiguousarray(np.asarray(inp["L1_attn_Wk"], np.float32))
    out["l1wv"] = np.ascontiguousarray(np.asarray(inp["L1_attn_Wv"], np.float32))
    out["l1wo"] = np.ascontiguousarray(np.asarray(inp["L1_attn_Wo"], np.float32))
    for nm, key in (("l1bq", "L1_attn_bq"), ("l1bv", "L1_attn_bv"),
                    ("l1bo", "L1_attn_bo")):
        out[nm] = np.ascontiguousarray(
            np.asarray(inp[key], np.float32).reshape(8, 128).T)
    out["l1mask"] = l1_mask_array().reshape(-1, 512)
    out["wmod1"] = np.ascontiguousarray(np.asarray(inp["L1_wmod"], np.float32))
    return out


def _prep_core(inp, core):
    out = {}
    out["x"] = np.ascontiguousarray(
        np.asarray(inp["L0_attn_x"][core], np.float32).T)
    out["c0"] = np.ascontiguousarray(
        np.asarray(inp["L0_c"][core], np.float32).reshape(8, 128).T)
    out["c1"] = np.ascontiguousarray(
        np.asarray(inp["L1_c"][core], np.float32).reshape(8, 128).T)
    return out



def _emit_round_halfup(c, v, vtile, itile, d1, d2, W, out, clamp):
    st_ = c.chain(v)
    st_(v.tensor_copy(itile, vtile))
    st_(v.tensor_tensor(d1, vtile, itile, ALU.subtract))
    st_(v.tensor_scalar(d2, d1, 0.5, None, ALU.is_ge))
    st_(v.tensor_tensor(d2, d2, itile, ALU.add))
    st_(v.tensor_scalar(d1, d1, -0.5, None, ALU.is_lt))
    if clamp:
        st_(v.tensor_tensor(d2, d2, d1, ALU.subtract))
        return v.tensor_scalar(out, d2, -128.0, 127.0, ALU.max, ALU.min)
    return v.tensor_tensor(out, d2, d1, ALU.subtract)


def build_launchB():
    nc = bass.Bass("TRN2", target_bir_lowering=False)
    h1_d = nc.dram_tensor("h1", [D, T], F32, kind="ExternalInput").ap()
    mx_d = nc.dram_tensor("maxes", [1, 8], F32, kind="ExternalInput").ap()
    clip_d = nc.dram_tensor("clipx", [1, 1], F32, kind="ExternalInput").ap()
    wg_d = nc.dram_tensor("l1wg", [I_FFN, D], F32, kind="ExternalInput").ap()
    wu_d = nc.dram_tensor("l1wu", [I_FFN, D], F32, kind="ExternalInput").ap()
    h2_d = nc.dram_tensor("h2", [I_FFN, T], F32, kind="ExternalOutput").ap()
    lm2_d = nc.dram_tensor("lm2", [1, 1], F32, kind="ExternalOutput").ap()

    with ExitStack() as stack:
        c = Ctx(nc, stack, None)
        c.register_consts([1.0 / 127.0])
        c.arena_bf = c.sb("arena_bf", [128, 62464], BF16)[:]
        c.arena_f32 = c.sb("arena_f32", [128, 8192], F32)[:]
        c.itile = c.sb("itile", [128, 1024], mybir.dt.int32)[:]
        c.psS1 = c.psum("psS1", [128, 1024])[:]
        c.psX = c.psum("psX", [128, 512])[:]
        c.psT = c.psum("psT", [128, 1024], BF16)[:]
        c.smallf = c.sb("smallf", [1, 16], F32)[:]
        c.swg = c.sb("swg", [128, 22], F32)[:]
        c.swu = c.sb("swu", [128, 22], F32)[:]
        c.svg = c.sb("svg", [128, 22], F32)[:]
        c.svu = c.sb("svu", [128, 22], F32)[:]
        c.rsxb = c.sb("rsxb", [128, 1], F32)[:]
        c.sxb = c.sb("sxb", [128, 1], F32)[:]
        c.am = c.sb("am", [128, 1], F32)[:]
        block = stack.enter_context(nc.Block())
        c.block = block

        hqf = c.arena_bf[:, 0:FC * T]
        wqg = c.arena_bf[:, 16384:16384 + FC * I_FFN]
        wqu = c.arena_bf[:, 38912:38912 + FC * I_FFN]
        wtm = c.arena_bf[:, 61440:61440 + 1024]
        h13 = h1_d.rearrange("(f p) t -> p f t", p=128)
        h23 = h2_d.rearrange("(i p) t -> p i t", p=128)

        setup_common(c)
        ld = c.sem("ld")
        lds = c.sem("lds")
        pe = c.sem("pe")
        ac = c.sem("ac")
        dv = c.sem("dv")
        st = c.sem("st")
        tps = c.sem("tps")
        tpe = c.sem("tpe")

        @c.block.sync
        def _(sync):
            c.qdma(sync, lds, c.smallf[0:1, 0:8], mx_d)
            c.qdma(sync, lds, c.smallf[0:1, 8:9], clip_d)

        @c.block.scalar
        def _(s, ldsn=lds.n):
            s.wait_ge(lds.h, ldsn)
            s.activation(c.smallf[0:1, 9:10], c.smallf[0:1, 8:9], AF.Sigmoid
                         ).then_inc(ac.h, 1)
        ac.n += 1

        @c.block.vector
        def _(v, acn=ac.n):
            v.wait_ge(ac.h, acn)
            st_ = c.chain(v)
            st_(v.tensor_reduce(c.smallf[0:1, 10:11], c.smallf[0:1, 0:8],
                                AX.X, ALU.max))
            st_(v.tensor_tensor(c.smallf[0:1, 11:12], c.smallf[0:1, 10:11],
                                c.smallf[0:1, 9:10], ALU.mult))
            st_(v.tensor_scalar(c.smallf[0:1, 12:13], c.smallf[0:1, 11:12],
                                1.0 / 127.0, None, ALU.mult))
            ins = v.reciprocal(c.smallf[0:1, 13:14], c.smallf[0:1, 12:13])
            dv.inc(ins)

        @c.block.tensor
        def _(t, dvn=dv.n):
            t.wait_ge(dv.h, dvn)
            t.matmul(c.psX[:, 0:1], c.ones1f[:], c.smallf[0:1, 13:14],
                     start=True, stop=True)
            t.matmul(c.psX[:, 1:2], c.ones1f[:], c.smallf[0:1, 12:13],
                     start=True, stop=True).then_inc(pe.h, 1)
        pe.n += 1

        @c.block.vector
        def _(v, pen=pe.n):
            v.wait_ge(pe.h, pen)
            st_ = c.chain(v)
            st_(v.tensor_copy(c.rsxb[:], c.psX[:, 0:1]))
            ins = v.tensor_copy(c.sxb[:], c.psX[:, 1:2])
            dv.inc(ins)
        c.barrier()

        wsrc = c.arena_f32[:, 0:1024]
        vt = c.arena_f32[:, 4096:5120]
        d1 = c.arena_f32[:, 5120:6144]
        d2 = c.arena_f32[:, 6144:7168]
        for wi, (wsrc_d, wdst, sw, sv) in enumerate(
                ((wg_d, wqg, c.swg, c.svg), (wu_d, wqu, c.swu, c.svu))):
            w3 = wsrc_d.rearrange("(n p) k -> p n k", p=128)
            for ch in range(IC):
                @c.block.sync
                def _(sync, w3=w3, ch=ch, dvn=dv.n):
                    sync.wait_ge(dv.h, dvn)
                    c.qdma(sync, ld, wsrc[:, 0:1024], w3[:, ch, :])

                @c.block.vector
                def _(v, ch=ch, sw=sw, ldn=ld.n, tpen=tpe.n):
                    v.wait_ge(ld.h, ldn)
                    v.wait_ge(tpe.h, tpen)
                    st_ = c.chain(v)
                    st_(v.tensor_reduce(sw[:, ch:ch + 1], wsrc[:, 0:1024],
                                        AX.X, ALU.max, apply_absolute_value=True))
                    st_(v.tensor_scalar(sw[:, ch:ch + 1], sw[:, ch:ch + 1],
                                        1.0 / 127.0, None, ALU.mult))
                    st_(v.reciprocal(d1[:, 0:1], sw[:, ch:ch + 1]))
                    st_(v.tensor_tensor(vt[:, 0:1024], wsrc[:, 0:1024],
                                        d1[:, 0:1].broadcast_to((128, 1024)),
                                        ALU.mult))
                    ins = _emit_round_halfup(c, v, vt[:, 0:1024],
                                             c.itile[:, 0:1024],
                                             d1[:, 0:1024], d2[:, 0:1024],
                                             1024, wtm[:, 0:1024], False)
                    ins.then_inc(dv.h, 1)
                dv.n += 1

                @c.block.tensor
                def _(t, dvn=dv.n, tps0=tps.n):
                    t.wait_ge(dv.h, dvn)
                    for g4 in range(2):
                        if tps0 + g4 >= 1:
                            t.wait_ge(tpe.h, tps0 + g4)
                        for j in range(4):
                            trc = g4 * 4 + j
                            ins = t.transpose(c.psT[:, j * 128:(j + 1) * 128],
                                              wtm[:, trc * 128:(trc + 1) * 128],
                                              c.identb[:])
                        ins.then_inc(tps.h, 1)

                @c.block.scalar
                def _(s, ch=ch, wdst=wdst, tps0=tps.n):
                    for g4 in range(2):
                        s.wait_ge(tps.h, tps0 + g4 + 1)
                        for j in range(4):
                            trc = g4 * 4 + j
                            ins = s.copy(
                                wdst[:, trc * I_FFN + ch * 128:
                                     trc * I_FFN + ch * 128 + 128],
                                c.psT[:, j * 128:(j + 1) * 128])
                        ins.then_inc(tpe.h, 1)
                tps.n += 2
                tpe.n += 2
        c.barrier()

        @c.block.vector
        def _(v):
            st_ = c.chain(v)
            st_(v.tensor_tensor(c.svg[:], c.swg[:],
                                c.sxb[:].broadcast_to((128, 22)), ALU.mult))
            st_(v.tensor_tensor(c.svu[:], c.swu[:],
                                c.sxb[:].broadcast_to((128, 22)), ALU.mult))
            st_(v.memset(c.am[:], 0.0))
        c.barrier()

        xf = c.arena_f32[:, 0:4096]
        for g in range(4):
            @c.block.sync
            def _(sync, g=g, dvn=dv.n):
                if g >= 1:
                    sync.wait_ge(dv.h, dvn)
                c.qdma(sync, ld, xf.rearrange("p (f t) -> p f t", f=8),
                       h13[:, :, g * 512:(g + 1) * 512])

            @c.block.vector
            def _(v, g=g, ldn=ld.n):
                v.wait_ge(ld.h, ldn)
                st_ = c.chain(v)
                for half in range(4):
                    seg = xf[:, half * 1024:(half + 1) * 1024]
                    st_(v.tensor_tensor(vt[:, 0:1024], seg,
                                        c.rsxb[:].broadcast_to((128, 1024)),
                                        ALU.mult))
                    ins = _emit_round_halfup(
                        c, v, vt[:, 0:1024], c.itile[:, 0:1024],
                        d1[:, 0:1024], d2[:, 0:1024], 1024,
                        wtm[:, 0:1024], True)
                    st_(ins)
                    for k in range(2):
                        fc = half * 2 + k
                        ins = v.tensor_copy(
                            hqf[:, fc * T + g * 512: fc * T + g * 512 + 512],
                            wtm[:, k * 512:(k + 1) * 512])
                        if not (half == 3 and k == 1):
                            st_(ins)
                ins.then_inc(dv.h, 1)
            dv.n += 1
            c.barrier()

        h2a = c.arena_f32[:, 0:512]
        h2b = c.arena_f32[:, 512:1024]
        sgt = c.arena_f32[:, 1024:1536]
        for ic in range(IC):
            for tb in range(4):
                idx = ic * 4 + tb

                @c.block.tensor
                def _(t, ic=ic, tb=tb, idx=idx, dvn=dv.n):
                    if idx >= 1:
                        t.wait_ge(dv.h, dvn)
                    for fc in range(FC):
                        t.matmul(c.psS1[:, 0:512],
                                 wqg[:, fc * I_FFN + ic * 128:
                                     fc * I_FFN + ic * 128 + 128],
                                 hqf[:, fc * T + tb * 512: fc * T + tb * 512 + 512],
                                 start=(fc == 0), stop=(fc == FC - 1))
                    for fc in range(FC):
                        ins = t.matmul(c.psS1[:, 512:1024],
                                       wqu[:, fc * I_FFN + ic * 128:
                                           fc * I_FFN + ic * 128 + 128],
                                       hqf[:, fc * T + tb * 512:
                                           fc * T + tb * 512 + 512],
                                       start=(fc == 0), stop=(fc == FC - 1))
                    ins.then_inc(pe.h, 1)
                pe.n += 1

                @c.block.scalar
                def _(s, ic=ic, pen=pe.n):
                    s.wait_ge(pe.h, pen)
                    s.activation(sgt, c.psS1[:, 0:512], AF.Silu,
                                 scale=c.svg[:, ic:ic + 1]).then_inc(ac.h, 1)
                ac.n += 1

                @c.block.vector
                def _(v, ic=ic, tb=tb, idx=idx, acn=ac.n, stn=st.n):
                    v.wait_ge(ac.h, acn)
                    if idx >= 2:
                        v.wait_ge(st.h, stn - 16)
                    tile = h2a if idx % 2 == 0 else h2b
                    st_ = c.chain(v)
                    st_(v.scalar_tensor_tensor(tile, c.psS1[:, 512:1024],
                                               c.svu[:, ic:ic + 1], sgt,
                                               ALU.mult, ALU.mult))
                    st_(v.tensor_reduce(d2[:, 0:1], tile, AX.X, ALU.max,
                                        apply_absolute_value=True))
                    v.tensor_tensor(c.am[:], c.am[:], d2[:, 0:1], ALU.max
                                    ).then_inc(dv.h, 1)
                dv.n += 1

                @c.block.gpsimd
                def _(g2, ic=ic, tb=tb, idx=idx, dvn=dv.n):
                    g2.wait_ge(dv.h, dvn)
                    tile = h2a if idx % 2 == 0 else h2b
                    c.qdma(g2, st, h23[:, ic, tb * 512:(tb + 1) * 512], tile)
        c.barrier()

        fin2 = c.sem("fin2")

        @c.block.tensor
        def _(t):
            t.transpose(c.psX[0:1, 0:128], c.am[:], c.identf[:]
                        ).then_inc(fin2.h, 1)

        @c.block.vector
        def _(v):
            v.wait_ge(fin2.h, 1)
            v.tensor_reduce(c.smallf[0:1, 14:15], c.psX[0:1, 0:128],
                            AX.X, ALU.max).then_inc(fin2.h, 1)

        @c.block.gpsimd
        def _(g):
            g.wait_ge(fin2.h, 2)
            c.qdma(g, c.sem("finst"), lm2_d, c.smallf[0:1, 14:15])
        c.barrier()
    return nc


def build_launchC():
    nc = bass.Bass("TRN2", target_bir_lowering=False)
    h2_d = nc.dram_tensor("h2", [I_FFN, T], F32, kind="ExternalInput").ap()
    mx_d = nc.dram_tensor("maxes2", [1, 8], F32, kind="ExternalInput").ap()
    clip_d = nc.dram_tensor("clipd", [1, 1], F32, kind="ExternalInput").ap()
    wd_d = nc.dram_tensor("l1wd", [D, I_FFN], F32, kind="ExternalInput").ap()
    x1_d = nc.dram_tensor("x1", [D, T], F32, kind="ExternalInput").ap()
    modg_d = nc.dram_tensor("modg", [128, 8], F32, kind="ExternalInput").ap()
    out_d = nc.dram_tensor("out", [D, T], F32, kind="ExternalOutput").ap()

    with ExitStack() as stack:
        c = Ctx(nc, stack, None)
        c.register_consts([1.0 / 127.0])
        c.arena_bf = c.sb("arena_bf", [128, 70400], BF16)[:]
        c.arena_f32 = c.sb("arena_f32", [128, 11776], F32)[:]
        c.itile = c.sb("itile", [128, 2816], mybir.dt.int32)[:]
        c.psS1 = c.psum("psS1", [128, 1024])[:]
        c.psX = c.psum("psX", [128, 512])[:]
        c.psT = c.psum("psT", [128, 1024], BF16)[:]
        c.smallf = c.sb("smallf", [1, 16], F32)[:]
        c.swd = c.sb("swd", [128, 8], F32)[:]
        c.coef = c.sb("coef", [128, 8], F32)[:]
        c.rsdb = c.sb("rsdb", [128, 1], F32)[:]
        c.sdb = c.sb("sdb", [128, 1], F32)[:]
        c.modg = c.sb("modgs", [128, 8], F32)[:]
        block = stack.enter_context(nc.Block())
        c.block = block

        h2qi = c.arena_bf[:, 0:IC * T]
        wdq = c.arena_bf[:, IC * T:IC * T + IC * 1024]
        wtm = c.arena_bf[:, IC * T + IC * 1024:IC * T + IC * 1024 + I_FFN]
        h23 = h2_d.rearrange("(i p) t -> p i t", p=128)
        x13 = x1_d.rearrange("(f p) t -> p f t", p=128)
        o3 = out_d.rearrange("(f p) t -> p f t", p=128)

        setup_common(c)
        ld = c.sem("ld")
        lds = c.sem("lds")
        pe = c.sem("pe")
        ac = c.sem("ac")
        dv = c.sem("dv")
        st = c.sem("st")
        tps = c.sem("tps")
        tpe = c.sem("tpe")

        @c.block.sync
        def _(sync):
            c.qdma(sync, lds, c.smallf[0:1, 0:8], mx_d)
            c.qdma(sync, lds, c.smallf[0:1, 8:9], clip_d)
            c.qdma(sync, lds, c.modg[:], modg_d)

        @c.block.scalar
        def _(s, ldsn=lds.n):
            s.wait_ge(lds.h, ldsn)
            s.activation(c.smallf[0:1, 9:10], c.smallf[0:1, 8:9], AF.Sigmoid
                         ).then_inc(ac.h, 1)
        ac.n += 1

        @c.block.vector
        def _(v, acn=ac.n):
            v.wait_ge(ac.h, acn)
            st_ = c.chain(v)
            st_(v.tensor_reduce(c.smallf[0:1, 10:11], c.smallf[0:1, 0:8],
                                AX.X, ALU.max))
            st_(v.tensor_tensor(c.smallf[0:1, 11:12], c.smallf[0:1, 10:11],
                                c.smallf[0:1, 9:10], ALU.mult))
            st_(v.tensor_scalar(c.smallf[0:1, 12:13], c.smallf[0:1, 11:12],
                                1.0 / 127.0, None, ALU.mult))
            ins = v.reciprocal(c.smallf[0:1, 13:14], c.smallf[0:1, 12:13])
            dv.inc(ins)

        @c.block.tensor
        def _(t, dvn=dv.n):
            t.wait_ge(dv.h, dvn)
            t.matmul(c.psX[:, 0:1], c.ones1f[:], c.smallf[0:1, 13:14],
                     start=True, stop=True)
            t.matmul(c.psX[:, 1:2], c.ones1f[:], c.smallf[0:1, 12:13],
                     start=True, stop=True).then_inc(pe.h, 1)
        pe.n += 1

        @c.block.vector
        def _(v, pen=pe.n):
            v.wait_ge(pe.h, pen)
            st_ = c.chain(v)
            st_(v.tensor_copy(c.rsdb[:], c.psX[:, 0:1]))
            ins = v.tensor_copy(c.sdb[:], c.psX[:, 1:2])
            dv.inc(ins)
        c.barrier()

        wsrc = c.arena_f32[:, 0:2816]
        vt = c.arena_f32[:, 2816:5632]
        dd1 = c.arena_f32[:, 5632:8448]
        dd2 = c.arena_f32[:, 8448:11264]
        w3 = wd_d.rearrange("(n p) k -> p n k", p=128)
        for ch in range(FC):
            @c.block.sync
            def _(sync, ch=ch, dvn=dv.n):
                sync.wait_ge(dv.h, dvn)
                c.qdma(sync, ld, wsrc[:, 0:I_FFN], w3[:, ch, :])

            @c.block.vector
            def _(v, ch=ch, ldn=ld.n, tpen=tpe.n):
                v.wait_ge(ld.h, ldn)
                v.wait_ge(tpe.h, tpen)
                st_ = c.chain(v)
                st_(v.tensor_reduce(c.swd[:, ch:ch + 1], wsrc[:, 0:I_FFN],
                                    AX.X, ALU.max, apply_absolute_value=True))
                st_(v.tensor_scalar(c.swd[:, ch:ch + 1], c.swd[:, ch:ch + 1],
                                    1.0 / 127.0, None, ALU.mult))
                st_(v.reciprocal(c.scr_small[:, 0:1], c.swd[:, ch:ch + 1]))
                st_(v.tensor_tensor(vt[:, 0:I_FFN], wsrc[:, 0:I_FFN],
                                    c.scr_small[:, 0:1].broadcast_to((128, I_FFN)),
                                    ALU.mult))
                ins = _emit_round_halfup(c, v, vt[:, 0:I_FFN],
                                         c.itile[:, 0:I_FFN],
                                         dd1[:, 0:I_FFN], dd2[:, 0:I_FFN],
                                         I_FFN, wtm[:, 0:I_FFN], False)
                ins.then_inc(dv.h, 1)
            dv.n += 1

            @c.block.tensor
            def _(t, dvn=dv.n, tps0=tps.n):
                t.wait_ge(dv.h, dvn)
                for g4 in range(6):
                    if tps0 + g4 >= 1:
                        t.wait_ge(tpe.h, tps0 + g4)
                    for j in range(min(4, 22 - g4 * 4)):
                        trc = g4 * 4 + j
                        ins = t.transpose(c.psT[:, j * 128:(j + 1) * 128],
                                          wtm[:, trc * 128:(trc + 1) * 128],
                                          c.identb[:])
                    ins.then_inc(tps.h, 1)

            @c.block.scalar
            def _(s, ch=ch, tps0=tps.n):
                for g4 in range(6):
                    s.wait_ge(tps.h, tps0 + g4 + 1)
                    for j in range(min(4, 22 - g4 * 4)):
                        trc = g4 * 4 + j
                        ins = s.copy(
                            wdq[:, trc * 1024 + ch * 128:
                                trc * 1024 + ch * 128 + 128],
                            c.psT[:, j * 128:(j + 1) * 128])
                    ins.then_inc(tpe.h, 1)
            tps.n += 6
            tpe.n += 6
        c.barrier()

        @c.block.vector
        def _(v):
            st_ = c.chain(v)
            st_(v.tensor_tensor(c.coef[:], c.swd[:],
                                c.sdb[:].broadcast_to((128, 8)), ALU.mult))
            st_(v.tensor_tensor(c.coef[:], c.coef[:], c.modg[:], ALU.mult))
        c.barrier()

        hsrc = c.arena_f32[:, 0:2048]
        vt2 = c.arena_f32[:, 2048:4096]
        for ch in range(IC):
            @c.block.sync
            def _(sync, ch=ch, dvn=dv.n):
                sync.wait_ge(dv.h, dvn)
                c.qdma(sync, ld, hsrc[:, 0:2048], h23[:, ch, :])

            @c.block.vector
            def _(v, ch=ch, ldn=ld.n):
                v.wait_ge(ld.h, ldn)
                st_ = c.chain(v)
                st_(v.tensor_tensor(vt2[:, 0:2048], hsrc[:, 0:2048],
                                    c.rsdb[:].broadcast_to((128, 2048)),
                                    ALU.mult))
                ins = _emit_round_halfup(c, v, vt2[:, 0:2048],
                                         c.itile[:, 0:2048],
                                         dd1[:, 0:2048], dd2[:, 0:2048],
                                         2048, h2qi[:, ch * T:(ch + 1) * T],
                                         True)
                ins.then_inc(dv.h, 1)
            dv.n += 1
            c.barrier()

        xw = c.arena_f32[:, 0:512]
        esg = c.arena_f32[:, 512:1024]
        xo = c.arena_f32[:, 1024:1536]
        for oc in range(FC):
            for tb in range(4):
                idx = oc * 4 + tb

                @c.block.tensor
                def _(t, oc=oc, tb=tb, idx=idx, dvn=dv.n):
                    if idx >= 2:
                        t.wait_ge(dv.h, dvn - 1)
                    slot = c.psS1[:, 0:512] if idx % 2 == 0 else c.psS1[:, 512:1024]
                    for ic in range(IC):
                        ins = t.matmul(slot,
                                       wdq[:, ic * 1024 + oc * 128:
                                           ic * 1024 + oc * 128 + 128],
                                       h2qi[:, ic * T + tb * 512:
                                            ic * T + tb * 512 + 512],
                                       start=(ic == 0), stop=(ic == IC - 1))
                    ins.then_inc(pe.h, 1)
                pe.n += 1

                @c.block.sync
                def _(sync, oc=oc, tb=tb, idx=idx, dvn=dv.n):
                    if idx >= 1:
                        sync.wait_ge(dv.h, dvn)
                    c.qdma(sync, lds, xw, x13[:, oc, tb * 512:(tb + 1) * 512])

                @c.block.vector
                def _(v, oc=oc, tb=tb, idx=idx, pen=pe.n, ldsn=lds.n, stn=st.n):
                    st_ = c.chain(v)
                    v.wait_ge(pe.h, pen)
                    v.wait_ge(lds.h, ldsn)
                    v.wait_ge(st.h, stn)
                    slot = c.psS1[:, 0:512] if idx % 2 == 0 else c.psS1[:, 512:1024]
                    st_(v.tensor_scalar(esg, slot, c.coef[:, oc:oc + 1], None,
                                        ALU.mult))
                    v.tensor_tensor(xo, xw, esg, ALU.add).then_inc(dv.h, 1)
                dv.n += 1

                @c.block.gpsimd
                def _(g, oc=oc, tb=tb, dvn=dv.n):
                    g.wait_ge(dv.h, dvn)
                    c.qdma(g, st, o3[:, oc, tb * 512:(tb + 1) * 512], xo)
        c.barrier()
    return nc



def kernel(**inp):
    common = _prep_common(inp)
    in_maps = []
    for core in range(8):
        m = dict(common)
        m.update(_prep_core(inp, core))
        in_maps.append(m)

    ncA = build_launchA(upto="full")
    resA = run_bass_kernel_spmd(ncA, in_maps, core_ids=list(range(8)))

    maxes = np.ascontiguousarray(
        np.stack([np.asarray(resA.results[cr]["lm"]).ravel()[0]
                  for cr in range(8)]).reshape(1, 8).astype(np.float32))
    clipx = np.asarray(inp["L1_ffn_clip_x"], np.float32).reshape(1, 1)
    wg1 = np.ascontiguousarray(np.asarray(inp["L1_ffn_w_gate"], np.float32))
    wu1 = np.ascontiguousarray(np.asarray(inp["L1_ffn_w_up"], np.float32))
    ncB = build_launchB()
    in_mapsB = [dict(h1=np.asarray(resA.results[cr]["h1"]),
                     maxes=maxes, clipx=clipx, l1wg=wg1, l1wu=wu1)
                for cr in range(8)]
    resB = run_bass_kernel_spmd(ncB, in_mapsB, core_ids=list(range(8)))

    maxes2 = np.ascontiguousarray(
        np.stack([np.asarray(resB.results[cr]["lm2"]).ravel()[0]
                  for cr in range(8)]).reshape(1, 8).astype(np.float32))
    clipd = np.asarray(inp["L1_ffn_clip_down"], np.float32).reshape(1, 1)
    wd1 = np.ascontiguousarray(np.asarray(inp["L1_ffn_w_down"], np.float32))
    ncC = build_launchC()
    in_mapsC = [dict(h2=np.asarray(resB.results[cr]["h2"]),
                     maxes2=maxes2, clipd=clipd, l1wd=wd1,
                     x1=np.asarray(resA.results[cr]["x1"]),
                     modg=np.asarray(resA.results[cr]["modg"]))
                for cr in range(8)]
    resC = run_bass_kernel_spmd(ncC, in_mapsC, core_ids=list(range(8)))

    out = np.stack([np.ascontiguousarray(
        np.asarray(resC.results[cr]["out"]).T) for cr in range(8)])
    return out.astype(np.float32)


# version 05
# speedup vs baseline: 1.9037x; 1.9037x over previous
import numpy as np
from contextlib import ExitStack

import concourse.bass as bass
import concourse.mybir as mybir
from concourse.bass_utils import run_bass_kernel_spmd

F32 = mybir.dt.float32
BF16 = mybir.dt.bfloat16
AF = mybir.ActivationFunctionType
ALU = mybir.AluOpType
AX = mybir.AxisListType

D = 1024
T = 2048
FC = D // 128
CTX_M = 4096
CTX_D = 768
CC = CTX_D // 128
L0_HEADS = 8
L1_HEADS = 16
L1_HD = 64
I_FFN = 2816
IC = I_FFN // 128
EXPS = 30.0
L0_QK_SCALE = 1.5 * float(np.float32(128.0 ** -0.5))
L1_QK_SCALE = float(np.float32(L1_HD ** -0.5))
FP4_THR = [0.25, 0.75, 1.25, 1.75, 2.5, 3.5, 5.0]

CHUNK_FRAMES = 100
LOOKAHEAD = 50


def _l1_tmax(q):
    return max((q // CHUNK_FRAMES) * CHUNK_FRAMES + CHUNK_FRAMES - 1, q + LOOKAHEAD)


def l1_tile_classes():
    classes = {}
    partials = []
    for qtb in range(T // 512):
        q0 = qtb * 512
        for ktc in range(T // 128):
            k0 = ktc * 128
            if k0 + 127 <= _l1_tmax(q0):
                classes[(qtb, ktc)] = "full"
            elif k0 > _l1_tmax(q0 + 511):
                classes[(qtb, ktc)] = "zero"
            else:
                classes[(qtb, ktc)] = "partial"
                partials.append((qtb, ktc))
    return classes, partials


def l1_mask_array():
    _, partials = l1_tile_classes()
    q = np.arange(T)
    tmax = np.maximum((q // CHUNK_FRAMES) * CHUNK_FRAMES + CHUNK_FRAMES - 1,
                      q + LOOKAHEAD)
    out = np.zeros((len(partials), 128, 512), np.float32)
    for i, (qtb, ktc) in enumerate(partials):
        ks = ktc * 128 + np.arange(128)
        qs = qtb * 512 + np.arange(512)
        out[i] = (ks[:, None] <= tmax[qs][None, :]).astype(np.float32)
    return out



class Sem:
    def __init__(self, stack, nc, name):
        self.h = stack.enter_context(nc.semaphore(name))
        self.n = 0

    def inc(self, ins, k=1):
        ins.then_inc(self.h, k)
        self.n += k
        return self.n


class Ctx:
    def __init__(self, nc, stack, block):
        self.nc = nc
        self.stack = stack
        self.block = block
        self._sems = {}
        self._cvals = []

    def sem(self, name):
        if name not in self._sems:
            self._sems[name] = Sem(self.stack, self.nc, name)
        return self._sems[name]

    def sb(self, name, shape, dtype):
        return self.stack.enter_context(self.nc.sbuf_tensor(name, shape, dtype))

    def psum(self, name, shape, dtype=F32):
        return self.stack.enter_context(self.nc.psum_tensor(name, shape, dtype))

    def register_consts(self, vals):
        vals = sorted(set(float(v) for v in vals) - {0.0, 1.0})
        self._cvals = vals
        self.consts = self.sb("consts", [128, max(len(vals), 1)], F32)
        for j, v in enumerate(vals):
            self.nc.const_aps.register(F32, v, self.consts[:, j:j + 1])

    def qdma(self, q, sem, out, in_):
        if sem.n > 0:
            q.wait_ge(sem.h, sem.n)
        ins = q.dma_start(out=out, in_=in_)
        sem.inc(ins, 16)
        return ins

    def chain(self, eng, name="chn"):
        sem = self.sem(name)

        def step(ins):
            sem.inc(ins)
            eng.wait_ge(sem.h, sem.n)
            return ins
        return step

    def barrier(self):
        b = self.sem("bar")
        pending = [(sm.h, sm.n) for nm, sm in self._sems.items()
                   if nm != "bar" and sm.n > 0]
        engines = ("sync", "tensor", "vector", "scalar", "gpsimd")
        target = b.n + len(engines)
        for name in engines:
            dec = getattr(self.block, name)

            def _body(e, name=name, target=target):
                if name == "gpsimd":
                    for h, n in pending:
                        e.wait_ge(h, n)
                    e.memset(self.barcell[:, 2:3], 0.0).then_inc(b.h, 1)
                elif name == "vector":
                    e.memset(self.barcell[:, 0:1], 0.0).then_inc(b.h, 1)
                elif name == "scalar":
                    e.memzero(self.barcell[:, 1:2]).then_inc(b.h, 1)
                else:
                    e.nop().then_inc(b.h, 1)
                e.wait_ge(b.h, target)
            dec(_body)
        b.n = target


def setup_common(c: Ctx):
    c.onesb = c.sb("onesb", [128, 1], BF16)
    c.ones1f = c.sb("ones1f", [1, 128], F32)
    c.identb = c.sb("identb", [128, 128], BF16)
    c.identf = c.sb("identf", [128, 128], F32)
    c.scr_small = c.sb("scr_small", [128, 64], F32)
    c.barcell = c.sb("barcell", [1, 4], F32)

    @c.block.vector
    def _(v):
        if c._cvals:
            for j, val in enumerate(c._cvals):
                v.memset(c.consts[:, j:j + 1], val)
        v.memset(c.onesb[:], 1.0)
        v.memset(c.ones1f[:], 1.0)

    @c.block.gpsimd
    def _(g):
        st_ = c.chain(g, "chg")
        for ident in (c.identb, c.identf):
            st_(g.memset(ident[:], 0.0))
            st_(g.affine_select(out=ident[:], in_=ident[:],
                                compare_op=ALU.not_equal, fill=1.0,
                                base=0, pattern=[[-1, 128]],
                                channel_multiplier=1))
    c.barrier()



def emit_mod(c: Ctx, tag, wmod_d, c_d, modT):
    ld = c.sem(f"md{tag}")
    pe = c.sem(f"mp{tag}")
    ac = c.sem(f"ma{tag}")

    crow = c.sb(f"crow{tag}", [128, 8], F32)
    siluf = c.sb(f"siluf{tag}", [128, 8], F32)
    wmf = c.arena_f32[:, 0:6144]
    wm3 = wmod_d.rearrange("(f p) j -> p f j", p=128)

    @c.block.sync
    def _(sync):
        c.qdma(sync, ld, crow[:], c_d)

    @c.block.scalar
    def _(s, ldn=ld.n):
        s.wait_ge(ld.h, ldn)
        s.activation(siluf[:], crow[:], AF.Silu).then_inc(ac.h, 1)
    ac.n += 1

    pe0, ac0 = pe.n, ac.n
    for piece in range(8):
        @c.block.sync
        def _(sync, piece=piece, pe0=pe0):
            if piece >= 1:
                sync.wait_ge(pe.h, pe0 + piece)
            c.qdma(sync, ld, wmf.rearrange("p (f j) -> p f j", f=8),
                   wm3[:, :, piece * 768:(piece + 1) * 768])

        @c.block.tensor
        def _(t, piece=piece, ldn=ld.n, ac0=ac0):
            t.wait_ge(ld.h, ldn)
            t.wait_ge(ac.h, ac0)
            for jloc in range(6):
                jc = piece * 6 + jloc
                for fc in range(FC):
                    ins = t.matmul(
                        c.psX[:, jc:jc + 1],
                        wmf[:, fc * 768 + jloc * 128: fc * 768 + (jloc + 1) * 128],
                        siluf[:, fc:fc + 1],
                        start=(fc == 0), stop=(fc == FC - 1))
            ins.then_inc(pe.h, 1)
    pe.n = pe0 + 8

    @c.block.vector
    def _(v, pen=pe.n):
        v.wait_ge(pe.h, pen)
        st_ = c.chain(v)
        st_(v.tensor_copy(modT[:, 0:48], c.psX[:, 0:48]))
        for col in (16, 40):
            g = modT[:, col:col + 8]
            sgn = c.scr_small[:, 0:8]
            av = c.scr_small[:, 8:16]
            st_(v.tensor_scalar(sgn, g, 0.0, None, ALU.is_ge))
            st_(v.tensor_scalar(sgn, sgn, 2.0, -1.0, ALU.mult, ALU.add))
            st_(v.tensor_scalar(av, g, -1.0, None, ALU.mult))
            st_(v.tensor_tensor(av, av, g, ALU.max))
            st_(v.tensor_scalar(av, av, 0.25, None, ALU.max))
            st_(v.tensor_tensor(g, sgn, av, ALU.mult))
        st_(v.tensor_scalar(modT[:, 48:56], modT[:, 8:16], 1.0, None, ALU.add))
        st_(v.tensor_scalar(modT[:, 56:64], modT[:, 32:40], 1.0, None, ALU.add))
    c.barrier()



def emit_ln(c: Ctx, tag, x_src3, h_dst, modT, onec_col, s_col,
            h_f32_dram3=None, absmax_to=None):
    ld = c.sem(f"ln_ld{tag}")
    pe = c.sem(f"ln_pe{tag}")
    dv = c.sem(f"ln_dv{tag}")
    ac = c.sem(f"ln_ac{tag}")
    st = c.sem(f"ln_st{tag}")

    xf = c.arena_f32[:, 0:4096]
    hf = xf
    xbb = c.arena_bf[:, c.A_SCRATCH:c.A_SCRATCH + 4096]
    xsq = c.arena_bf[:, c.A_SCRATCH + 4096:c.A_SCRATCH + 8192]
    small = c.ln_small

    if absmax_to is not None:
        @c.block.vector
        def _(v):
            v.memset(absmax_to, 0.0)

    for g in range(4):
        @c.block.sync
        def _(sync, g=g):
            c.qdma(sync, ld, xf.rearrange("p (f t) -> p f t", f=8),
                   x_src3[:, :, g * 512:(g + 1) * 512])

        @c.block.vector
        def _(v, ldn=ld.n):
            v.wait_ge(ld.h, ldn)
            ins = v.tensor_copy(xbb, xf)
            dv.inc(ins)

        @c.block.scalar
        def _(s, dvn=dv.n):
            s.wait_ge(dv.h, dvn)
            ins = s.activation(xsq, xbb, AF.Square)
            ac.inc(ins)

        @c.block.tensor
        def _(t, acn=ac.n):
            t.wait_ge(ac.h, acn)
            for fc in range(FC):
                t.matmul(c.psS1[0:1, 0:512], c.onesb[:],
                         xbb[:, fc * 512:(fc + 1) * 512],
                         start=(fc == 0), stop=(fc == FC - 1))
            for fc in range(FC):
                ins = t.matmul(c.psS1[0:1, 512:1024], c.onesb[:],
                               xsq[:, fc * 512:(fc + 1) * 512],
                               start=(fc == 0), stop=(fc == FC - 1))
            pe.inc(ins)

        @c.block.vector
        def _(v, pen=pe.n):
            v.wait_ge(pe.h, pen)
            st_ = c.chain(v)
            m, varr, lv = (small[0:1, 0:512], small[0:1, 512:1024],
                           small[0:1, 1024:1536])
            st_(v.tensor_scalar(m, c.psS1[0:1, 0:512], 1.0 / D, None, ALU.mult))
            st_(v.tensor_scalar(lv, c.psS1[0:1, 512:1024], 1.0 / D, None, ALU.mult))
            st_(v.tensor_tensor(varr, m, m, ALU.mult))
            ins = v.tensor_tensor(varr, lv, varr, ALU.subtract)
            dv.inc(ins)

        @c.block.scalar
        def _(s, dvn=dv.n):
            s.wait_ge(dv.h, dvn)
            st_ = c.chain(s, "chs")
            varr, lv = small[0:1, 512:1024], small[0:1, 1024:1536]
            st_(s.activation(lv, varr, AF.Ln, bias=1e-5))
            ins = s.activation(varr, lv, AF.Exp, scale=-0.5)
            ac.inc(ins)

        @c.block.tensor
        def _(t, acn=ac.n):
            t.wait_ge(ac.h, acn)
            t.matmul(c.psX[:, 0:512], c.ones1f[:], small[0:1, 0:512],
                     start=True, stop=True)
            ins = t.matmul(c.psO[:, 0:512], c.ones1f[:], small[0:1, 512:1024],
                           start=True, stop=True)
            pe.inc(ins)

        @c.block.vector
        def _(v, pen=pe.n):
            v.wait_ge(pe.h, pen)
            st_ = c.chain(v)
            for fc in range(FC):
                hseg = hf[:, fc * 512:(fc + 1) * 512]
                st_(v.tensor_tensor(hseg, xf[:, fc * 512:(fc + 1) * 512],
                                    c.psX[:, 0:512], ALU.subtract))
                if fc < FC - 1:
                    st_(v.tensor_tensor(hseg, hseg, c.psO[:, 0:512], ALU.mult))
                else:
                    ins = v.tensor_tensor(hseg, hseg, c.psO[:, 0:512], ALU.mult)
            dv.inc(ins)

        @c.block.vector
        def _(v, g=g, dvn=dv.n):
            st_ = c.chain(v)
            for fc in range(FC):
                if h_dst is not None:
                    ins = v.tensor_scalar(
                        h_dst[:, fc * T + g * 512: fc * T + g * 512 + 512],
                        hf[:, fc * 512:(fc + 1) * 512],
                        modT[:, onec_col + fc:onec_col + fc + 1],
                        modT[:, s_col + fc:s_col + fc + 1],
                        ALU.mult, ALU.add)
                    if h_f32_dram3 is None and fc < FC - 1:
                        st_(ins)
                if h_f32_dram3 is not None:
                    hstage = c.arena_f32[:, 4096 + (fc % 2) * 512:
                                         4096 + (fc % 2) * 512 + 512]
                    tgt = st.n + 16 * (fc - 1)
                    if tgt > 0:
                        v.wait_ge(st.h, tgt)
                    st_(v.tensor_tensor(
                        hstage, hf[:, fc * 512:(fc + 1) * 512],
                        modT[:, onec_col + fc:onec_col + fc + 1]
                        .broadcast_to((128, 512)), ALU.mult))
                    ins = v.tensor_tensor(
                        hstage, hstage,
                        modT[:, s_col + fc:s_col + fc + 1]
                        .broadcast_to((128, 512)), ALU.add)
                    ins.then_inc(dv.h, 1)
                    if absmax_to is not None:
                        v.wait_ge(dv.h, dvn + fc + 1)
                        tm = c.scr_small[:, 16:17]
                        st_(v.tensor_reduce(tm, hstage, AX.X, ALU.max,
                                            apply_absolute_value=True))
                        st_(v.tensor_tensor(absmax_to, absmax_to, tm, ALU.max))
            if h_f32_dram3 is None:
                ac.inc(ins)

        if h_f32_dram3 is not None:
            dv.n += FC

            @c.block.gpsimd
            def _(gq, g=g, dv0=dv.n - FC, st0=st.n):
                for fc in range(FC):
                    hstage = c.arena_f32[:, 4096 + (fc % 2) * 512:
                                         4096 + (fc % 2) * 512 + 512]
                    gq.wait_ge(dv.h, dv0 + fc + 1)
                    c.qdma(gq, st, h_f32_dram3[:, fc, g * 512:(g + 1) * 512],
                           hstage)



        c.barrier()

    if h_f32_dram3 is not None:
        @c.block.gpsimd
        def _(gq):
            gq.wait_ge(st.h, st.n)
        c.barrier()

    if h_f32_dram3 is not None:
        @c.block.gpsimd
        def _(gq, stn=st.n):
            gq.wait_ge(st.h, stn)
        c.barrier()



def emit_l0_attn(c: Ctx, ctx_d, wq_d, wk_d, wv_d, wout_d, bout_d, x_in3, xs3,
                 modT):
    A = c.A
    ld = c.sem("a2ld")
    pe = c.sem("a2pe")
    ac = c.sem("a2ac")
    dv = c.sem("a2dv")
    st = c.sem("a2st")

    ctxb = c.arena_bf[:, A["CTX"]:A["CTX"] + CC * CTX_M]
    wqb = c.arena_bf[:, A["WQ"]:A["WQ"] + FC * 1024]
    wkb = c.arena_bf[:, A["WK"]:A["WK"] + CC * 1024]
    wvb = c.arena_bf[:, A["WV"]:A["WV"] + CC * 1024]
    h_bf = c.arena_bf[:, A["H"]:A["H"] + FC * T]
    ao = c.arena_bf[:, A["AO"]:A["AO"] + FC * T]
    qh = c.arena_bf[:, A["QH"]:A["QH"] + T]
    kh = c.arena_bf[:, A["KH"]:A["KH"] + CTX_M]
    vh = c.arena_bf[:, A["VH"]:A["VH"] + CTX_M]
    pt = c.arena_bf[:, A["P"]:A["P"] + 2048]
    stage0 = c.arena_f32[:, 0:512]
    stage1 = c.arena_f32[:, 512:1024]
    stage2 = c.arena_f32[:, 1024:1536]
    stage3 = c.arena_f32[:, 1536:2048]
    smalls = c.ln_small

    @c.block.gpsimd
    def _(g):
        c.qdma(g, ld, ctxb.rearrange("p (c m) -> p c m", c=CC),
               ctx_d.rearrange("(c p) m -> p c m", p=128))
        c.qdma(g, ld, wqb.rearrange("p (f o) -> p f o", f=FC),
               wq_d.rearrange("(f p) o -> p f o", p=128))
        c.qdma(g, ld, wkb.rearrange("p (f o) -> p f o", f=CC),
               wk_d.rearrange("(f p) o -> p f o", p=128))
        c.qdma(g, ld, wvb.rearrange("p (f o) -> p f o", f=CC),
               wv_d.rearrange("(f p) o -> p f o", p=128))

    @c.block.tensor
    def _(t, ldn=ld.n):
        t.wait_ge(ld.h, ldn)
    c.barrier()

    for head in range(L0_HEADS):
        pe0, ac0 = pe.n, ac.n
        slots = [c.psX, c.psO]

        @c.block.tensor
        def _(t, head=head, pe0=pe0, ac0=ac0):
            gi = 0
            for qtb in range(4):
                if gi >= 2:
                    t.wait_ge(ac.h, ac0 + gi - 1)
                for fc in range(FC):
                    ins = t.matmul(
                        slots[gi % 2][:, 0:512],
                        wqb[:, fc * 1024 + head * 128: fc * 1024 + head * 128 + 128],
                        h_bf[:, fc * T + qtb * 512: fc * T + qtb * 512 + 512],
                        start=(fc == 0), stop=(fc == FC - 1))
                ins.then_inc(pe.h, 1)
                gi += 1
            for kb in range(8):
                t.wait_ge(ac.h, ac0 + gi - 1)
                for cc in range(CC):
                    ins = t.matmul(
                        slots[gi % 2][:, 0:512],
                        wkb[:, cc * 1024 + head * 128: cc * 1024 + head * 128 + 128],
                        ctxb[:, cc * CTX_M + kb * 512: cc * CTX_M + kb * 512 + 512],
                        start=(cc == 0), stop=(cc == CC - 1))
                ins.then_inc(pe.h, 1)
                gi += 1
            for vg in range(8):
                t.wait_ge(ac.h, ac0 + gi - 1)
                for j in range(4):
                    ktc = vg * 4 + j
                    for cc in range(CC):
                        ins = t.matmul(
                            slots[gi % 2][:, j * 128:(j + 1) * 128],
                            ctxb[:, cc * CTX_M + ktc * 128: cc * CTX_M + ktc * 128 + 128],
                            wvb[:, cc * 1024 + head * 128: cc * 1024 + head * 128 + 128],
                            start=(cc == 0), stop=(cc == CC - 1))
                ins.then_inc(pe.h, 1)
                gi += 1
        pe.n = pe0 + 20

        @c.block.scalar
        def _(s, pe0=pe0):
            gi = 0
            for qtb in range(4):
                s.wait_ge(pe.h, pe0 + gi + 1)
                s.copy(qh[:, qtb * 512:(qtb + 1) * 512],
                       slots[gi % 2][:, 0:512]).then_inc(ac.h, 1)
                gi += 1
            for kb in range(8):
                s.wait_ge(pe.h, pe0 + gi + 1)
                s.copy(kh[:, kb * 512:(kb + 1) * 512],
                       slots[gi % 2][:, 0:512]).then_inc(ac.h, 1)
                gi += 1
            for vg in range(8):
                s.wait_ge(pe.h, pe0 + gi + 1)
                s.copy(vh[:, vg * 512:(vg + 1) * 512],
                       slots[gi % 2][:, 0:512]).then_inc(ac.h, 1)
                gi += 1
        ac.n = ac0 + 20
        c.barrier()

        for qtb in range(4):
            pe0, ac0, dv0 = pe.n, ac.n, dv.n

            @c.block.tensor
            def _(t, head=head, qtb=qtb, pe0=pe0, ac0=ac0, dv0=dv0):
                def scores(i):
                    slot = c.psS1 if i % 2 == 0 else c.psS2
                    if i >= 2:
                        t.wait_ge(ac.h, ac0 + i - 1)
                    for j in range(2):
                        ktc = i * 2 + j
                        ins = t.matmul(slot[:, j * 512:(j + 1) * 512],
                                       kh[:, ktc * 128:(ktc + 1) * 128],
                                       qh[:, qtb * 512:(qtb + 1) * 512],
                                       start=True, stop=True)
                    ins.then_inc(pe.h, 1)

                def pv(i):
                    t.wait_ge(ac.h, ac0 + i + 1)
                    pslot = pt[:, (i % 2) * 1024:(i % 2) * 1024 + 1024]
                    for j in range(2):
                        ktc = i * 2 + j
                        t.matmul(c.psO[:, 0:512],
                                 vh[:, ktc * 128:(ktc + 1) * 128],
                                 pslot[:, j * 512:(j + 1) * 512],
                                 start=(ktc == 0), stop=(ktc == 31))
                        ins = t.matmul(c.psSum[0:1, 0:512], c.onesb[:],
                                       pslot[:, j * 512:(j + 1) * 512],
                                       start=(ktc == 0), stop=(ktc == 31))
                    ins.then_inc(pe.h, 1)

                scores(0)
                scores(1)
                for i in range(14):
                    pv(i)
                    scores(i + 2)
                pv(14)
                pv(15)
                t.wait_ge(dv.h, dv0 + 1)
                t.matmul(c.psX[:, 0:512], c.ones1f[:],
                         smalls[0:1, 512:1024], start=True, stop=True
                         ).then_inc(pe.h, 1)

            pe.n = pe0 + 33

            @c.block.scalar
            def _(s, head=head, qtb=qtb, pe0=pe0, ac0=ac0):
                for i in range(16):
                    s.wait_ge(pe.h, pe0 + (i + 1 if i < 2 else 2 * i))
                    slot = c.psS1 if i % 2 == 0 else c.psS2
                    s.activation(pt[:, (i % 2) * 1024:(i % 2) * 1024 + 1024],
                                 slot[:, 0:1024], AF.Exp,
                                 scale=L0_QK_SCALE, bias=-EXPS
                                 ).then_inc(ac.h, 1)
                s.wait_ge(pe.h, pe0 + 32)
                s.copy(smalls[0:1, 0:512], c.psSum[0:1, 0:512]).then_inc(ac.h, 1)
                s.copy(stage0[:], c.psO[:, 0:512]).then_inc(ac.h, 1)
            ac.n = ac0 + 18

            @c.block.vector
            def _(v, head=head, qtb=qtb, ac0=ac0, pe0=pe0):
                v.wait_ge(ac.h, ac0 + 17)
                v.reciprocal(smalls[0:1, 512:1024], smalls[0:1, 0:512]
                             ).then_inc(dv.h, 1)
                v.wait_ge(ac.h, ac0 + 18)
                v.wait_ge(pe.h, pe0 + 33)
                v.tensor_tensor(
                    ao[:, head * T + qtb * 512: head * T + qtb * 512 + 512],
                    stage0[:], c.psX[:, 0:512], ALU.mult).then_inc(dv.h, 1)
            dv.n = dv0 + 2
            c.barrier()

    woutb = wqb
    bout_t = c.sb("bout_t", [128, 8], F32)
    gab = c.scr_small[:, 24:32]

    @c.block.gpsimd
    def _(g):
        c.qdma(g, ld, woutb.rearrange("p (f o) -> p f o", f=FC),
               wout_d.rearrange("(f p) o -> p f o", p=128))
        c.qdma(g, ld, bout_t[:], bout_d)

    @c.block.vector
    def _(v, ldn=ld.n):
        v.wait_ge(ld.h, ldn)
        v.tensor_tensor(gab, modT[:, 16:24], bout_t[:], ALU.mult).then_inc(dv.h, 1)
    dv.n += 1
    c.barrier()

    pe0, ac0, dv0, st0 = pe.n, ac.n, dv.n, st.n
    lds = c.sem("a2lds")
    ntile = 32

    def tile_oc_qtb(i):
        return i % 8, i // 8

    @c.block.sync
    def _(sync, dv0=dv0):
        for i in range(ntile):
            oc, qtb = tile_oc_qtb(i)
            if i >= 2:
                sync.wait_ge(dv.h, dv0 + i - 1)
            c.qdma(sync, lds, (stage1 if i % 2 == 0 else stage2),
                   x_in3[:, oc, qtb * 512:(qtb + 1) * 512])

    @c.block.tensor
    def _(t, pe0=pe0, dv0=dv0):
        for i in range(ntile):
            oc, qtb = tile_oc_qtb(i)
            slot = c.psS1 if i % 2 == 0 else c.psS2
            if i >= 2:
                t.wait_ge(dv.h, dv0 + i - 1)
            for fc in range(FC):
                ins = t.matmul(
                    slot[:, 0:512],
                    woutb[:, fc * 1024 + oc * 128: fc * 1024 + oc * 128 + 128],
                    ao[:, fc * T + qtb * 512: fc * T + qtb * 512 + 512],
                    start=(fc == 0), stop=(fc == FC - 1))
            ins.then_inc(pe.h, 1)
    pe.n = pe0 + ntile

    @c.block.vector
    def _(v, pe0=pe0, dv0=dv0, st0=st0):
        st_ = c.chain(v)
        for i in range(ntile):
            oc, qtb = tile_oc_qtb(i)
            slot = c.psS1 if i % 2 == 0 else c.psS2
            v.wait_ge(pe.h, pe0 + i + 1)
            v.wait_ge(lds.h, 16 * (i + 1))
            if i >= 1:
                v.wait_ge(st.h, st0 + 16 * i)
            st_(v.tensor_scalar(stage0[:], slot[:, 0:512],
                                modT[:, 16 + oc:16 + oc + 1],
                                gab[:, oc:oc + 1],
                                ALU.mult, ALU.add))
            v.tensor_tensor(stage3[:],
                            (stage1 if i % 2 == 0 else stage2),
                            stage0[:], ALU.add).then_inc(dv.h, 1)
    dv.n = dv0 + ntile

    @c.block.gpsimd
    def _(g, dv0=dv0, st0=st0):
        for i in range(ntile):
            oc, qtb = tile_oc_qtb(i)
            g.wait_ge(dv.h, dv0 + i + 1)
            c.qdma(g, st, xs3[:, oc, qtb * 512:(qtb + 1) * 512], stage3[:])
    c.barrier()



def _fp4_half(c, eng, st_, src, dst, utile, qtile, tmp, rs, hs, o, w,
              has_stt=True):
    b0, wb = o // 16, w // 16
    s_ = lambda a: a[:, o:o + w]
    s3 = lambda a: a[:, o:o + w].rearrange("p (n b) -> p n b", b=16)
    r3 = lambda a: (a[:, b0:b0 + wb].rearrange("p (n o) -> p n o", o=1)
                    .broadcast_to((128, wb, 16)))
    if has_stt:
        st_(eng.scalar_tensor_tensor(s_(utile), s_(src), -1.0, s_(src),
                                     ALU.mult, ALU.max))
    else:
        st_(eng.tensor_scalar(s_(tmp), s_(src), 0.0, None, ALU.is_ge))
        st_(eng.tensor_scalar(s_(tmp), s_(tmp), 2.0, -1.0, ALU.mult, ALU.add))
        st_(eng.tensor_tensor(s_(utile), s_(src), s_(tmp), ALU.mult))
    st_(eng.tensor_tensor(s3(utile), s3(utile), r3(rs), ALU.mult))
    st_(eng.tensor_scalar(s_(qtile), s_(utile), FP4_THR[0], None, ALU.is_gt))
    for thr in FP4_THR[1:4]:
        st_(eng.tensor_scalar(s_(tmp), s_(utile), thr, None, ALU.is_gt))
        st_(eng.tensor_tensor(s_(qtile), s_(qtile), s_(tmp), ALU.add))
    for thr, wt in ((FP4_THR[4], 2.0), (FP4_THR[5], 2.0), (FP4_THR[6], 4.0)):
        st_(eng.tensor_scalar(s_(tmp), s_(utile), thr, None, ALU.is_gt))
        if has_stt:
            st_(eng.scalar_tensor_tensor(s_(qtile), s_(tmp), wt, s_(qtile),
                                         ALU.mult, ALU.add))
        else:
            st_(eng.tensor_scalar(s_(tmp), s_(tmp), wt, None, ALU.mult))
            st_(eng.tensor_tensor(s_(qtile), s_(qtile), s_(tmp), ALU.add))
    st_(eng.tensor_scalar(s_(tmp), s_(src), 0.0, None, ALU.is_ge))
    st_(eng.tensor_scalar(s_(tmp), s_(tmp), 2.0, -1.0, ALU.mult, ALU.add))
    st_(eng.tensor_tensor(s_(qtile), s_(qtile), s_(tmp), ALU.mult))
    return eng.tensor_tensor(s3(dst), s3(qtile), r3(hs), ALU.mult)


def emit_fp4_quant_dual(c, src, dst, W, utile, qtile, tmp, qsm,
                        vwaits=(), gwaits=(), dv=None, gq=None):
    nb = W // 16
    hw = (W // 32) * 16
    bm = qsm[:, 0:nb]
    sc = qsm[:, nb:2 * nb]
    rs = qsm[:, 2 * nb:3 * nb]
    hs = qsm[:, 3 * nb:4 * nb]
    smalls_at = dv.n + 1
    gq_prev = gq.n

    @c.block.vector
    def _(v, vwaits=tuple(vwaits)):
        st_ = c.chain(v)
        for h, n in vwaits:
            v.wait_ge(h, n)
        st_(v.tensor_reduce(bm, src.rearrange("p (n b) -> p n b", b=16),
                            AX.X, ALU.max, apply_absolute_value=True))
        st_(v.tensor_scalar(sc, bm, 1.0 / 6.0, 1e-12, ALU.mult, ALU.max))
        st_(v.tensor_scalar(hs, sc, 0.5, None, ALU.mult))
        st_(v.reciprocal(rs, sc))
        v.memset(c.scr_small[0:1, 23:24], 0.0).then_inc(dv.h, 1)
        ins = _fp4_half(c, v, st_, src, dst, utile, qtile, tmp, rs, hs, 0, W)
        ins.then_inc(dv.h, 1)
    dv.n += 2

    @c.block.gpsimd
    def _(g, dv_done=dv.n):
        g.wait_ge(dv.h, dv_done)
        g.memset(c.scr_small[0:1, 22:23], 0.0).then_inc(gq.h, 1)
    gq.n += 1



A3 = dict(WG=0, WU=22528, WD=45056, H2Q=67584, HQT=78848, H2QTM=79872,
          QS1=82688, QS2=85504, QS3=88320)


def emit_l0_ffn(c: Ctx, wg_d, wu_d, wd_d, xs3, h0scr3, modT):
    ld = c.sem("f0ld")
    pe = c.sem("f0pe")
    ac = c.sem("f0ac")
    dv = c.sem("f0dv")
    st = c.sem("f0st")

    wgf = c.arena_bf[:, A3["WG"]:A3["WG"] + FC * I_FFN]
    wuf = c.arena_bf[:, A3["WU"]:A3["WU"] + FC * I_FFN]
    wdf = c.arena_bf[:, A3["WD"]:A3["WD"] + IC * 1024]
    h2q = c.arena_bf[:, A3["H2Q"]:A3["H2Q"] + IC * 512]
    hqt = c.arena_bf[:, A3["HQT"]:A3["HQT"] + 1024]
    h2qtm = c.arena_bf[:, A3["H2QTM"]:A3["H2QTM"] + I_FFN]
    qs1 = c.arena_bf[:, A3["QS1"]:A3["QS1"] + I_FFN]
    qs2 = c.arena_bf[:, A3["QS2"]:A3["QS2"] + I_FFN]
    qs3 = c.arena_bf[:, A3["QS3"]:A3["QS3"] + 1792]
    hx = c.arena_f32[:, 0:1024]
    htm = c.arena_f32[:, 1024:2048]
    sg = c.arena_f32[:, 2048:2560]
    h2f = c.arena_f32[:, 2560:5376]
    qsm = c.arena_f32[:, 6416:6416 + 704]
    xw = c.arena_f32[:, 0:512]
    gq = c.sem("f0gq")

    tps = c.sem("f0tps")
    tpe = c.sem("f0tpe")
    wsrcs = (c.arena_f32[:, 0:2816], c.arena_f32[:, 2816:5632])
    chl = []
    for wi, (wd_src, wdst, nchunk, wwidth) in enumerate(
            ((wg_d, wgf, IC, 1024), (wu_d, wuf, IC, 1024), (wd_d, wdf, FC, I_FFN))):
        w3 = wd_src.rearrange("(n p) k -> p n k", p=128)
        for ch in range(nchunk):
            chl.append((w3, ch, wdst, wwidth, wi))
    prevw = {}
    for k, (w3, ch, wdst, wwidth, wi) in enumerate(chl):
        wsrc = wsrcs[k % 2]
        @c.block.sync
        def _(sync, w3=w3, ch=ch, wwidth=wwidth, wsrc=wsrc,
              pw=prevw.get(k % 2)):
            if pw is not None:
                sync.wait_ge(dv.h, pw[0])
                sync.wait_ge(gq.h, pw[1])
            c.qdma(sync, ld, wsrc[:, 0:wwidth], w3[:, ch, :])

        emit_fp4_quant_dual(c, wsrc[:, 0:wwidth], qs2[:, 0:wwidth], wwidth,
                            qs1[:, 0:wwidth], h2qtm[:, 0:wwidth],
                            h2q[:, 0:wwidth], qsm,
                            vwaits=[(ld.h, ld.n), (tpe.h, tpe.n)],
                            gwaits=[(ld.h, ld.n), (tpe.h, tpe.n)],
                            dv=dv, gq=gq)
        prevw[k % 2] = (dv.n, gq.n)

        ntr = wwidth // 128
        ngrp = (ntr + 3) // 4
        @c.block.tensor
        def _(t, ntr=ntr, ngrp=ngrp, dvn=dv.n, gqn=gq.n, tps0=tps.n,
              tpe0=tpe.n):
            t.wait_ge(dv.h, dvn)
            t.wait_ge(gq.h, gqn)
            for g4 in range(ngrp):
                if tps0 + g4 >= 1:
                    t.wait_ge(tpe.h, tps0 + g4)
                for j in range(min(4, ntr - g4 * 4)):
                    trc = g4 * 4 + j
                    ins = t.transpose(
                        c.psT[:, j * 128:(j + 1) * 128],
                        qs2[:, trc * 128:(trc + 1) * 128], c.identb[:])
                ins.then_inc(tps.h, 1)

        @c.block.scalar
        def _(s, ch=ch, wdst=wdst, ntr=ntr, ngrp=ngrp, wi=wi, tps0=tps.n):
            for g4 in range(ngrp):
                s.wait_ge(tps.h, tps0 + g4 + 1)
                nj = min(4, ntr - g4 * 4)
                for j in range(nj):
                    trc = g4 * 4 + j
                    stride = I_FFN if wi < 2 else 1024
                    dst = wdst[:, trc * stride + ch * 128:
                               trc * stride + ch * 128 + 128]
                    ins = s.copy(dst, c.psT[:, j * 128:(j + 1) * 128])
                ins.then_inc(tpe.h, 1)
        tps.n += ngrp
        tpe.n += ngrp
    c.barrier()

    IBW = [512, 512, 512, 512, 512, 256]
    for tb in range(4):
        for tci in range(4):
            tc = tb * 4 + tci
            @c.block.sync
            def _(sync, tc=tc, dvn=dv.n):
                sync.wait_ge(dv.h, dvn)
                c.qdma(sync, ld, hx.rearrange("p (f t) -> p f t", f=8),
                       h0scr3[:, :, tc * 128:(tc + 1) * 128])

            @c.block.tensor
            def _(t, ldn=ld.n, acn=ac.n):
                t.wait_ge(ld.h, ldn)
                t.wait_ge(ac.h, acn)
                for j in range(8):
                    slot = c.psX if j < 4 else c.psO
                    ins = t.transpose(slot[:, (j % 4) * 128:(j % 4) * 128 + 128],
                                      hx[:, j * 128:(j + 1) * 128], c.identf[:])
                ins.then_inc(pe.h, 1)
            pe.n += 1

            @c.block.scalar
            def _(s, pen=pe.n):
                s.wait_ge(pe.h, pen)
                st2 = c.chain(s, "chs")
                st2(s.copy(htm[:, 0:512], c.psX[:, 0:512]))
                s.copy(htm[:, 512:1024], c.psO[:, 0:512]).then_inc(ac.h, 1)
            ac.n += 1
            c.barrier()

            emit_fp4_quant_dual(c, htm[:, 0:1024], qs2[:, 0:1024], 1024,
                                qs1[:, 0:1024], h2qtm[:, 0:1024],
                                qs3[:, 0:1024], qsm,
                                vwaits=[(ac.h, ac.n)], gwaits=[(ac.h, ac.n)],
                                dv=dv, gq=gq)
            c.barrier()

            @c.block.tensor
            def _(t, dvn=dv.n, gqn=gq.n, tpe0=tpe.n):
                t.wait_ge(dv.h, dvn)
                t.wait_ge(gq.h, gqn)
                for g4 in range(2):
                    t.wait_ge(tpe.h, tpe0 + g4)
                    for j in range(4):
                        trc = g4 * 4 + j
                        ins = t.transpose(c.psT[:, j * 128:(j + 1) * 128],
                                          qs2[:, trc * 128:(trc + 1) * 128],
                                          c.identb[:])
                    ins.then_inc(tps.h, 1)
            tps.n += 2

            @c.block.scalar
            def _(s, tps0=tps.n - 2):
                for g4 in range(2):
                    s.wait_ge(tps.h, tps0 + g4 + 1)
                    for j in range(4):
                        trc = g4 * 4 + j
                        ins = s.copy(hqt[:, trc * 128:(trc + 1) * 128],
                                     c.psT[:, j * 128:(j + 1) * 128])
                    ins.then_inc(tpe.h, 1)
            tpe.n += 2
            c.barrier()

            off = 0
            for ib, ibw in enumerate(IBW):
                @c.block.tensor
                def _(t, ib=ib, ibw=ibw, off=off, tpen=tpe.n, dvn=dv.n):
                    t.wait_ge(tpe.h, tpen)
                    if ib >= 1:
                        t.wait_ge(dv.h, dvn)
                    for fc in range(FC):
                        t.matmul(c.psS2[:, 0:ibw],
                                 hqt[:, fc * 128:(fc + 1) * 128],
                                 wgf[:, fc * I_FFN + off: fc * I_FFN + off + ibw],
                                 start=(fc == 0), stop=(fc == FC - 1))
                    for fc in range(FC):
                        ins = t.matmul(c.psS2[:, 512:512 + ibw],
                                       hqt[:, fc * 128:(fc + 1) * 128],
                                       wuf[:, fc * I_FFN + off: fc * I_FFN + off + ibw],
                                       start=(fc == 0), stop=(fc == FC - 1))
                    ins.then_inc(pe.h, 1)
                pe.n += 1

                @c.block.scalar
                def _(s, ibw=ibw, pen=pe.n):
                    s.wait_ge(pe.h, pen)
                    s.activation(sg[:, 0:ibw], c.psS2[:, 0:ibw], AF.Silu
                                 ).then_inc(ac.h, 1)
                ac.n += 1

                @c.block.vector
                def _(v, ibw=ibw, off=off, acn=ac.n):
                    v.wait_ge(ac.h, acn)
                    v.tensor_tensor(h2f[:, off:off + ibw], sg[:, 0:ibw],
                                    c.psS2[:, 512:512 + ibw], ALU.mult
                                    ).then_inc(dv.h, 1)
                dv.n += 1
                off += ibw
                c.barrier()

            emit_fp4_quant_dual(c, h2f[:, 0:I_FFN], h2qtm[:, 0:I_FFN], I_FFN,
                                qs1[:, 0:I_FFN], qs2[:, 0:I_FFN],
                                h2qtm[:, 0:I_FFN], qsm,
                                vwaits=[(dv.h, dv.n)], gwaits=[(dv.h, dv.n)],
                                dv=dv, gq=gq)
            c.barrier()

            @c.block.tensor
            def _(t, dvn=dv.n, gqn=gq.n, tpe0=tpe.n):
                t.wait_ge(dv.h, dvn)
                t.wait_ge(gq.h, gqn)
                for g4 in range(6):
                    t.wait_ge(tpe.h, tpe0 + g4)
                    for j in range(min(4, 22 - g4 * 4)):
                        trc = g4 * 4 + j
                        ins = t.transpose(c.psT[:, j * 128:(j + 1) * 128],
                                          h2qtm[:, trc * 128:(trc + 1) * 128],
                                          c.identb[:])
                    ins.then_inc(tps.h, 1)
            tps.n += 6

            @c.block.scalar
            def _(s, tci=tci, tps0=tps.n - 6):
                for g4 in range(6):
                    s.wait_ge(tps.h, tps0 + g4 + 1)
                    for j in range(min(4, 22 - g4 * 4)):
                        trc = g4 * 4 + j
                        ins = s.copy(h2q[:, trc * 512 + tci * 128:
                                        trc * 512 + tci * 128 + 128],
                                     c.psT[:, j * 128:(j + 1) * 128])
                    ins.then_inc(tpe.h, 1)
            tpe.n += 6
            c.barrier()

        for oc in range(FC):
            @c.block.tensor
            def _(t, oc=oc, tpen=tpe.n, dvn=dv.n):
                t.wait_ge(tpe.h, tpen)
                if oc >= 2:
                    t.wait_ge(dv.h, dvn - 1)
                slot = c.psS1[:, 0:512] if oc % 2 == 0 else c.psS1[:, 512:1024]
                for ic in range(IC):
                    ins = t.matmul(slot,
                                   wdf[:, ic * 1024 + oc * 128:
                                       ic * 1024 + oc * 128 + 128],
                                   h2q[:, ic * 512:(ic + 1) * 512],
                                   start=(ic == 0), stop=(ic == IC - 1))
                ins.then_inc(pe.h, 1)
            pe.n += 1

            @c.block.sync
            def _(sync, tb=tb, oc=oc, dvn=dv.n):
                if oc >= 1:
                    sync.wait_ge(dv.h, dvn)
                c.qdma(sync, ld, xw[:, 0:512],
                       xs3[:, oc, tb * 512:(tb + 1) * 512])

            @c.block.vector
            def _(v, oc=oc, tb=tb, pen=pe.n, ldn=ld.n, stn=st.n):
                st_ = c.chain(v)
                v.wait_ge(pe.h, pen)
                v.wait_ge(ld.h, ldn)
                v.wait_ge(st.h, stn)
                slot = c.psS1[:, 0:512] if oc % 2 == 0 else c.psS1[:, 512:1024]
                esg = c.arena_f32[:, 5904:6416]
                st_(v.tensor_scalar(esg, slot,
                                    modT[:, 40 + oc:40 + oc + 1], None, ALU.mult))
                v.tensor_tensor(h2f[:, 0:512], xw[:, 0:512], esg,
                                ALU.add).then_inc(dv.h, 1)
            dv.n += 1

            @c.block.gpsimd
            def _(g, oc=oc, tb=tb, dvn=dv.n):
                g.wait_ge(dv.h, dvn)
                c.qdma(g, st, xs3[:, oc, tb * 512:(tb + 1) * 512],
                       h2f[:, 0:512])
            c.barrier()
    c.barrier()



A5 = dict(WQ=0, WK=8192, WV=16384, WO=24576, MASK=32768, QH=77824, KH=79872,
          VP=81920, VP2=82960, PM=84000)


def emit_l1_attn(c: Ctx, wq_d, wk_d, wv_d, wo_d, bq_d, bv_d, bo_d, mask_d,
                 n_par, xs3, modT):
    A = c.A
    ld = c.sem("a5ld")
    pe = c.sem("a5pe")
    ac = c.sem("a5ac")
    dv = c.sem("a5dv")
    st = c.sem("a5st")

    wqb = c.arena_bf[:, A5["WQ"]:A5["WQ"] + FC * 1024]
    wkb = c.arena_bf[:, A5["WK"]:A5["WK"] + FC * 1024]
    wvb = c.arena_bf[:, A5["WV"]:A5["WV"] + FC * 1024]
    wob = c.arena_bf[:, A5["WO"]:A5["WO"] + FC * 1024]
    maskb = c.arena_bf[:, A5["MASK"]:A5["MASK"] + n_par * 512]
    h_bf = c.arena_bf[:, A["H"]:A["H"] + FC * T]
    ao = c.arena_bf[:, A["AO"]:A["AO"] + FC * T]
    qh2 = c.arena_bf[:, A5["QH"]:A5["QH"] + T]
    kh2 = c.arena_bf[:, A5["KH"]:A5["KH"] + T]
    vps = [c.arena_bf[:, A5["VP"]:A5["VP"] + 16 * 65],
           c.arena_bf[:, A5["VP2"]:A5["VP2"] + 16 * 65]]
    ptm = c.arena_bf[:, A5["PM"]:A5["PM"] + 512]
    pt = c.arena_bf[:, A["P"]:A["P"] + 2048]
    stage0 = c.arena_f32[:, 0:512]
    stage1 = c.arena_f32[:, 512:1024]
    stage2 = c.arena_f32[:, 1024:1536]
    stage3 = c.arena_f32[:, 1536:2048]
    smalls = c.ln_small

    bq_t = c.sb("bq_t", [128, 8], F32)
    bv_t = c.sb("bv_t", [128, 8], F32)
    bo_t = c.sb("bo_t", [128, 8], F32)
    bqs = c.sb("bqs", [128, 8], F32)
    gab1 = c.scr_small[:, 40:48]

    classes, partials = l1_tile_classes()
    pidx = {tt_: i for i, tt_ in enumerate(partials)}

    @c.block.gpsimd
    def _(g):
        for wd, dst in ((wq_d, wqb), (wk_d, wkb), (wv_d, wvb), (wo_d, wob)):
            c.qdma(g, ld, dst.rearrange("p (f o) -> p f o", f=FC),
                   wd.rearrange("(f p) o -> p f o", p=128))
        c.qdma(g, ld, maskb.rearrange("p (n q) -> p n q", n=n_par),
               mask_d.rearrange("(n p) q -> p n q", p=128))

    lds = c.sem("a5lds")

    @c.block.sync
    def _(sync):
        c.qdma(sync, lds, bq_t[:], bq_d)
        c.qdma(sync, lds, bv_t[:], bv_d)
        c.qdma(sync, lds, bo_t[:], bo_d)

    @c.block.vector
    def _(v, ldn=ld.n, ldsn=lds.n):
        v.wait_ge(ld.h, ldn)
        v.wait_ge(lds.h, ldsn)
        st_ = c.chain(v)
        st_(v.tensor_scalar(bqs[:], bq_t[:], L1_QK_SCALE, None, ALU.mult))
        st_(v.tensor_tensor(gab1, modT[:, 16:24], bo_t[:], ALU.mult))
        st_(v.memset(vps[0].rearrange("p (n w) -> p n w", w=65)[:, :, 64:65], 1.0))
        ins = v.memset(vps[1].rearrange("p (n w) -> p n w", w=65)[:, :, 64:65], 1.0)
        dv.inc(ins)
    c.barrier()

    for hp in range(L1_HEADS // 2):
        pe0, ac0, dv0 = pe.n, ac.n, dv.n

        @c.block.tensor
        def _(t, hp=hp, pe0=pe0, dv0=dv0):
            for qtb in range(4):
                if qtb >= 1:
                    t.wait_ge(dv.h, dv0 + 2 * qtb - 1)
                for fc in range(FC):
                    ins = t.matmul(
                        c.psX[:, 0:512],
                        wqb[:, fc * 1024 + hp * 128: fc * 1024 + hp * 128 + 128],
                        h_bf[:, fc * T + qtb * 512: fc * T + qtb * 512 + 512],
                        start=(fc == 0), stop=(fc == FC - 1))
                ins.then_inc(pe.h, 1)
                if qtb >= 1:
                    t.wait_ge(dv.h, dv0 + 2 * qtb)
                for fc in range(FC):
                    ins = t.matmul(
                        c.psO[:, 0:512],
                        wkb[:, fc * 1024 + hp * 128: fc * 1024 + hp * 128 + 128],
                        h_bf[:, fc * T + qtb * 512: fc * T + qtb * 512 + 512],
                        start=(fc == 0), stop=(fc == FC - 1))
                ins.then_inc(pe.h, 1)
            for tg in range(4):
                if tg >= 1:
                    t.wait_ge(dv.h, dv0 + 8 + tg)
                for j in range(4):
                    tc = tg * 4 + j
                    for fc in range(FC):
                        ins = t.matmul(
                            c.psS1[:, j * 256:j * 256 + 128],
                            h_bf[:, fc * T + tc * 128: fc * T + tc * 128 + 128],
                            wvb[:, fc * 1024 + hp * 128: fc * 1024 + hp * 128 + 128],
                            start=(fc == 0), stop=(fc == FC - 1))
                ins.then_inc(pe.h, 1)

        @c.block.vector
        def _(v, hp=hp, pe0=pe0):
            st_ = c.chain(v)
            for qtb in range(4):
                v.wait_ge(pe.h, pe0 + 2 * qtb + 1)
                v.tensor_scalar(qh2[:, qtb * 512:(qtb + 1) * 512],
                                c.psX[:, 0:512], L1_QK_SCALE,
                                bqs[:, hp:hp + 1],
                                ALU.mult, ALU.add).then_inc(dv.h, 1)
                v.wait_ge(pe.h, pe0 + 2 * qtb + 2)
                v.tensor_copy(kh2[:, qtb * 512:(qtb + 1) * 512],
                              c.psO[:, 0:512]).then_inc(dv.h, 1)
            for tg in range(4):
                v.wait_ge(pe.h, pe0 + 8 + tg + 1)
                for j in range(4):
                    tc = tg * 4 + j
                    st_(v.tensor_copy(vps[0][:, tc * 65:tc * 65 + 64],
                                      c.psS1[:, j * 256:j * 256 + 64]))
                    ins = v.tensor_copy(vps[1][:, tc * 65:tc * 65 + 64],
                                        c.psS1[:, j * 256 + 64:j * 256 + 128])
                ins.then_inc(dv.h, 1)
        pe.n += 12
        dv.n += 12
        c.barrier()

        sc = c.sem("a5sc")
        pvs = c.sem("a5pv")
        bc = c.sem("a5bc")
        for sub in range(2):
          head = 2 * hp + sub
          hb = sub * 64
          vpp = vps[sub]
          for qtb in range(4):
            allowed = [k for k in range(16) if classes[(qtb, k)] != "zero"]
            groups = [allowed[i:i + 2] for i in range(0, len(allowed), 2)]
            ng = len(groups)
            ac0, dv0 = ac.n, dv.n
            sc0, pv0, bc0 = sc.n, pvs.n, bc.n

            @c.block.tensor
            def _(t, head=head, hb=hb, vpp=vpp, qtb=qtb, ac0=ac0, dv0=dv0,
                  sc0=sc0, pv0=pv0, groups=groups, ng=ng):
                def scores(i):
                    slot = c.psS1 if i % 2 == 0 else c.psS2
                    if i >= 2:
                        t.wait_ge(ac.h, ac0 + i - 1)
                    for j, ktc in enumerate(groups[i]):
                        ins = t.matmul(slot[:, j * 512:(j + 1) * 512],
                                       kh2[hb:hb + 64,
                                           ktc * 128:(ktc + 1) * 128],
                                       qh2[hb:hb + 64,
                                           qtb * 512:(qtb + 1) * 512],
                                       start=True, stop=True)
                    ins.then_inc(sc.h, 1)

                def pv(i):
                    t.wait_ge(dv.h, dv0 + i + 1)
                    pslot = pt[:, (i % 2) * 1024:(i % 2) * 1024 + 1024]
                    for j, ktc in enumerate(groups[i]):
                        first = (i == 0 and j == 0)
                        last = (i == ng - 1 and j == len(groups[i]) - 1)
                        ins = t.matmul(c.psO[0:65, 0:512],
                                       vpp[:, ktc * 65:(ktc + 1) * 65],
                                       pslot[:, j * 512:(j + 1) * 512],
                                       start=first, stop=last)
                    ins.then_inc(pvs.h, 1)

                scores(0)
                if ng > 1:
                    scores(1)
                for i in range(ng - 2):
                    pv(i)
                    scores(i + 2)
                if ng > 1:
                    pv(ng - 2)
                pv(ng - 1)
                t.wait_ge(dv.h, dv0 + ng + 1)
                t.matmul(c.psX[:, 0:512], c.ones1f[:],
                         smalls[0:1, 512:1024], start=True, stop=True
                         ).then_inc(bc.h, 1)
            sc.n = sc0 + ng
            pvs.n = pv0 + ng
            bc.n = bc0 + 1

            @c.block.scalar
            def _(s, qtb=qtb, ac0=ac0, sc0=sc0, pv0=pv0, groups=groups, ng=ng):
                for i in range(ng):
                    s.wait_ge(sc.h, sc0 + i + 1)
                    if i >= 2:
                        s.wait_ge(pvs.h, pv0 + i - 1)
                    slot = c.psS1 if i % 2 == 0 else c.psS2
                    w = 512 * len(groups[i])
                    s.activation(pt[:, (i % 2) * 1024:(i % 2) * 1024 + w],
                                 slot[:, 0:w], AF.Exp, bias=-EXPS
                                 ).then_inc(ac.h, 1)
                s.wait_ge(pvs.h, pv0 + ng)
                s.copy(stage0[0:65, :], c.psO[0:65, 0:512]).then_inc(ac.h, 1)
            ac.n = ac0 + ng + 1

            @c.block.vector
            def _(v, head=head, hb=hb, qtb=qtb, ac0=ac0, dv0=dv0, bc0=bc0,
                  groups=groups, ng=ng):
                st_ = c.chain(v)
                for i in range(ng):
                    v.wait_ge(ac.h, ac0 + i + 1)
                    for j, ktc in enumerate(groups[i]):
                        if classes[(qtb, ktc)] == "partial":
                            pi = pidx[(qtb, ktc)]
                            seg = pt[:, (i % 2) * 1024 + j * 512:
                                     (i % 2) * 1024 + j * 512 + 512]
                            st_(v.tensor_tensor(
                                ptm[:, 0:512], seg,
                                maskb[:, pi * 512:(pi + 1) * 512], ALU.mult))
                            st_(v.tensor_copy(seg, ptm[:, 0:512]))
                    ins = v.memset(c.barcell[:, 3:4], 0.0)
                    dv.inc(ins)
                    v.wait_ge(dv.h, dv.n)
                v.wait_ge(ac.h, ac0 + ng + 1)
                ins = v.reciprocal(smalls[0:1, 512:1024], stage0[64:65, 0:512])
                dv.inc(ins)
                v.wait_ge(bc.h, bc0 + 1)
                st_(v.tensor_tensor(stage1[0:64, :], stage0[0:64, :],
                                    c.psX[0:64, 0:512], ALU.mult))
                ins = v.tensor_scalar(
                    ao[hb:hb + 64,
                       (head // 2) * T + qtb * 512:
                       (head // 2) * T + qtb * 512 + 512],
                    stage1[0:64, :], 1.0,
                    bv_t[hb:hb + 64, head // 2:head // 2 + 1],
                    ALU.mult, ALU.add)
                dv.inc(ins)
            c.barrier()

    pe0, ac0, dv0, st0, ld0 = pe.n, ac.n, dv.n, st.n, ld.n
    lds0 = lds.n
    ntile = 32

    def tile_oc_qtb(i):
        return i % 8, i // 8

    @c.block.sync
    def _(sync, dv0=dv0, lds0=lds0):
        for i in range(ntile):
            oc, qtb = tile_oc_qtb(i)
            if i >= 2:
                sync.wait_ge(dv.h, dv0 + i - 1)
            c.qdma(sync, lds, (stage1 if i % 2 == 0 else stage2),
                   xs3[:, oc, qtb * 512:(qtb + 1) * 512])

    @c.block.tensor
    def _(t, pe0=pe0, dv0=dv0):
        for i in range(ntile):
            oc, qtb = tile_oc_qtb(i)
            slot = c.psS1 if i % 2 == 0 else c.psS2
            if i >= 2:
                t.wait_ge(dv.h, dv0 + i - 1)
            for fc in range(FC):
                ins = t.matmul(
                    slot[:, 0:512],
                    wob[:, fc * 1024 + oc * 128: fc * 1024 + oc * 128 + 128],
                    ao[:, fc * T + qtb * 512: fc * T + qtb * 512 + 512],
                    start=(fc == 0), stop=(fc == FC - 1))
            ins.then_inc(pe.h, 1)
    pe.n = pe0 + ntile

    @c.block.vector
    def _(v, pe0=pe0, dv0=dv0, st0=st0, lds0=lds0):
        st_ = c.chain(v)
        for i in range(ntile):
            oc, qtb = tile_oc_qtb(i)
            slot = c.psS1 if i % 2 == 0 else c.psS2
            v.wait_ge(pe.h, pe0 + i + 1)
            v.wait_ge(lds.h, lds0 + 16 * (i + 1))
            if i >= 1:
                v.wait_ge(st.h, st0 + 16 * i)
            st_(v.tensor_scalar(stage0[:, 0:512], slot[:, 0:512],
                                modT[:, 16 + oc:16 + oc + 1],
                                gab1[:, oc:oc + 1],
                                ALU.mult, ALU.add))
            v.tensor_tensor(stage3[:],
                            (stage1 if i % 2 == 0 else stage2),
                            stage0[:, 0:512], ALU.add).then_inc(dv.h, 1)
    dv.n = dv0 + ntile

    @c.block.gpsimd
    def _(g, dv0=dv0, st0=st0):
        for i in range(ntile):
            oc, qtb = tile_oc_qtb(i)
            g.wait_ge(dv.h, dv0 + i + 1)
            c.qdma(g, st, xs3[:, oc, qtb * 512:(qtb + 1) * 512], stage3[:])
    c.barrier()


LA_CONSTS = [L0_QK_SCALE, -EXPS, 1e-5, -0.5, L1_QK_SCALE]
A_MAP = dict(CTX=0, WQ=24576, WK=32768, WV=38912, H=45056, AO=61440,
             QH=77824, KH=79872, VH=83968, P=88064)


def build_launchA(upto="attn1", dbg=()):
    nc = bass.Bass("TRN2", target_bir_lowering=False)
    dram = {}

    def din(name, shape):
        dram[name] = nc.dram_tensor(name, shape, F32, kind="ExternalInput").ap()
        return dram[name]

    def dout(name, shape):
        dram[name] = nc.dram_tensor(name, shape, F32, kind="ExternalOutput").ap()
        return dram[name]

    x_d = din("x", [D, T])
    ctx_d = din("ctx", [CTX_D, CTX_M])
    wq_d = din("l0wq", [D, D])
    wk_d = din("l0wk", [CTX_D, D])
    wv_d = din("l0wv", [CTX_D, D])
    wout_d = din("l0wo", [D, D])
    bout_d = din("l0bout", [128, 8])
    c0_d = din("c0", [128, 8])
    wmod0_d = din("wmod0", [D, 6 * D])
    x1_d = dout("x1", [D, T])
    wg_d = din("l0wg", [I_FFN, D])
    wu_d = din("l0wu", [I_FFN, D])
    wd_d = din("l0wd", [D, I_FFN])
    _, _partials = l1_tile_classes()
    NPAR = len(_partials)
    l1wq_d = din("l1wq", [D, D])
    l1wk_d = din("l1wk", [D, D])
    l1wv_d = din("l1wv", [D, D])
    l1wo_d = din("l1wo", [D, D])
    l1bq_d = din("l1bq", [128, 8])
    l1bv_d = din("l1bv", [128, 8])
    l1bo_d = din("l1bo", [128, 8])
    l1mask_d = din("l1mask", [NPAR * 128, 512])
    c1_d = din("c1", [128, 8])
    wmod1_d = din("wmod1", [D, 6 * D])
    want_hb = any(n == "dbg_hb" for n, _ in dbg)
    h0scr_d = nc.dram_tensor("h0scr", [D, T], F32,
                             kind=("ExternalOutput" if want_hb else "Internal")).ap()
    if want_hb:
        dram["h0scr"] = h0scr_d
    full = (upto == "full")
    if full:
        h1_d = dout("h1", [D, T])
        lm_d = dout("lm", [1, 1])
        modg_d = dout("modg", [128, 8])

    dbg_outs = {}
    for name, shape in dbg:
        if name == "dbg_hb":
            continue
        dbg_outs[name] = dout(name, shape)

    with ExitStack() as stack:
        c = Ctx(nc, stack, None)
        c.register_consts(LA_CONSTS)
        c.arena_bf = c.sb("arena_bf", [128, 90112], BF16)[:]
        c.arena_f32 = c.sb("arena_f32", [128, 7168], F32)[:]
        c.ln_small = c.arena_f32[0:1, 5120:6656]
        c.psS1 = c.psum("psS1", [128, 1024])[:]
        c.psS2 = c.psum("psS2", [128, 1024])[:]
        c.psO = c.psum("psO", [128, 512])[:]
        c.psX = c.psum("psX", [128, 512])[:]
        c.psSum = c.psum("psSum", [1, 512])[:]
        c.psT = c.psum("psT", [128, 1024], BF16)[:]
        c.modT0 = c.sb("modT0", [128, 64], F32)[:]
        c.modT1 = c.sb("modT1", [128, 64], F32)[:]
        c.A = A_MAP
        c.A_SCRATCH = 0
        block = stack.enter_context(nc.Block())
        c.block = block

        setup_common(c)
        emit_mod(c, "0", wmod0_d, c0_d, c.modT0)
        x_in3 = x_d.rearrange("(f p) t -> p f t", p=128)
        xs3 = x1_d.rearrange("(f p) t -> p f t", p=128)
        emit_ln(c, "a0", x_in3, c.arena_bf[:, A_MAP["H"]:A_MAP["H"] + FC * T],
                c.modT0, 48, 0)
        if upto == "ln0":
            pass
        else:
            emit_l0_attn(c, ctx_d, wq_d, wk_d, wv_d, wout_d, bout_d,
                         x_in3, xs3, c.modT0)
        h0scr3 = h0scr_d.rearrange("(f p) t -> p f t", p=128)
        if upto in ("ffn0", "lnb0", "attn1", "full"):
            emit_ln(c, "b0", xs3, None, c.modT0, 56, 24, h_f32_dram3=h0scr3)
        if upto in ("ffn0", "attn1", "full"):
            emit_l0_ffn(c, wg_d, wu_d, wd_d, xs3, h0scr3, c.modT0)
        if upto in ("attn1", "full"):
            emit_mod(c, "1", wmod1_d, c1_d, c.modT1)
            emit_ln(c, "a1", xs3, c.arena_bf[:, A_MAP["H"]:A_MAP["H"] + FC * T],
                    c.modT1, 48, 0)
            emit_l1_attn(c, l1wq_d, l1wk_d, l1wv_d, l1wo_d, l1bq_d, l1bv_d,
                         l1bo_d, l1mask_d, NPAR, xs3, c.modT1)
        if full:
            am = c.scr_small[:, 18:19]
            h13 = h1_d.rearrange("(f p) t -> p f t", p=128)
            emit_ln(c, "b1", xs3, None, c.modT1, 56, 24,
                    h_f32_dram3=h13, absmax_to=am)
            fin2 = c.sem("fin2")

            @c.block.tensor
            def _(t):
                t.transpose(c.psX[0:1, 0:128], am, c.identf[:]
                            ).then_inc(fin2.h, 1)

            @c.block.vector
            def _(v):
                v.wait_ge(fin2.h, 1)
                v.tensor_reduce(c.scr_small[0:1, 20:21], c.psX[0:1, 0:128],
                                AX.X, ALU.max).then_inc(fin2.h, 1)

            @c.block.gpsimd
            def _(g):
                g.wait_ge(fin2.h, 2)
                c.qdma(g, c.sem("finst"), lm_d, c.scr_small[0:1, 20:21])
                c.qdma(g, c.sem("finst"), modg_d, c.modT1[:, 40:48])
            c.barrier()

        fin = c.sem("fin")

        @c.block.gpsimd
        def _(g):
            for name in dbg_outs:
                if name == "dbg_h":
                    c.qdma(g, fin, dbg_outs[name],
                           c.arena_bf[:, A_MAP["H"]:A_MAP["H"] + FC * T])
                elif name == "dbg_modT":
                    c.qdma(g, fin, dbg_outs[name], c.modT0)
                elif name == "dbg_ao":
                    c.qdma(g, fin, dbg_outs[name],
                           c.arena_bf[:, A_MAP["AO"]:A_MAP["AO"] + FC * T])
                elif name == "dbg_small":
                    c.qdma(g, fin, dbg_outs[name], c.ln_small)
            g.wait_ge(fin.h, fin.n)
        c.barrier()
    return nc



def _prep_common(inp):
    out = {}
    ctx = np.asarray(inp["L0_attn_context"], np.float32).reshape(CTX_M, CTX_D)
    out["ctx"] = np.ascontiguousarray(ctx.T)
    out["l0wq"] = np.ascontiguousarray(np.asarray(inp["L0_attn_Wq"], np.float32))
    out["l0wk"] = np.ascontiguousarray(np.asarray(inp["L0_attn_Wk"], np.float32))
    out["l0wv"] = np.ascontiguousarray(np.asarray(inp["L0_attn_Wv"], np.float32))
    out["l0wo"] = np.ascontiguousarray(np.asarray(inp["L0_attn_Wout"], np.float32))
    out["l0bout"] = np.ascontiguousarray(
        np.asarray(inp["L0_attn_bout"], np.float32).reshape(8, 128).T)
    out["wmod0"] = np.ascontiguousarray(np.asarray(inp["L0_wmod"], np.float32))
    out["l0wg"] = np.ascontiguousarray(np.asarray(inp["L0_ffn_w_gate"], np.float32))
    out["l0wu"] = np.ascontiguousarray(np.asarray(inp["L0_ffn_w_up"], np.float32))
    out["l0wd"] = np.ascontiguousarray(np.asarray(inp["L0_ffn_w_down"], np.float32))
    out["l1wq"] = np.ascontiguousarray(np.asarray(inp["L1_attn_Wq"], np.float32))
    out["l1wk"] = np.ascontiguousarray(np.asarray(inp["L1_attn_Wk"], np.float32))
    out["l1wv"] = np.ascontiguousarray(np.asarray(inp["L1_attn_Wv"], np.float32))
    out["l1wo"] = np.ascontiguousarray(np.asarray(inp["L1_attn_Wo"], np.float32))
    for nm, key in (("l1bq", "L1_attn_bq"), ("l1bv", "L1_attn_bv"),
                    ("l1bo", "L1_attn_bo")):
        out[nm] = np.ascontiguousarray(
            np.asarray(inp[key], np.float32).reshape(8, 128).T)
    out["l1mask"] = l1_mask_array().reshape(-1, 512)
    out["wmod1"] = np.ascontiguousarray(np.asarray(inp["L1_wmod"], np.float32))
    return out


def _prep_core(inp, core):
    out = {}
    out["x"] = np.ascontiguousarray(
        np.asarray(inp["L0_attn_x"][core], np.float32).T)
    out["c0"] = np.ascontiguousarray(
        np.asarray(inp["L0_c"][core], np.float32).reshape(8, 128).T)
    out["c1"] = np.ascontiguousarray(
        np.asarray(inp["L1_c"][core], np.float32).reshape(8, 128).T)
    return out



def _emit_round_halfup(c, v, vtile, itile, d1, d2, W, out, clamp):
    st_ = c.chain(v)
    st_(v.tensor_copy(itile, vtile))
    st_(v.tensor_tensor(d1, vtile, itile, ALU.subtract))
    st_(v.tensor_scalar(d2, d1, 0.5, None, ALU.is_ge))
    st_(v.tensor_tensor(d2, d2, itile, ALU.add))
    st_(v.tensor_scalar(d1, d1, -0.5, None, ALU.is_lt))
    if clamp:
        st_(v.tensor_tensor(d2, d2, d1, ALU.subtract))
        return v.tensor_scalar(out, d2, -128.0, 127.0, ALU.max, ALU.min)
    return v.tensor_tensor(out, d2, d1, ALU.subtract)


def build_launchB():
    nc = bass.Bass("TRN2", target_bir_lowering=False)
    h1_d = nc.dram_tensor("h1", [D, T], F32, kind="ExternalInput").ap()
    mx_d = nc.dram_tensor("maxes", [1, 8], F32, kind="ExternalInput").ap()
    clip_d = nc.dram_tensor("clipx", [1, 1], F32, kind="ExternalInput").ap()
    wg_d = nc.dram_tensor("l1wg", [I_FFN, D], F32, kind="ExternalInput").ap()
    wu_d = nc.dram_tensor("l1wu", [I_FFN, D], F32, kind="ExternalInput").ap()
    h2_d = nc.dram_tensor("h2", [I_FFN, T], F32, kind="ExternalOutput").ap()
    lm2_d = nc.dram_tensor("lm2", [1, 1], F32, kind="ExternalOutput").ap()

    with ExitStack() as stack:
        c = Ctx(nc, stack, None)
        c.register_consts([1.0 / 127.0])
        c.arena_bf = c.sb("arena_bf", [128, 62464], BF16)[:]
        c.arena_f32 = c.sb("arena_f32", [128, 8192], F32)[:]
        c.itile = c.sb("itile", [128, 1024], mybir.dt.int32)[:]
        c.psS1 = c.psum("psS1", [128, 1024])[:]
        c.psX = c.psum("psX", [128, 512])[:]
        c.psT = c.psum("psT", [128, 1024], BF16)[:]
        c.smallf = c.sb("smallf", [1, 16], F32)[:]
        c.swg = c.sb("swg", [128, 22], F32)[:]
        c.swu = c.sb("swu", [128, 22], F32)[:]
        c.svg = c.sb("svg", [128, 22], F32)[:]
        c.svu = c.sb("svu", [128, 22], F32)[:]
        c.rsxb = c.sb("rsxb", [128, 1], F32)[:]
        c.sxb = c.sb("sxb", [128, 1], F32)[:]
        c.am = c.sb("am", [128, 1], F32)[:]
        block = stack.enter_context(nc.Block())
        c.block = block

        hqf = c.arena_bf[:, 0:FC * T]
        wqg = c.arena_bf[:, 16384:16384 + FC * I_FFN]
        wqu = c.arena_bf[:, 38912:38912 + FC * I_FFN]
        wtm = c.arena_bf[:, 61440:61440 + 1024]
        h13 = h1_d.rearrange("(f p) t -> p f t", p=128)
        h23 = h2_d.rearrange("(i p) t -> p i t", p=128)

        setup_common(c)
        ld = c.sem("ld")
        lds = c.sem("lds")
        pe = c.sem("pe")
        ac = c.sem("ac")
        dv = c.sem("dv")
        st = c.sem("st")
        tps = c.sem("tps")
        tpe = c.sem("tpe")

        @c.block.sync
        def _(sync):
            c.qdma(sync, lds, c.smallf[0:1, 0:8], mx_d)
            c.qdma(sync, lds, c.smallf[0:1, 8:9], clip_d)

        @c.block.scalar
        def _(s, ldsn=lds.n):
            s.wait_ge(lds.h, ldsn)
            s.activation(c.smallf[0:1, 9:10], c.smallf[0:1, 8:9], AF.Sigmoid
                         ).then_inc(ac.h, 1)
        ac.n += 1

        @c.block.vector
        def _(v, acn=ac.n):
            v.wait_ge(ac.h, acn)
            st_ = c.chain(v)
            st_(v.tensor_reduce(c.smallf[0:1, 10:11], c.smallf[0:1, 0:8],
                                AX.X, ALU.max))
            st_(v.tensor_tensor(c.smallf[0:1, 11:12], c.smallf[0:1, 10:11],
                                c.smallf[0:1, 9:10], ALU.mult))
            st_(v.tensor_scalar(c.smallf[0:1, 12:13], c.smallf[0:1, 11:12],
                                1.0 / 127.0, None, ALU.mult))
            ins = v.reciprocal(c.smallf[0:1, 13:14], c.smallf[0:1, 12:13])
            dv.inc(ins)

        @c.block.tensor
        def _(t, dvn=dv.n):
            t.wait_ge(dv.h, dvn)
            t.matmul(c.psX[:, 0:1], c.ones1f[:], c.smallf[0:1, 13:14],
                     start=True, stop=True)
            t.matmul(c.psX[:, 1:2], c.ones1f[:], c.smallf[0:1, 12:13],
                     start=True, stop=True).then_inc(pe.h, 1)
        pe.n += 1

        @c.block.vector
        def _(v, pen=pe.n):
            v.wait_ge(pe.h, pen)
            st_ = c.chain(v)
            st_(v.tensor_copy(c.rsxb[:], c.psX[:, 0:1]))
            ins = v.tensor_copy(c.sxb[:], c.psX[:, 1:2])
            dv.inc(ins)
        c.barrier()

        wsrc = c.arena_f32[:, 0:1024]
        vt = c.arena_f32[:, 4096:5120]
        d1 = c.arena_f32[:, 5120:6144]
        d2 = c.arena_f32[:, 6144:7168]
        for wi, (wsrc_d, wdst, sw, sv) in enumerate(
                ((wg_d, wqg, c.swg, c.svg), (wu_d, wqu, c.swu, c.svu))):
            w3 = wsrc_d.rearrange("(n p) k -> p n k", p=128)
            for ch in range(IC):
                @c.block.sync
                def _(sync, w3=w3, ch=ch, dvn=dv.n):
                    sync.wait_ge(dv.h, dvn)
                    c.qdma(sync, ld, wsrc[:, 0:1024], w3[:, ch, :])

                @c.block.vector
                def _(v, ch=ch, sw=sw, ldn=ld.n, tpen=tpe.n):
                    v.wait_ge(ld.h, ldn)
                    v.wait_ge(tpe.h, tpen)
                    st_ = c.chain(v)
                    st_(v.tensor_reduce(sw[:, ch:ch + 1], wsrc[:, 0:1024],
                                        AX.X, ALU.max, apply_absolute_value=True))
                    st_(v.tensor_scalar(sw[:, ch:ch + 1], sw[:, ch:ch + 1],
                                        1.0 / 127.0, None, ALU.mult))
                    st_(v.reciprocal(d1[:, 0:1], sw[:, ch:ch + 1]))
                    st_(v.tensor_tensor(vt[:, 0:1024], wsrc[:, 0:1024],
                                        d1[:, 0:1].broadcast_to((128, 1024)),
                                        ALU.mult))
                    ins = _emit_round_halfup(c, v, vt[:, 0:1024],
                                             c.itile[:, 0:1024],
                                             d1[:, 0:1024], d2[:, 0:1024],
                                             1024, wtm[:, 0:1024], False)
                    ins.then_inc(dv.h, 1)
                dv.n += 1

                @c.block.tensor
                def _(t, dvn=dv.n, tps0=tps.n):
                    t.wait_ge(dv.h, dvn)
                    for g4 in range(2):
                        if tps0 + g4 >= 1:
                            t.wait_ge(tpe.h, tps0 + g4)
                        for j in range(4):
                            trc = g4 * 4 + j
                            ins = t.transpose(c.psT[:, j * 128:(j + 1) * 128],
                                              wtm[:, trc * 128:(trc + 1) * 128],
                                              c.identb[:])
                        ins.then_inc(tps.h, 1)

                @c.block.scalar
                def _(s, ch=ch, wdst=wdst, tps0=tps.n):
                    for g4 in range(2):
                        s.wait_ge(tps.h, tps0 + g4 + 1)
                        for j in range(4):
                            trc = g4 * 4 + j
                            ins = s.copy(
                                wdst[:, trc * I_FFN + ch * 128:
                                     trc * I_FFN + ch * 128 + 128],
                                c.psT[:, j * 128:(j + 1) * 128])
                        ins.then_inc(tpe.h, 1)
                tps.n += 2
                tpe.n += 2
        c.barrier()

        @c.block.vector
        def _(v):
            st_ = c.chain(v)
            st_(v.tensor_tensor(c.svg[:], c.swg[:],
                                c.sxb[:].broadcast_to((128, 22)), ALU.mult))
            st_(v.tensor_tensor(c.svu[:], c.swu[:],
                                c.sxb[:].broadcast_to((128, 22)), ALU.mult))
            st_(v.memset(c.am[:], 0.0))
        c.barrier()

        xf = c.arena_f32[:, 0:4096]
        for g in range(4):
            @c.block.sync
            def _(sync, g=g, dvn=dv.n):
                if g >= 1:
                    sync.wait_ge(dv.h, dvn)
                c.qdma(sync, ld, xf.rearrange("p (f t) -> p f t", f=8),
                       h13[:, :, g * 512:(g + 1) * 512])

            @c.block.vector
            def _(v, g=g, ldn=ld.n):
                v.wait_ge(ld.h, ldn)
                st_ = c.chain(v)
                for half in range(4):
                    seg = xf[:, half * 1024:(half + 1) * 1024]
                    st_(v.tensor_tensor(vt[:, 0:1024], seg,
                                        c.rsxb[:].broadcast_to((128, 1024)),
                                        ALU.mult))
                    ins = _emit_round_halfup(
                        c, v, vt[:, 0:1024], c.itile[:, 0:1024],
                        d1[:, 0:1024], d2[:, 0:1024], 1024,
                        wtm[:, 0:1024], True)
                    st_(ins)
                    for k in range(2):
                        fc = half * 2 + k
                        ins = v.tensor_copy(
                            hqf[:, fc * T + g * 512: fc * T + g * 512 + 512],
                            wtm[:, k * 512:(k + 1) * 512])
                        if not (half == 3 and k == 1):
                            st_(ins)
                ins.then_inc(dv.h, 1)
            dv.n += 1
            c.barrier()

        h2a = c.arena_f32[:, 0:512]
        h2b = c.arena_f32[:, 512:1024]
        sgt = c.arena_f32[:, 1024:1536]
        for ic in range(IC):
            for tb in range(4):
                idx = ic * 4 + tb

                @c.block.tensor
                def _(t, ic=ic, tb=tb, idx=idx, dvn=dv.n):
                    if idx >= 1:
                        t.wait_ge(dv.h, dvn)
                    for fc in range(FC):
                        t.matmul(c.psS1[:, 0:512],
                                 wqg[:, fc * I_FFN + ic * 128:
                                     fc * I_FFN + ic * 128 + 128],
                                 hqf[:, fc * T + tb * 512: fc * T + tb * 512 + 512],
                                 start=(fc == 0), stop=(fc == FC - 1))
                    for fc in range(FC):
                        ins = t.matmul(c.psS1[:, 512:1024],
                                       wqu[:, fc * I_FFN + ic * 128:
                                           fc * I_FFN + ic * 128 + 128],
                                       hqf[:, fc * T + tb * 512:
                                           fc * T + tb * 512 + 512],
                                       start=(fc == 0), stop=(fc == FC - 1))
                    ins.then_inc(pe.h, 1)
                pe.n += 1

                @c.block.scalar
                def _(s, ic=ic, pen=pe.n):
                    s.wait_ge(pe.h, pen)
                    s.activation(sgt, c.psS1[:, 0:512], AF.Silu,
                                 scale=c.svg[:, ic:ic + 1]).then_inc(ac.h, 1)
                ac.n += 1

                @c.block.vector
                def _(v, ic=ic, tb=tb, idx=idx, acn=ac.n, stn=st.n):
                    v.wait_ge(ac.h, acn)
                    if idx >= 2:
                        v.wait_ge(st.h, stn - 16)
                    tile = h2a if idx % 2 == 0 else h2b
                    st_ = c.chain(v)
                    st_(v.scalar_tensor_tensor(tile, c.psS1[:, 512:1024],
                                               c.svu[:, ic:ic + 1], sgt,
                                               ALU.mult, ALU.mult))
                    st_(v.tensor_reduce(d2[:, 0:1], tile, AX.X, ALU.max,
                                        apply_absolute_value=True))
                    v.tensor_tensor(c.am[:], c.am[:], d2[:, 0:1], ALU.max
                                    ).then_inc(dv.h, 1)
                dv.n += 1

                @c.block.gpsimd
                def _(g2, ic=ic, tb=tb, idx=idx, dvn=dv.n):
                    g2.wait_ge(dv.h, dvn)
                    tile = h2a if idx % 2 == 0 else h2b
                    c.qdma(g2, st, h23[:, ic, tb * 512:(tb + 1) * 512], tile)
        c.barrier()

        fin2 = c.sem("fin2")

        @c.block.tensor
        def _(t):
            t.transpose(c.psX[0:1, 0:128], c.am[:], c.identf[:]
                        ).then_inc(fin2.h, 1)

        @c.block.vector
        def _(v):
            v.wait_ge(fin2.h, 1)
            v.tensor_reduce(c.smallf[0:1, 14:15], c.psX[0:1, 0:128],
                            AX.X, ALU.max).then_inc(fin2.h, 1)

        @c.block.gpsimd
        def _(g):
            g.wait_ge(fin2.h, 2)
            c.qdma(g, c.sem("finst"), lm2_d, c.smallf[0:1, 14:15])
        c.barrier()
    return nc


def build_launchC():
    nc = bass.Bass("TRN2", target_bir_lowering=False)
    h2_d = nc.dram_tensor("h2", [I_FFN, T], F32, kind="ExternalInput").ap()
    mx_d = nc.dram_tensor("maxes2", [1, 8], F32, kind="ExternalInput").ap()
    clip_d = nc.dram_tensor("clipd", [1, 1], F32, kind="ExternalInput").ap()
    wd_d = nc.dram_tensor("l1wd", [D, I_FFN], F32, kind="ExternalInput").ap()
    x1_d = nc.dram_tensor("x1", [D, T], F32, kind="ExternalInput").ap()
    modg_d = nc.dram_tensor("modg", [128, 8], F32, kind="ExternalInput").ap()
    out_d = nc.dram_tensor("out", [D, T], F32, kind="ExternalOutput").ap()

    with ExitStack() as stack:
        c = Ctx(nc, stack, None)
        c.register_consts([1.0 / 127.0])
        c.arena_bf = c.sb("arena_bf", [128, 70400], BF16)[:]
        c.arena_f32 = c.sb("arena_f32", [128, 11776], F32)[:]
        c.itile = c.sb("itile", [128, 2816], mybir.dt.int32)[:]
        c.psS1 = c.psum("psS1", [128, 1024])[:]
        c.psX = c.psum("psX", [128, 512])[:]
        c.psT = c.psum("psT", [128, 1024], BF16)[:]
        c.smallf = c.sb("smallf", [1, 16], F32)[:]
        c.swd = c.sb("swd", [128, 8], F32)[:]
        c.coef = c.sb("coef", [128, 8], F32)[:]
        c.rsdb = c.sb("rsdb", [128, 1], F32)[:]
        c.sdb = c.sb("sdb", [128, 1], F32)[:]
        c.modg = c.sb("modgs", [128, 8], F32)[:]
        block = stack.enter_context(nc.Block())
        c.block = block

        h2qi = c.arena_bf[:, 0:IC * T]
        wdq = c.arena_bf[:, IC * T:IC * T + IC * 1024]
        wtm = c.arena_bf[:, IC * T + IC * 1024:IC * T + IC * 1024 + I_FFN]
        h23 = h2_d.rearrange("(i p) t -> p i t", p=128)
        x13 = x1_d.rearrange("(f p) t -> p f t", p=128)
        o3 = out_d.rearrange("(f p) t -> p f t", p=128)

        setup_common(c)
        ld = c.sem("ld")
        lds = c.sem("lds")
        pe = c.sem("pe")
        ac = c.sem("ac")
        dv = c.sem("dv")
        st = c.sem("st")
        tps = c.sem("tps")
        tpe = c.sem("tpe")

        @c.block.sync
        def _(sync):
            c.qdma(sync, lds, c.smallf[0:1, 0:8], mx_d)
            c.qdma(sync, lds, c.smallf[0:1, 8:9], clip_d)
            c.qdma(sync, lds, c.modg[:], modg_d)

        @c.block.scalar
        def _(s, ldsn=lds.n):
            s.wait_ge(lds.h, ldsn)
            s.activation(c.smallf[0:1, 9:10], c.smallf[0:1, 8:9], AF.Sigmoid
                         ).then_inc(ac.h, 1)
        ac.n += 1

        @c.block.vector
        def _(v, acn=ac.n):
            v.wait_ge(ac.h, acn)
            st_ = c.chain(v)
            st_(v.tensor_reduce(c.smallf[0:1, 10:11], c.smallf[0:1, 0:8],
                                AX.X, ALU.max))
            st_(v.tensor_tensor(c.smallf[0:1, 11:12], c.smallf[0:1, 10:11],
                                c.smallf[0:1, 9:10], ALU.mult))
            st_(v.tensor_scalar(c.smallf[0:1, 12:13], c.smallf[0:1, 11:12],
                                1.0 / 127.0, None, ALU.mult))
            ins = v.reciprocal(c.smallf[0:1, 13:14], c.smallf[0:1, 12:13])
            dv.inc(ins)

        @c.block.tensor
        def _(t, dvn=dv.n):
            t.wait_ge(dv.h, dvn)
            t.matmul(c.psX[:, 0:1], c.ones1f[:], c.smallf[0:1, 13:14],
                     start=True, stop=True)
            t.matmul(c.psX[:, 1:2], c.ones1f[:], c.smallf[0:1, 12:13],
                     start=True, stop=True).then_inc(pe.h, 1)
        pe.n += 1

        @c.block.vector
        def _(v, pen=pe.n):
            v.wait_ge(pe.h, pen)
            st_ = c.chain(v)
            st_(v.tensor_copy(c.rsdb[:], c.psX[:, 0:1]))
            ins = v.tensor_copy(c.sdb[:], c.psX[:, 1:2])
            dv.inc(ins)
        c.barrier()

        wsrc = c.arena_f32[:, 0:2816]
        vt = c.arena_f32[:, 2816:5632]
        dd1 = c.arena_f32[:, 5632:8448]
        dd2 = c.arena_f32[:, 8448:11264]
        w3 = wd_d.rearrange("(n p) k -> p n k", p=128)
        for ch in range(FC):
            @c.block.sync
            def _(sync, ch=ch, dvn=dv.n):
                sync.wait_ge(dv.h, dvn)
                c.qdma(sync, ld, wsrc[:, 0:I_FFN], w3[:, ch, :])

            @c.block.vector
            def _(v, ch=ch, ldn=ld.n, tpen=tpe.n):
                v.wait_ge(ld.h, ldn)
                v.wait_ge(tpe.h, tpen)
                st_ = c.chain(v)
                st_(v.tensor_reduce(c.swd[:, ch:ch + 1], wsrc[:, 0:I_FFN],
                                    AX.X, ALU.max, apply_absolute_value=True))
                st_(v.tensor_scalar(c.swd[:, ch:ch + 1], c.swd[:, ch:ch + 1],
                                    1.0 / 127.0, None, ALU.mult))
                st_(v.reciprocal(c.scr_small[:, 0:1], c.swd[:, ch:ch + 1]))
                st_(v.tensor_tensor(vt[:, 0:I_FFN], wsrc[:, 0:I_FFN],
                                    c.scr_small[:, 0:1].broadcast_to((128, I_FFN)),
                                    ALU.mult))
                ins = _emit_round_halfup(c, v, vt[:, 0:I_FFN],
                                         c.itile[:, 0:I_FFN],
                                         dd1[:, 0:I_FFN], dd2[:, 0:I_FFN],
                                         I_FFN, wtm[:, 0:I_FFN], False)
                ins.then_inc(dv.h, 1)
            dv.n += 1

            @c.block.tensor
            def _(t, dvn=dv.n, tps0=tps.n):
                t.wait_ge(dv.h, dvn)
                for g4 in range(6):
                    if tps0 + g4 >= 1:
                        t.wait_ge(tpe.h, tps0 + g4)
                    for j in range(min(4, 22 - g4 * 4)):
                        trc = g4 * 4 + j
                        ins = t.transpose(c.psT[:, j * 128:(j + 1) * 128],
                                          wtm[:, trc * 128:(trc + 1) * 128],
                                          c.identb[:])
                    ins.then_inc(tps.h, 1)

            @c.block.scalar
            def _(s, ch=ch, tps0=tps.n):
                for g4 in range(6):
                    s.wait_ge(tps.h, tps0 + g4 + 1)
                    for j in range(min(4, 22 - g4 * 4)):
                        trc = g4 * 4 + j
                        ins = s.copy(
                            wdq[:, trc * 1024 + ch * 128:
                                trc * 1024 + ch * 128 + 128],
                            c.psT[:, j * 128:(j + 1) * 128])
                    ins.then_inc(tpe.h, 1)
            tps.n += 6
            tpe.n += 6
        c.barrier()

        @c.block.vector
        def _(v):
            st_ = c.chain(v)
            st_(v.tensor_tensor(c.coef[:], c.swd[:],
                                c.sdb[:].broadcast_to((128, 8)), ALU.mult))
            st_(v.tensor_tensor(c.coef[:], c.coef[:], c.modg[:], ALU.mult))
        c.barrier()

        hsrc = c.arena_f32[:, 0:2048]
        vt2 = c.arena_f32[:, 2048:4096]
        for ch in range(IC):
            @c.block.sync
            def _(sync, ch=ch, dvn=dv.n):
                sync.wait_ge(dv.h, dvn)
                c.qdma(sync, ld, hsrc[:, 0:2048], h23[:, ch, :])

            @c.block.vector
            def _(v, ch=ch, ldn=ld.n):
                v.wait_ge(ld.h, ldn)
                st_ = c.chain(v)
                st_(v.tensor_tensor(vt2[:, 0:2048], hsrc[:, 0:2048],
                                    c.rsdb[:].broadcast_to((128, 2048)),
                                    ALU.mult))
                ins = _emit_round_halfup(c, v, vt2[:, 0:2048],
                                         c.itile[:, 0:2048],
                                         dd1[:, 0:2048], dd2[:, 0:2048],
                                         2048, h2qi[:, ch * T:(ch + 1) * T],
                                         True)
                ins.then_inc(dv.h, 1)
            dv.n += 1
            c.barrier()

        xw = c.arena_f32[:, 0:512]
        esg = c.arena_f32[:, 512:1024]
        xo = c.arena_f32[:, 1024:1536]
        for oc in range(FC):
            for tb in range(4):
                idx = oc * 4 + tb

                @c.block.tensor
                def _(t, oc=oc, tb=tb, idx=idx, dvn=dv.n):
                    if idx >= 2:
                        t.wait_ge(dv.h, dvn - 1)
                    slot = c.psS1[:, 0:512] if idx % 2 == 0 else c.psS1[:, 512:1024]
                    for ic in range(IC):
                        ins = t.matmul(slot,
                                       wdq[:, ic * 1024 + oc * 128:
                                           ic * 1024 + oc * 128 + 128],
                                       h2qi[:, ic * T + tb * 512:
                                            ic * T + tb * 512 + 512],
                                       start=(ic == 0), stop=(ic == IC - 1))
                    ins.then_inc(pe.h, 1)
                pe.n += 1

                @c.block.sync
                def _(sync, oc=oc, tb=tb, idx=idx, dvn=dv.n):
                    if idx >= 1:
                        sync.wait_ge(dv.h, dvn)
                    c.qdma(sync, lds, xw, x13[:, oc, tb * 512:(tb + 1) * 512])

                @c.block.vector
                def _(v, oc=oc, tb=tb, idx=idx, pen=pe.n, ldsn=lds.n, stn=st.n):
                    st_ = c.chain(v)
                    v.wait_ge(pe.h, pen)
                    v.wait_ge(lds.h, ldsn)
                    v.wait_ge(st.h, stn)
                    slot = c.psS1[:, 0:512] if idx % 2 == 0 else c.psS1[:, 512:1024]
                    st_(v.tensor_scalar(esg, slot, c.coef[:, oc:oc + 1], None,
                                        ALU.mult))
                    v.tensor_tensor(xo, xw, esg, ALU.add).then_inc(dv.h, 1)
                dv.n += 1

                @c.block.gpsimd
                def _(g, oc=oc, tb=tb, dvn=dv.n):
                    g.wait_ge(dv.h, dvn)
                    c.qdma(g, st, o3[:, oc, tb * 512:(tb + 1) * 512], xo)
        c.barrier()
    return nc



def kernel(**inp):
    common = _prep_common(inp)
    in_maps = []
    for core in range(8):
        m = dict(common)
        m.update(_prep_core(inp, core))
        in_maps.append(m)

    ncA = build_launchA(upto="full")
    resA = run_bass_kernel_spmd(ncA, in_maps, core_ids=list(range(8)))

    maxes = np.ascontiguousarray(
        np.stack([np.asarray(resA.results[cr]["lm"]).ravel()[0]
                  for cr in range(8)]).reshape(1, 8).astype(np.float32))
    clipx = np.asarray(inp["L1_ffn_clip_x"], np.float32).reshape(1, 1)
    wg1 = np.ascontiguousarray(np.asarray(inp["L1_ffn_w_gate"], np.float32))
    wu1 = np.ascontiguousarray(np.asarray(inp["L1_ffn_w_up"], np.float32))
    ncB = build_launchB()
    in_mapsB = [dict(h1=np.asarray(resA.results[cr]["h1"]),
                     maxes=maxes, clipx=clipx, l1wg=wg1, l1wu=wu1)
                for cr in range(8)]
    resB = run_bass_kernel_spmd(ncB, in_mapsB, core_ids=list(range(8)))

    maxes2 = np.ascontiguousarray(
        np.stack([np.asarray(resB.results[cr]["lm2"]).ravel()[0]
                  for cr in range(8)]).reshape(1, 8).astype(np.float32))
    clipd = np.asarray(inp["L1_ffn_clip_down"], np.float32).reshape(1, 1)
    wd1 = np.ascontiguousarray(np.asarray(inp["L1_ffn_w_down"], np.float32))
    ncC = build_launchC()
    in_mapsC = [dict(h2=np.asarray(resB.results[cr]["h2"]),
                     maxes2=maxes2, clipd=clipd, l1wd=wd1,
                     x1=np.asarray(resA.results[cr]["x1"]),
                     modg=np.asarray(resA.results[cr]["modg"]))
                for cr in range(8)]
    resC = run_bass_kernel_spmd(ncC, in_mapsC, core_ids=list(range(8)))

    out = np.stack([np.ascontiguousarray(
        np.asarray(resC.results[cr]["out"]).T) for cr in range(8)])
    return out.astype(np.float32)


# version 06
# speedup vs baseline: 1.9174x; 1.0072x over previous
import numpy as np
from contextlib import ExitStack

import concourse.bass as bass
import concourse.mybir as mybir
from concourse.bass_utils import run_bass_kernel_spmd

F32 = mybir.dt.float32
BF16 = mybir.dt.bfloat16
AF = mybir.ActivationFunctionType
ALU = mybir.AluOpType
AX = mybir.AxisListType

D = 1024
T = 2048
FC = D // 128
CTX_M = 4096
CTX_D = 768
CC = CTX_D // 128
L0_HEADS = 8
L1_HEADS = 16
L1_HD = 64
I_FFN = 2816
IC = I_FFN // 128
EXPS = 30.0
L0_QK_SCALE = 1.5 * float(np.float32(128.0 ** -0.5))
L1_QK_SCALE = float(np.float32(L1_HD ** -0.5))
FP4_THR = [0.25, 0.75, 1.25, 1.75, 2.5, 3.5, 5.0]

CHUNK_FRAMES = 100
LOOKAHEAD = 50


def _l1_tmax(q):
    return max((q // CHUNK_FRAMES) * CHUNK_FRAMES + CHUNK_FRAMES - 1, q + LOOKAHEAD)


def l1_tile_classes():
    classes = {}
    partials = []
    for qtb in range(T // 512):
        q0 = qtb * 512
        for ktc in range(T // 128):
            k0 = ktc * 128
            if k0 + 127 <= _l1_tmax(q0):
                classes[(qtb, ktc)] = "full"
            elif k0 > _l1_tmax(q0 + 511):
                classes[(qtb, ktc)] = "zero"
            else:
                classes[(qtb, ktc)] = "partial"
                partials.append((qtb, ktc))
    return classes, partials


def l1_mask_array():
    _, partials = l1_tile_classes()
    q = np.arange(T)
    tmax = np.maximum((q // CHUNK_FRAMES) * CHUNK_FRAMES + CHUNK_FRAMES - 1,
                      q + LOOKAHEAD)
    out = np.zeros((len(partials), 128, 512), np.float32)
    for i, (qtb, ktc) in enumerate(partials):
        ks = ktc * 128 + np.arange(128)
        qs = qtb * 512 + np.arange(512)
        out[i] = (ks[:, None] <= tmax[qs][None, :]).astype(np.float32)
    return out



class Sem:
    def __init__(self, stack, nc, name):
        self.h = stack.enter_context(nc.semaphore(name))
        self.n = 0

    def inc(self, ins, k=1):
        ins.then_inc(self.h, k)
        self.n += k
        return self.n


class Ctx:
    def __init__(self, nc, stack, block):
        self.nc = nc
        self.stack = stack
        self.block = block
        self._sems = {}
        self._cvals = []

    def sem(self, name):
        if name not in self._sems:
            self._sems[name] = Sem(self.stack, self.nc, name)
        return self._sems[name]

    def sb(self, name, shape, dtype):
        return self.stack.enter_context(self.nc.sbuf_tensor(name, shape, dtype))

    def psum(self, name, shape, dtype=F32):
        return self.stack.enter_context(self.nc.psum_tensor(name, shape, dtype))

    def register_consts(self, vals):
        vals = sorted(set(float(v) for v in vals) - {0.0, 1.0})
        self._cvals = vals
        self.consts = self.sb("consts", [128, max(len(vals), 1)], F32)
        for j, v in enumerate(vals):
            self.nc.const_aps.register(F32, v, self.consts[:, j:j + 1])

    def qdma(self, q, sem, out, in_):
        if sem.n > 0:
            q.wait_ge(sem.h, sem.n)
        ins = q.dma_start(out=out, in_=in_)
        sem.inc(ins, 16)
        return ins

    def chain(self, eng, name="chn"):
        sem = self.sem(name)

        def step(ins):
            sem.inc(ins)
            eng.wait_ge(sem.h, sem.n)
            return ins
        return step

    def barrier(self):
        b = self.sem("bar")
        pending = [(sm.h, sm.n) for nm, sm in self._sems.items()
                   if nm != "bar" and sm.n > 0]
        engines = ("sync", "tensor", "vector", "scalar", "gpsimd")
        target = b.n + len(engines)
        for name in engines:
            dec = getattr(self.block, name)

            def _body(e, name=name, target=target):
                if name == "gpsimd":
                    for h, n in pending:
                        e.wait_ge(h, n)
                    e.memset(self.barcell[:, 2:3], 0.0).then_inc(b.h, 1)
                elif name == "vector":
                    e.memset(self.barcell[:, 0:1], 0.0).then_inc(b.h, 1)
                elif name == "scalar":
                    e.memzero(self.barcell[:, 1:2]).then_inc(b.h, 1)
                else:
                    e.nop().then_inc(b.h, 1)
                e.wait_ge(b.h, target)
            dec(_body)
        b.n = target


def setup_common(c: Ctx):
    c.onesb = c.sb("onesb", [128, 1], BF16)
    c.ones1f = c.sb("ones1f", [1, 128], F32)
    c.identb = c.sb("identb", [128, 128], BF16)
    c.identf = c.sb("identf", [128, 128], F32)
    c.scr_small = c.sb("scr_small", [128, 64], F32)
    c.barcell = c.sb("barcell", [1, 4], F32)

    @c.block.vector
    def _(v):
        if c._cvals:
            for j, val in enumerate(c._cvals):
                v.memset(c.consts[:, j:j + 1], val)
        v.memset(c.onesb[:], 1.0)
        v.memset(c.ones1f[:], 1.0)

    @c.block.gpsimd
    def _(g):
        st_ = c.chain(g, "chg")
        for ident in (c.identb, c.identf):
            st_(g.memset(ident[:], 0.0))
            st_(g.affine_select(out=ident[:], in_=ident[:],
                                compare_op=ALU.not_equal, fill=1.0,
                                base=0, pattern=[[-1, 128]],
                                channel_multiplier=1))
    c.barrier()



def emit_mod(c: Ctx, tag, wmod_d, c_d, modT):
    ld = c.sem(f"md{tag}")
    pe = c.sem(f"mp{tag}")
    ac = c.sem(f"ma{tag}")

    crow = c.sb(f"crow{tag}", [128, 8], F32)
    siluf = c.sb(f"siluf{tag}", [128, 8], F32)
    wmf = c.arena_f32[:, 0:6144]
    wm3 = wmod_d.rearrange("(f p) j -> p f j", p=128)

    @c.block.sync
    def _(sync):
        c.qdma(sync, ld, crow[:], c_d)

    @c.block.scalar
    def _(s, ldn=ld.n):
        s.wait_ge(ld.h, ldn)
        s.activation(siluf[:], crow[:], AF.Silu).then_inc(ac.h, 1)
    ac.n += 1

    pe0, ac0 = pe.n, ac.n
    for piece in range(8):
        @c.block.sync
        def _(sync, piece=piece, pe0=pe0):
            if piece >= 1:
                sync.wait_ge(pe.h, pe0 + piece)
            c.qdma(sync, ld, wmf.rearrange("p (f j) -> p f j", f=8),
                   wm3[:, :, piece * 768:(piece + 1) * 768])

        @c.block.tensor
        def _(t, piece=piece, ldn=ld.n, ac0=ac0):
            t.wait_ge(ld.h, ldn)
            t.wait_ge(ac.h, ac0)
            for jloc in range(6):
                jc = piece * 6 + jloc
                for fc in range(FC):
                    ins = t.matmul(
                        c.psX[:, jc:jc + 1],
                        wmf[:, fc * 768 + jloc * 128: fc * 768 + (jloc + 1) * 128],
                        siluf[:, fc:fc + 1],
                        start=(fc == 0), stop=(fc == FC - 1))
            ins.then_inc(pe.h, 1)
    pe.n = pe0 + 8

    @c.block.vector
    def _(v, pen=pe.n):
        v.wait_ge(pe.h, pen)
        st_ = c.chain(v)
        st_(v.tensor_copy(modT[:, 0:48], c.psX[:, 0:48]))
        for col in (16, 40):
            g = modT[:, col:col + 8]
            sgn = c.scr_small[:, 0:8]
            av = c.scr_small[:, 8:16]
            st_(v.tensor_scalar(sgn, g, 0.0, None, ALU.is_ge))
            st_(v.tensor_scalar(sgn, sgn, 2.0, -1.0, ALU.mult, ALU.add))
            st_(v.tensor_scalar(av, g, -1.0, None, ALU.mult))
            st_(v.tensor_tensor(av, av, g, ALU.max))
            st_(v.tensor_scalar(av, av, 0.25, None, ALU.max))
            st_(v.tensor_tensor(g, sgn, av, ALU.mult))
        st_(v.tensor_scalar(modT[:, 48:56], modT[:, 8:16], 1.0, None, ALU.add))
        st_(v.tensor_scalar(modT[:, 56:64], modT[:, 32:40], 1.0, None, ALU.add))
    c.barrier()



def emit_ln(c: Ctx, tag, x_src3, h_dst, modT, onec_col, s_col,
            h_f32_dram3=None, absmax_to=None):
    ld = c.sem(f"ln_ld{tag}")
    pe = c.sem(f"ln_pe{tag}")
    dv = c.sem(f"ln_dv{tag}")
    ac = c.sem(f"ln_ac{tag}")
    st = c.sem(f"ln_st{tag}")

    xf = c.arena_f32[:, 0:4096]
    hf = xf
    xbb = c.arena_bf[:, c.A_SCRATCH:c.A_SCRATCH + 4096]
    xsq = c.arena_bf[:, c.A_SCRATCH + 4096:c.A_SCRATCH + 8192]
    small = c.ln_small

    if absmax_to is not None:
        @c.block.vector
        def _(v):
            v.memset(absmax_to, 0.0)

    for g in range(4):
        @c.block.sync
        def _(sync, g=g):
            c.qdma(sync, ld, xf.rearrange("p (f t) -> p f t", f=8),
                   x_src3[:, :, g * 512:(g + 1) * 512])

        @c.block.vector
        def _(v, ldn=ld.n):
            v.wait_ge(ld.h, ldn)
            ins = v.tensor_copy(xbb, xf)
            dv.inc(ins)

        @c.block.scalar
        def _(s, dvn=dv.n):
            s.wait_ge(dv.h, dvn)
            ins = s.activation(xsq, xbb, AF.Square)
            ac.inc(ins)

        @c.block.tensor
        def _(t, acn=ac.n):
            t.wait_ge(ac.h, acn)
            for fc in range(FC):
                t.matmul(c.psS1[0:1, 0:512], c.onesb[:],
                         xbb[:, fc * 512:(fc + 1) * 512],
                         start=(fc == 0), stop=(fc == FC - 1))
            for fc in range(FC):
                ins = t.matmul(c.psS1[0:1, 512:1024], c.onesb[:],
                               xsq[:, fc * 512:(fc + 1) * 512],
                               start=(fc == 0), stop=(fc == FC - 1))
            pe.inc(ins)

        @c.block.vector
        def _(v, pen=pe.n):
            v.wait_ge(pe.h, pen)
            st_ = c.chain(v)
            m, varr, lv = (small[0:1, 0:512], small[0:1, 512:1024],
                           small[0:1, 1024:1536])
            st_(v.tensor_scalar(m, c.psS1[0:1, 0:512], 1.0 / D, None, ALU.mult))
            st_(v.tensor_scalar(lv, c.psS1[0:1, 512:1024], 1.0 / D, None, ALU.mult))
            st_(v.tensor_tensor(varr, m, m, ALU.mult))
            ins = v.tensor_tensor(varr, lv, varr, ALU.subtract)
            dv.inc(ins)

        @c.block.scalar
        def _(s, dvn=dv.n):
            s.wait_ge(dv.h, dvn)
            st_ = c.chain(s, "chs")
            varr, lv = small[0:1, 512:1024], small[0:1, 1024:1536]
            st_(s.activation(lv, varr, AF.Ln, bias=1e-5))
            ins = s.activation(varr, lv, AF.Exp, scale=-0.5)
            ac.inc(ins)

        @c.block.tensor
        def _(t, acn=ac.n):
            t.wait_ge(ac.h, acn)
            t.matmul(c.psX[:, 0:512], c.ones1f[:], small[0:1, 0:512],
                     start=True, stop=True)
            ins = t.matmul(c.psO[:, 0:512], c.ones1f[:], small[0:1, 512:1024],
                           start=True, stop=True)
            pe.inc(ins)

        @c.block.vector
        def _(v, pen=pe.n):
            v.wait_ge(pe.h, pen)
            st_ = c.chain(v)
            for fc in range(FC):
                hseg = hf[:, fc * 512:(fc + 1) * 512]
                st_(v.tensor_tensor(hseg, xf[:, fc * 512:(fc + 1) * 512],
                                    c.psX[:, 0:512], ALU.subtract))
                if fc < FC - 1:
                    st_(v.tensor_tensor(hseg, hseg, c.psO[:, 0:512], ALU.mult))
                else:
                    ins = v.tensor_tensor(hseg, hseg, c.psO[:, 0:512], ALU.mult)
            dv.inc(ins)

        @c.block.vector
        def _(v, g=g, dvn=dv.n):
            st_ = c.chain(v)
            for fc in range(FC):
                if h_dst is not None:
                    ins = v.tensor_scalar(
                        h_dst[:, fc * T + g * 512: fc * T + g * 512 + 512],
                        hf[:, fc * 512:(fc + 1) * 512],
                        modT[:, onec_col + fc:onec_col + fc + 1],
                        modT[:, s_col + fc:s_col + fc + 1],
                        ALU.mult, ALU.add)
                    if h_f32_dram3 is None and fc < FC - 1:
                        st_(ins)
                if h_f32_dram3 is not None:
                    hstage = c.arena_f32[:, 4096 + (fc % 2) * 512:
                                         4096 + (fc % 2) * 512 + 512]
                    tgt = st.n + 16 * (fc - 1)
                    if tgt > 0:
                        v.wait_ge(st.h, tgt)
                    st_(v.tensor_tensor(
                        hstage, hf[:, fc * 512:(fc + 1) * 512],
                        modT[:, onec_col + fc:onec_col + fc + 1]
                        .broadcast_to((128, 512)), ALU.mult))
                    ins = v.tensor_tensor(
                        hstage, hstage,
                        modT[:, s_col + fc:s_col + fc + 1]
                        .broadcast_to((128, 512)), ALU.add)
                    ins.then_inc(dv.h, 1)
                    if absmax_to is not None:
                        v.wait_ge(dv.h, dvn + fc + 1)
                        tm = c.scr_small[:, 16:17]
                        st_(v.tensor_reduce(tm, hstage, AX.X, ALU.max,
                                            apply_absolute_value=True))
                        st_(v.tensor_tensor(absmax_to, absmax_to, tm, ALU.max))
            if h_f32_dram3 is None:
                ac.inc(ins)

        if h_f32_dram3 is not None:
            dv.n += FC

            @c.block.gpsimd
            def _(gq, g=g, dv0=dv.n - FC, st0=st.n):
                for fc in range(FC):
                    hstage = c.arena_f32[:, 4096 + (fc % 2) * 512:
                                         4096 + (fc % 2) * 512 + 512]
                    gq.wait_ge(dv.h, dv0 + fc + 1)
                    c.qdma(gq, st, h_f32_dram3[:, fc, g * 512:(g + 1) * 512],
                           hstage)



        c.barrier()

    if h_f32_dram3 is not None:
        @c.block.gpsimd
        def _(gq):
            gq.wait_ge(st.h, st.n)
        c.barrier()

    if h_f32_dram3 is not None:
        @c.block.gpsimd
        def _(gq, stn=st.n):
            gq.wait_ge(st.h, stn)
        c.barrier()



def emit_l0_attn(c: Ctx, ctx_d, wq_d, wk_d, wv_d, wout_d, bout_d, x_in3, xs3,
                 modT):
    A = c.A
    ld = c.sem("a2ld")
    pe = c.sem("a2pe")
    ac = c.sem("a2ac")
    dv = c.sem("a2dv")
    st = c.sem("a2st")

    ctxb = c.arena_bf[:, A["CTX"]:A["CTX"] + CC * CTX_M]
    wqb = c.arena_bf[:, A["WQ"]:A["WQ"] + FC * 1024]
    wkb = c.arena_bf[:, A["WK"]:A["WK"] + CC * 1024]
    wvb = c.arena_bf[:, A["WV"]:A["WV"] + CC * 1024]
    h_bf = c.arena_bf[:, A["H"]:A["H"] + FC * T]
    ao = c.arena_bf[:, A["AO"]:A["AO"] + FC * T]
    qh = c.arena_bf[:, A["QH"]:A["QH"] + T]
    kh = c.arena_bf[:, A["KH"]:A["KH"] + CTX_M]
    vh = c.arena_bf[:, A["VH"]:A["VH"] + CTX_M]
    pt = c.arena_bf[:, A["P"]:A["P"] + 2048]
    stage0 = c.arena_f32[:, 0:512]
    stage1 = c.arena_f32[:, 512:1024]
    stage2 = c.arena_f32[:, 1024:1536]
    stage3 = c.arena_f32[:, 1536:2048]
    smalls = c.ln_small

    @c.block.gpsimd
    def _(g):
        c.qdma(g, ld, ctxb.rearrange("p (c m) -> p c m", c=CC),
               ctx_d.rearrange("(c p) m -> p c m", p=128))
        c.qdma(g, ld, wqb.rearrange("p (f o) -> p f o", f=FC),
               wq_d.rearrange("(f p) o -> p f o", p=128))
        c.qdma(g, ld, wkb.rearrange("p (f o) -> p f o", f=CC),
               wk_d.rearrange("(f p) o -> p f o", p=128))
        c.qdma(g, ld, wvb.rearrange("p (f o) -> p f o", f=CC),
               wv_d.rearrange("(f p) o -> p f o", p=128))

    @c.block.tensor
    def _(t, ldn=ld.n):
        t.wait_ge(ld.h, ldn)
    c.barrier()

    for head in range(L0_HEADS):
        pe0, ac0 = pe.n, ac.n
        slots = [c.psX, c.psO]

        @c.block.tensor
        def _(t, head=head, pe0=pe0, ac0=ac0):
            gi = 0
            for qtb in range(4):
                if gi >= 2:
                    t.wait_ge(ac.h, ac0 + gi - 1)
                for fc in range(FC):
                    ins = t.matmul(
                        slots[gi % 2][:, 0:512],
                        wqb[:, fc * 1024 + head * 128: fc * 1024 + head * 128 + 128],
                        h_bf[:, fc * T + qtb * 512: fc * T + qtb * 512 + 512],
                        start=(fc == 0), stop=(fc == FC - 1))
                ins.then_inc(pe.h, 1)
                gi += 1
            for kb in range(8):
                t.wait_ge(ac.h, ac0 + gi - 1)
                for cc in range(CC):
                    ins = t.matmul(
                        slots[gi % 2][:, 0:512],
                        wkb[:, cc * 1024 + head * 128: cc * 1024 + head * 128 + 128],
                        ctxb[:, cc * CTX_M + kb * 512: cc * CTX_M + kb * 512 + 512],
                        start=(cc == 0), stop=(cc == CC - 1))
                ins.then_inc(pe.h, 1)
                gi += 1
            for vg in range(8):
                t.wait_ge(ac.h, ac0 + gi - 1)
                for j in range(4):
                    ktc = vg * 4 + j
                    for cc in range(CC):
                        ins = t.matmul(
                            slots[gi % 2][:, j * 128:(j + 1) * 128],
                            ctxb[:, cc * CTX_M + ktc * 128: cc * CTX_M + ktc * 128 + 128],
                            wvb[:, cc * 1024 + head * 128: cc * 1024 + head * 128 + 128],
                            start=(cc == 0), stop=(cc == CC - 1))
                ins.then_inc(pe.h, 1)
                gi += 1
        pe.n = pe0 + 20

        @c.block.scalar
        def _(s, pe0=pe0):
            gi = 0
            for qtb in range(4):
                s.wait_ge(pe.h, pe0 + gi + 1)
                s.copy(qh[:, qtb * 512:(qtb + 1) * 512],
                       slots[gi % 2][:, 0:512]).then_inc(ac.h, 1)
                gi += 1
            for kb in range(8):
                s.wait_ge(pe.h, pe0 + gi + 1)
                s.copy(kh[:, kb * 512:(kb + 1) * 512],
                       slots[gi % 2][:, 0:512]).then_inc(ac.h, 1)
                gi += 1
            for vg in range(8):
                s.wait_ge(pe.h, pe0 + gi + 1)
                s.copy(vh[:, vg * 512:(vg + 1) * 512],
                       slots[gi % 2][:, 0:512]).then_inc(ac.h, 1)
                gi += 1
        ac.n = ac0 + 20
        c.barrier()

        for qtb in range(4):
            pe0, ac0, dv0 = pe.n, ac.n, dv.n

            @c.block.tensor
            def _(t, head=head, qtb=qtb, pe0=pe0, ac0=ac0, dv0=dv0):
                def scores(i):
                    slot = c.psS1 if i % 2 == 0 else c.psS2
                    if i >= 2:
                        t.wait_ge(ac.h, ac0 + i - 1)
                    for j in range(2):
                        ktc = i * 2 + j
                        ins = t.matmul(slot[:, j * 512:(j + 1) * 512],
                                       kh[:, ktc * 128:(ktc + 1) * 128],
                                       qh[:, qtb * 512:(qtb + 1) * 512],
                                       start=True, stop=True)
                    ins.then_inc(pe.h, 1)

                def pv(i):
                    t.wait_ge(ac.h, ac0 + i + 1)
                    pslot = pt[:, (i % 2) * 1024:(i % 2) * 1024 + 1024]
                    for j in range(2):
                        ktc = i * 2 + j
                        t.matmul(c.psO[:, 0:512],
                                 vh[:, ktc * 128:(ktc + 1) * 128],
                                 pslot[:, j * 512:(j + 1) * 512],
                                 start=(ktc == 0), stop=(ktc == 31))
                        ins = t.matmul(c.psSum[0:1, 0:512], c.onesb[:],
                                       pslot[:, j * 512:(j + 1) * 512],
                                       start=(ktc == 0), stop=(ktc == 31))
                    ins.then_inc(pe.h, 1)

                scores(0)
                scores(1)
                for i in range(14):
                    pv(i)
                    scores(i + 2)
                pv(14)
                pv(15)
                t.wait_ge(dv.h, dv0 + 1)
                t.matmul(c.psX[:, 0:512], c.ones1f[:],
                         smalls[0:1, 512:1024], start=True, stop=True
                         ).then_inc(pe.h, 1)

            pe.n = pe0 + 33

            @c.block.scalar
            def _(s, head=head, qtb=qtb, pe0=pe0, ac0=ac0):
                for i in range(16):
                    s.wait_ge(pe.h, pe0 + (i + 1 if i < 2 else 2 * i))
                    slot = c.psS1 if i % 2 == 0 else c.psS2
                    s.activation(pt[:, (i % 2) * 1024:(i % 2) * 1024 + 1024],
                                 slot[:, 0:1024], AF.Exp,
                                 scale=L0_QK_SCALE, bias=-EXPS
                                 ).then_inc(ac.h, 1)
                s.wait_ge(pe.h, pe0 + 32)
                s.copy(smalls[0:1, 0:512], c.psSum[0:1, 0:512]).then_inc(ac.h, 1)
                s.copy(stage0[:], c.psO[:, 0:512]).then_inc(ac.h, 1)
            ac.n = ac0 + 18

            @c.block.vector
            def _(v, head=head, qtb=qtb, ac0=ac0, pe0=pe0):
                v.wait_ge(ac.h, ac0 + 17)
                v.reciprocal(smalls[0:1, 512:1024], smalls[0:1, 0:512]
                             ).then_inc(dv.h, 1)
                v.wait_ge(ac.h, ac0 + 18)
                v.wait_ge(pe.h, pe0 + 33)
                v.tensor_tensor(
                    ao[:, head * T + qtb * 512: head * T + qtb * 512 + 512],
                    stage0[:], c.psX[:, 0:512], ALU.mult).then_inc(dv.h, 1)
            dv.n = dv0 + 2
            c.barrier()

    woutb = wqb
    bout_t = c.sb("bout_t", [128, 8], F32)
    gab = c.scr_small[:, 24:32]

    @c.block.gpsimd
    def _(g):
        c.qdma(g, ld, woutb.rearrange("p (f o) -> p f o", f=FC),
               wout_d.rearrange("(f p) o -> p f o", p=128))
        c.qdma(g, ld, bout_t[:], bout_d)

    @c.block.vector
    def _(v, ldn=ld.n):
        v.wait_ge(ld.h, ldn)
        v.tensor_tensor(gab, modT[:, 16:24], bout_t[:], ALU.mult).then_inc(dv.h, 1)
    dv.n += 1
    c.barrier()

    pe0, ac0, dv0, st0 = pe.n, ac.n, dv.n, st.n
    lds = c.sem("a2lds")
    ntile = 32

    def tile_oc_qtb(i):
        return i % 8, i // 8

    @c.block.sync
    def _(sync, dv0=dv0):
        for i in range(ntile):
            oc, qtb = tile_oc_qtb(i)
            if i >= 2:
                sync.wait_ge(dv.h, dv0 + i - 1)
            c.qdma(sync, lds, (stage1 if i % 2 == 0 else stage2),
                   x_in3[:, oc, qtb * 512:(qtb + 1) * 512])

    @c.block.tensor
    def _(t, pe0=pe0, dv0=dv0):
        for i in range(ntile):
            oc, qtb = tile_oc_qtb(i)
            slot = c.psS1 if i % 2 == 0 else c.psS2
            if i >= 2:
                t.wait_ge(dv.h, dv0 + i - 1)
            for fc in range(FC):
                ins = t.matmul(
                    slot[:, 0:512],
                    woutb[:, fc * 1024 + oc * 128: fc * 1024 + oc * 128 + 128],
                    ao[:, fc * T + qtb * 512: fc * T + qtb * 512 + 512],
                    start=(fc == 0), stop=(fc == FC - 1))
            ins.then_inc(pe.h, 1)
    pe.n = pe0 + ntile

    @c.block.vector
    def _(v, pe0=pe0, dv0=dv0, st0=st0):
        st_ = c.chain(v)
        for i in range(ntile):
            oc, qtb = tile_oc_qtb(i)
            slot = c.psS1 if i % 2 == 0 else c.psS2
            v.wait_ge(pe.h, pe0 + i + 1)
            v.wait_ge(lds.h, 16 * (i + 1))
            if i >= 1:
                v.wait_ge(st.h, st0 + 16 * i)
            st_(v.tensor_scalar(stage0[:], slot[:, 0:512],
                                modT[:, 16 + oc:16 + oc + 1],
                                gab[:, oc:oc + 1],
                                ALU.mult, ALU.add))
            v.tensor_tensor(stage3[:],
                            (stage1 if i % 2 == 0 else stage2),
                            stage0[:], ALU.add).then_inc(dv.h, 1)
    dv.n = dv0 + ntile

    @c.block.gpsimd
    def _(g, dv0=dv0, st0=st0):
        for i in range(ntile):
            oc, qtb = tile_oc_qtb(i)
            g.wait_ge(dv.h, dv0 + i + 1)
            c.qdma(g, st, xs3[:, oc, qtb * 512:(qtb + 1) * 512], stage3[:])
    c.barrier()



def _fp4_half_ops(eng, src, dst, utile, qtile, tmp, rs, hs, o, w):
    b0, wb = o // 16, w // 16
    s_ = lambda a: a[:, o:o + w]
    s3 = lambda a: a[:, o:o + w].rearrange("p (n b) -> p n b", b=16)
    r3 = lambda a: (a[:, b0:b0 + wb].rearrange("p (n o) -> p n o", o=1)
                    .broadcast_to((128, wb, 16)))
    ops = [lambda: eng.scalar_tensor_tensor(s_(utile), s_(src), -1.0,
                                            s_(src), ALU.mult, ALU.max),
           lambda: eng.tensor_tensor(s3(utile), s3(utile), r3(rs), ALU.mult),
           lambda: eng.tensor_scalar(s_(qtile), s_(utile), FP4_THR[0],
                                     None, ALU.is_gt)]
    for thr in FP4_THR[1:4]:
        ops.append(lambda thr=thr: eng.tensor_scalar(
            s_(tmp), s_(utile), thr, None, ALU.is_gt))
        ops.append(lambda: eng.tensor_tensor(s_(qtile), s_(qtile), s_(tmp),
                                             ALU.add))
    for thr, wt in ((FP4_THR[4], 2.0), (FP4_THR[5], 2.0), (FP4_THR[6], 4.0)):
        ops.append(lambda thr=thr: eng.tensor_scalar(
            s_(tmp), s_(utile), thr, None, ALU.is_gt))
        ops.append(lambda wt=wt: eng.scalar_tensor_tensor(
            s_(qtile), s_(tmp), wt, s_(qtile), ALU.mult, ALU.add))
    ops.append(lambda: eng.tensor_scalar(s_(tmp), s_(src), 0.0, None,
                                         ALU.is_ge))
    ops.append(lambda: eng.tensor_scalar(s_(tmp), s_(tmp), 2.0, -1.0,
                                         ALU.mult, ALU.add))
    ops.append(lambda: eng.tensor_tensor(s_(qtile), s_(qtile), s_(tmp),
                                         ALU.mult))
    ops.append(lambda: eng.tensor_tensor(s3(dst), s3(qtile), r3(hs),
                                         ALU.mult))
    return ops


def _interleave_chains(c, eng, ops_lists, sem_names):
    sems = [c.sem(nm) for nm in sem_names]
    idx = [0] * len(ops_lists)
    pend = [None] * len(ops_lists)
    finals = [None] * len(ops_lists)
    remaining = sum(len(o) for o in ops_lists)
    while remaining:
        for i, ops in enumerate(ops_lists):
            if idx[i] >= len(ops):
                continue
            if pend[i] is not None:
                eng.wait_ge(sems[i].h, pend[i])
            ins = ops[idx[i]]()
            idx[i] += 1
            remaining -= 1
            if idx[i] == len(ops):
                finals[i] = ins
            else:
                sems[i].inc(ins)
                pend[i] = sems[i].n
    return finals


def emit_fp4_quant_dual(c, src, dst, W, utile, qtile, tmp, qsm,
                        vwaits=(), gwaits=(), dv=None, gq=None):
    nb = W // 16
    hw = (W // 32) * 16
    bm = qsm[:, 0:nb]
    sc = qsm[:, nb:2 * nb]
    rs = qsm[:, 2 * nb:3 * nb]
    hs = qsm[:, 3 * nb:4 * nb]
    smalls_at = dv.n + 1
    gq_prev = gq.n

    ilB = c.sem("ilB")

    @c.block.vector
    def _(v, vwaits=tuple(vwaits), ilB_prev=ilB.n):
        st_ = c.chain(v)
        for h, n in vwaits:
            v.wait_ge(h, n)
        if ilB_prev:
            v.wait_ge(ilB.h, ilB_prev)
        st_(v.tensor_reduce(bm, src.rearrange("p (n b) -> p n b", b=16),
                            AX.X, ALU.max, apply_absolute_value=True))
        st_(v.tensor_scalar(sc, bm, 1.0 / 6.0, 1e-12, ALU.mult, ALU.max))
        st_(v.tensor_scalar(hs, sc, 0.5, None, ALU.mult))
        st_(v.reciprocal(rs, sc))
        v.memset(c.scr_small[0:1, 23:24], 0.0).then_inc(dv.h, 1)
        opsA = _fp4_half_ops(v, src, dst, utile, qtile, tmp, rs, hs, 0, hw)
        opsB = _fp4_half_ops(v, src, dst, utile, qtile, tmp, rs, hs,
                             hw, W - hw)
        finA, finB = _interleave_chains(c, v, [opsA, opsB], ["ilA", "ilB"])
        finA.then_inc(dv.h, 1)
        ilB.inc(finB)
    dv.n += 2

    @c.block.gpsimd
    def _(g, dv_done=dv.n, ilB_done=ilB.n):
        g.wait_ge(dv.h, dv_done)
        g.wait_ge(ilB.h, ilB_done)
        g.memset(c.scr_small[0:1, 22:23], 0.0).then_inc(gq.h, 1)
    gq.n += 1



A3 = dict(WG=0, WU=22528, WD=45056, H2Q=67584, HQT=78848, H2QTM=79872,
          QS1=82688, QS2=85504, QS3=88320)


def emit_l0_ffn(c: Ctx, wg_d, wu_d, wd_d, xs3, h0scr3, modT):
    ld = c.sem("f0ld")
    pe = c.sem("f0pe")
    ac = c.sem("f0ac")
    dv = c.sem("f0dv")
    st = c.sem("f0st")

    wgf = c.arena_bf[:, A3["WG"]:A3["WG"] + FC * I_FFN]
    wuf = c.arena_bf[:, A3["WU"]:A3["WU"] + FC * I_FFN]
    wdf = c.arena_bf[:, A3["WD"]:A3["WD"] + IC * 1024]
    h2q = c.arena_bf[:, A3["H2Q"]:A3["H2Q"] + IC * 512]
    hqt = c.arena_bf[:, A3["HQT"]:A3["HQT"] + 1024]
    h2qtm = c.arena_bf[:, A3["H2QTM"]:A3["H2QTM"] + I_FFN]
    qs1 = c.arena_bf[:, A3["QS1"]:A3["QS1"] + I_FFN]
    qs2 = c.arena_bf[:, A3["QS2"]:A3["QS2"] + I_FFN]
    qs3 = c.arena_bf[:, A3["QS3"]:A3["QS3"] + 1792]
    hx = c.arena_f32[:, 0:1024]
    htm = c.arena_f32[:, 1024:2048]
    sg = c.arena_f32[:, 2048:2560]
    h2f = c.arena_f32[:, 2560:5376]
    qsm = c.arena_f32[:, 6416:6416 + 704]
    xw = c.arena_f32[:, 0:512]
    gq = c.sem("f0gq")

    tps = c.sem("f0tps")
    tpe = c.sem("f0tpe")
    wsrcs = (c.arena_f32[:, 0:2816], c.arena_f32[:, 2816:5632])
    chl = []
    for wi, (wd_src, wdst, nchunk, wwidth) in enumerate(
            ((wg_d, wgf, IC, 1024), (wu_d, wuf, IC, 1024), (wd_d, wdf, FC, I_FFN))):
        w3 = wd_src.rearrange("(n p) k -> p n k", p=128)
        for ch in range(nchunk):
            chl.append((w3, ch, wdst, wwidth, wi))
    prevw = {}
    for k, (w3, ch, wdst, wwidth, wi) in enumerate(chl):
        wsrc = wsrcs[k % 2]
        @c.block.sync
        def _(sync, w3=w3, ch=ch, wwidth=wwidth, wsrc=wsrc,
              pw=prevw.get(k % 2)):
            if pw is not None:
                sync.wait_ge(dv.h, pw[0])
                sync.wait_ge(gq.h, pw[1])
            c.qdma(sync, ld, wsrc[:, 0:wwidth], w3[:, ch, :])

        emit_fp4_quant_dual(c, wsrc[:, 0:wwidth], qs2[:, 0:wwidth], wwidth,
                            qs1[:, 0:wwidth], h2qtm[:, 0:wwidth],
                            h2q[:, 0:wwidth], qsm,
                            vwaits=[(ld.h, ld.n), (tpe.h, tpe.n)],
                            gwaits=[(ld.h, ld.n), (tpe.h, tpe.n)],
                            dv=dv, gq=gq)
        prevw[k % 2] = (dv.n, gq.n)

        ntr = wwidth // 128
        ngrp = (ntr + 3) // 4
        @c.block.tensor
        def _(t, ntr=ntr, ngrp=ngrp, dvn=dv.n, gqn=gq.n, tps0=tps.n,
              tpe0=tpe.n):
            t.wait_ge(dv.h, dvn)
            t.wait_ge(gq.h, gqn)
            for g4 in range(ngrp):
                if tps0 + g4 >= 1:
                    t.wait_ge(tpe.h, tps0 + g4)
                for j in range(min(4, ntr - g4 * 4)):
                    trc = g4 * 4 + j
                    ins = t.transpose(
                        c.psT[:, j * 128:(j + 1) * 128],
                        qs2[:, trc * 128:(trc + 1) * 128], c.identb[:])
                ins.then_inc(tps.h, 1)

        @c.block.scalar
        def _(s, ch=ch, wdst=wdst, ntr=ntr, ngrp=ngrp, wi=wi, tps0=tps.n):
            for g4 in range(ngrp):
                s.wait_ge(tps.h, tps0 + g4 + 1)
                nj = min(4, ntr - g4 * 4)
                for j in range(nj):
                    trc = g4 * 4 + j
                    stride = I_FFN if wi < 2 else 1024
                    dst = wdst[:, trc * stride + ch * 128:
                               trc * stride + ch * 128 + 128]
                    ins = s.copy(dst, c.psT[:, j * 128:(j + 1) * 128])
                ins.then_inc(tpe.h, 1)
        tps.n += ngrp
        tpe.n += ngrp
    c.barrier()

    IBW = [512, 512, 512, 512, 512, 256]
    for tb in range(4):
        for tci in range(4):
            tc = tb * 4 + tci
            @c.block.sync
            def _(sync, tc=tc, dvn=dv.n):
                sync.wait_ge(dv.h, dvn)
                c.qdma(sync, ld, hx.rearrange("p (f t) -> p f t", f=8),
                       h0scr3[:, :, tc * 128:(tc + 1) * 128])

            @c.block.tensor
            def _(t, ldn=ld.n, acn=ac.n):
                t.wait_ge(ld.h, ldn)
                t.wait_ge(ac.h, acn)
                for j in range(8):
                    slot = c.psX if j < 4 else c.psO
                    ins = t.transpose(slot[:, (j % 4) * 128:(j % 4) * 128 + 128],
                                      hx[:, j * 128:(j + 1) * 128], c.identf[:])
                ins.then_inc(pe.h, 1)
            pe.n += 1

            @c.block.scalar
            def _(s, pen=pe.n):
                s.wait_ge(pe.h, pen)
                st2 = c.chain(s, "chs")
                st2(s.copy(htm[:, 0:512], c.psX[:, 0:512]))
                s.copy(htm[:, 512:1024], c.psO[:, 0:512]).then_inc(ac.h, 1)
            ac.n += 1
            c.barrier()

            emit_fp4_quant_dual(c, htm[:, 0:1024], qs2[:, 0:1024], 1024,
                                qs1[:, 0:1024], h2qtm[:, 0:1024],
                                qs3[:, 0:1024], qsm,
                                vwaits=[(ac.h, ac.n)], gwaits=[(ac.h, ac.n)],
                                dv=dv, gq=gq)
            c.barrier()

            @c.block.tensor
            def _(t, dvn=dv.n, gqn=gq.n, tpe0=tpe.n):
                t.wait_ge(dv.h, dvn)
                t.wait_ge(gq.h, gqn)
                for g4 in range(2):
                    t.wait_ge(tpe.h, tpe0 + g4)
                    for j in range(4):
                        trc = g4 * 4 + j
                        ins = t.transpose(c.psT[:, j * 128:(j + 1) * 128],
                                          qs2[:, trc * 128:(trc + 1) * 128],
                                          c.identb[:])
                    ins.then_inc(tps.h, 1)
            tps.n += 2

            @c.block.scalar
            def _(s, tps0=tps.n - 2):
                for g4 in range(2):
                    s.wait_ge(tps.h, tps0 + g4 + 1)
                    for j in range(4):
                        trc = g4 * 4 + j
                        ins = s.copy(hqt[:, trc * 128:(trc + 1) * 128],
                                     c.psT[:, j * 128:(j + 1) * 128])
                    ins.then_inc(tpe.h, 1)
            tpe.n += 2
            c.barrier()

            off = 0
            for ib, ibw in enumerate(IBW):
                @c.block.tensor
                def _(t, ib=ib, ibw=ibw, off=off, tpen=tpe.n, dvn=dv.n):
                    t.wait_ge(tpe.h, tpen)
                    if ib >= 1:
                        t.wait_ge(dv.h, dvn)
                    for fc in range(FC):
                        t.matmul(c.psS2[:, 0:ibw],
                                 hqt[:, fc * 128:(fc + 1) * 128],
                                 wgf[:, fc * I_FFN + off: fc * I_FFN + off + ibw],
                                 start=(fc == 0), stop=(fc == FC - 1))
                    for fc in range(FC):
                        ins = t.matmul(c.psS2[:, 512:512 + ibw],
                                       hqt[:, fc * 128:(fc + 1) * 128],
                                       wuf[:, fc * I_FFN + off: fc * I_FFN + off + ibw],
                                       start=(fc == 0), stop=(fc == FC - 1))
                    ins.then_inc(pe.h, 1)
                pe.n += 1

                @c.block.scalar
                def _(s, ibw=ibw, pen=pe.n):
                    s.wait_ge(pe.h, pen)
                    s.activation(sg[:, 0:ibw], c.psS2[:, 0:ibw], AF.Silu
                                 ).then_inc(ac.h, 1)
                ac.n += 1

                @c.block.vector
                def _(v, ibw=ibw, off=off, acn=ac.n):
                    v.wait_ge(ac.h, acn)
                    v.tensor_tensor(h2f[:, off:off + ibw], sg[:, 0:ibw],
                                    c.psS2[:, 512:512 + ibw], ALU.mult
                                    ).then_inc(dv.h, 1)
                dv.n += 1
                off += ibw
                c.barrier()

            emit_fp4_quant_dual(c, h2f[:, 0:I_FFN], h2qtm[:, 0:I_FFN], I_FFN,
                                qs1[:, 0:I_FFN], qs2[:, 0:I_FFN],
                                h2qtm[:, 0:I_FFN], qsm,
                                vwaits=[(dv.h, dv.n)], gwaits=[(dv.h, dv.n)],
                                dv=dv, gq=gq)
            c.barrier()

            @c.block.tensor
            def _(t, dvn=dv.n, gqn=gq.n, tpe0=tpe.n):
                t.wait_ge(dv.h, dvn)
                t.wait_ge(gq.h, gqn)
                for g4 in range(6):
                    t.wait_ge(tpe.h, tpe0 + g4)
                    for j in range(min(4, 22 - g4 * 4)):
                        trc = g4 * 4 + j
                        ins = t.transpose(c.psT[:, j * 128:(j + 1) * 128],
                                          h2qtm[:, trc * 128:(trc + 1) * 128],
                                          c.identb[:])
                    ins.then_inc(tps.h, 1)
            tps.n += 6

            @c.block.scalar
            def _(s, tci=tci, tps0=tps.n - 6):
                for g4 in range(6):
                    s.wait_ge(tps.h, tps0 + g4 + 1)
                    for j in range(min(4, 22 - g4 * 4)):
                        trc = g4 * 4 + j
                        ins = s.copy(h2q[:, trc * 512 + tci * 128:
                                        trc * 512 + tci * 128 + 128],
                                     c.psT[:, j * 128:(j + 1) * 128])
                    ins.then_inc(tpe.h, 1)
            tpe.n += 6
            c.barrier()

        for oc in range(FC):
            @c.block.tensor
            def _(t, oc=oc, tpen=tpe.n, dvn=dv.n):
                t.wait_ge(tpe.h, tpen)
                if oc >= 2:
                    t.wait_ge(dv.h, dvn - 1)
                slot = c.psS1[:, 0:512] if oc % 2 == 0 else c.psS1[:, 512:1024]
                for ic in range(IC):
                    ins = t.matmul(slot,
                                   wdf[:, ic * 1024 + oc * 128:
                                       ic * 1024 + oc * 128 + 128],
                                   h2q[:, ic * 512:(ic + 1) * 512],
                                   start=(ic == 0), stop=(ic == IC - 1))
                ins.then_inc(pe.h, 1)
            pe.n += 1

            @c.block.sync
            def _(sync, tb=tb, oc=oc, dvn=dv.n):
                if oc >= 1:
                    sync.wait_ge(dv.h, dvn)
                c.qdma(sync, ld, xw[:, 0:512],
                       xs3[:, oc, tb * 512:(tb + 1) * 512])

            @c.block.vector
            def _(v, oc=oc, tb=tb, pen=pe.n, ldn=ld.n, stn=st.n):
                st_ = c.chain(v)
                v.wait_ge(pe.h, pen)
                v.wait_ge(ld.h, ldn)
                v.wait_ge(st.h, stn)
                slot = c.psS1[:, 0:512] if oc % 2 == 0 else c.psS1[:, 512:1024]
                esg = c.arena_f32[:, 5904:6416]
                st_(v.tensor_scalar(esg, slot,
                                    modT[:, 40 + oc:40 + oc + 1], None, ALU.mult))
                v.tensor_tensor(h2f[:, 0:512], xw[:, 0:512], esg,
                                ALU.add).then_inc(dv.h, 1)
            dv.n += 1

            @c.block.gpsimd
            def _(g, oc=oc, tb=tb, dvn=dv.n):
                g.wait_ge(dv.h, dvn)
                c.qdma(g, st, xs3[:, oc, tb * 512:(tb + 1) * 512],
                       h2f[:, 0:512])
            c.barrier()
    c.barrier()



A5 = dict(WQ=0, WK=8192, WV=16384, WO=24576, MASK=32768, QH=77824, KH=79872,
          VP=81920, VP2=82960, PM=84000)


def emit_l1_attn(c: Ctx, wq_d, wk_d, wv_d, wo_d, bq_d, bv_d, bo_d, mask_d,
                 n_par, xs3, modT):
    A = c.A
    ld = c.sem("a5ld")
    pe = c.sem("a5pe")
    ac = c.sem("a5ac")
    dv = c.sem("a5dv")
    st = c.sem("a5st")

    wqb = c.arena_bf[:, A5["WQ"]:A5["WQ"] + FC * 1024]
    wkb = c.arena_bf[:, A5["WK"]:A5["WK"] + FC * 1024]
    wvb = c.arena_bf[:, A5["WV"]:A5["WV"] + FC * 1024]
    wob = c.arena_bf[:, A5["WO"]:A5["WO"] + FC * 1024]
    maskb = c.arena_bf[:, A5["MASK"]:A5["MASK"] + n_par * 512]
    h_bf = c.arena_bf[:, A["H"]:A["H"] + FC * T]
    ao = c.arena_bf[:, A["AO"]:A["AO"] + FC * T]
    qh2 = c.arena_bf[:, A5["QH"]:A5["QH"] + T]
    kh2 = c.arena_bf[:, A5["KH"]:A5["KH"] + T]
    vps = [c.arena_bf[:, A5["VP"]:A5["VP"] + 16 * 65],
           c.arena_bf[:, A5["VP2"]:A5["VP2"] + 16 * 65]]
    ptm = c.arena_bf[:, A5["PM"]:A5["PM"] + 512]
    pt = c.arena_bf[:, A["P"]:A["P"] + 2048]
    stage0 = c.arena_f32[:, 0:512]
    stage1 = c.arena_f32[:, 512:1024]
    stage2 = c.arena_f32[:, 1024:1536]
    stage3 = c.arena_f32[:, 1536:2048]
    smalls = c.ln_small

    bq_t = c.sb("bq_t", [128, 8], F32)
    bv_t = c.sb("bv_t", [128, 8], F32)
    bo_t = c.sb("bo_t", [128, 8], F32)
    bqs = c.sb("bqs", [128, 8], F32)
    gab1 = c.scr_small[:, 40:48]

    classes, partials = l1_tile_classes()
    pidx = {tt_: i for i, tt_ in enumerate(partials)}

    @c.block.gpsimd
    def _(g):
        for wd, dst in ((wq_d, wqb), (wk_d, wkb), (wv_d, wvb), (wo_d, wob)):
            c.qdma(g, ld, dst.rearrange("p (f o) -> p f o", f=FC),
                   wd.rearrange("(f p) o -> p f o", p=128))
        c.qdma(g, ld, maskb.rearrange("p (n q) -> p n q", n=n_par),
               mask_d.rearrange("(n p) q -> p n q", p=128))

    lds = c.sem("a5lds")

    @c.block.sync
    def _(sync):
        c.qdma(sync, lds, bq_t[:], bq_d)
        c.qdma(sync, lds, bv_t[:], bv_d)
        c.qdma(sync, lds, bo_t[:], bo_d)

    @c.block.vector
    def _(v, ldn=ld.n, ldsn=lds.n):
        v.wait_ge(ld.h, ldn)
        v.wait_ge(lds.h, ldsn)
        st_ = c.chain(v)
        st_(v.tensor_scalar(bqs[:], bq_t[:], L1_QK_SCALE, None, ALU.mult))
        st_(v.tensor_tensor(gab1, modT[:, 16:24], bo_t[:], ALU.mult))
        st_(v.memset(vps[0].rearrange("p (n w) -> p n w", w=65)[:, :, 64:65], 1.0))
        ins = v.memset(vps[1].rearrange("p (n w) -> p n w", w=65)[:, :, 64:65], 1.0)
        dv.inc(ins)
    c.barrier()

    for hp in range(L1_HEADS // 2):
        pe0, ac0, dv0 = pe.n, ac.n, dv.n

        @c.block.tensor
        def _(t, hp=hp, pe0=pe0, dv0=dv0):
            for qtb in range(4):
                if qtb >= 1:
                    t.wait_ge(dv.h, dv0 + 2 * qtb - 1)
                for fc in range(FC):
                    ins = t.matmul(
                        c.psX[:, 0:512],
                        wqb[:, fc * 1024 + hp * 128: fc * 1024 + hp * 128 + 128],
                        h_bf[:, fc * T + qtb * 512: fc * T + qtb * 512 + 512],
                        start=(fc == 0), stop=(fc == FC - 1))
                ins.then_inc(pe.h, 1)
                if qtb >= 1:
                    t.wait_ge(dv.h, dv0 + 2 * qtb)
                for fc in range(FC):
                    ins = t.matmul(
                        c.psO[:, 0:512],
                        wkb[:, fc * 1024 + hp * 128: fc * 1024 + hp * 128 + 128],
                        h_bf[:, fc * T + qtb * 512: fc * T + qtb * 512 + 512],
                        start=(fc == 0), stop=(fc == FC - 1))
                ins.then_inc(pe.h, 1)
            for tg in range(4):
                if tg >= 1:
                    t.wait_ge(dv.h, dv0 + 8 + tg)
                for j in range(4):
                    tc = tg * 4 + j
                    for fc in range(FC):
                        ins = t.matmul(
                            c.psS1[:, j * 256:j * 256 + 128],
                            h_bf[:, fc * T + tc * 128: fc * T + tc * 128 + 128],
                            wvb[:, fc * 1024 + hp * 128: fc * 1024 + hp * 128 + 128],
                            start=(fc == 0), stop=(fc == FC - 1))
                ins.then_inc(pe.h, 1)

        @c.block.vector
        def _(v, hp=hp, pe0=pe0):
            st_ = c.chain(v)
            for qtb in range(4):
                v.wait_ge(pe.h, pe0 + 2 * qtb + 1)
                v.tensor_scalar(qh2[:, qtb * 512:(qtb + 1) * 512],
                                c.psX[:, 0:512], L1_QK_SCALE,
                                bqs[:, hp:hp + 1],
                                ALU.mult, ALU.add).then_inc(dv.h, 1)
                v.wait_ge(pe.h, pe0 + 2 * qtb + 2)
                v.tensor_copy(kh2[:, qtb * 512:(qtb + 1) * 512],
                              c.psO[:, 0:512]).then_inc(dv.h, 1)
            for tg in range(4):
                v.wait_ge(pe.h, pe0 + 8 + tg + 1)
                for j in range(4):
                    tc = tg * 4 + j
                    st_(v.tensor_copy(vps[0][:, tc * 65:tc * 65 + 64],
                                      c.psS1[:, j * 256:j * 256 + 64]))
                    ins = v.tensor_copy(vps[1][:, tc * 65:tc * 65 + 64],
                                        c.psS1[:, j * 256 + 64:j * 256 + 128])
                ins.then_inc(dv.h, 1)
        pe.n += 12
        dv.n += 12
        c.barrier()

        sc = c.sem("a5sc")
        pvs = c.sem("a5pv")
        bc = c.sem("a5bc")
        for sub in range(2):
          head = 2 * hp + sub
          hb = sub * 64
          vpp = vps[sub]
          for qtb in range(4):
            allowed = [k for k in range(16) if classes[(qtb, k)] != "zero"]
            groups = [allowed[i:i + 2] for i in range(0, len(allowed), 2)]
            ng = len(groups)
            ac0, dv0 = ac.n, dv.n
            sc0, pv0, bc0 = sc.n, pvs.n, bc.n

            @c.block.tensor
            def _(t, head=head, hb=hb, vpp=vpp, qtb=qtb, ac0=ac0, dv0=dv0,
                  sc0=sc0, pv0=pv0, groups=groups, ng=ng):
                def scores(i):
                    slot = c.psS1 if i % 2 == 0 else c.psS2
                    if i >= 2:
                        t.wait_ge(ac.h, ac0 + i - 1)
                    for j, ktc in enumerate(groups[i]):
                        ins = t.matmul(slot[:, j * 512:(j + 1) * 512],
                                       kh2[hb:hb + 64,
                                           ktc * 128:(ktc + 1) * 128],
                                       qh2[hb:hb + 64,
                                           qtb * 512:(qtb + 1) * 512],
                                       start=True, stop=True)
                    ins.then_inc(sc.h, 1)

                def pv(i):
                    t.wait_ge(dv.h, dv0 + i + 1)
                    pslot = pt[:, (i % 2) * 1024:(i % 2) * 1024 + 1024]
                    for j, ktc in enumerate(groups[i]):
                        first = (i == 0 and j == 0)
                        last = (i == ng - 1 and j == len(groups[i]) - 1)
                        ins = t.matmul(c.psO[0:65, 0:512],
                                       vpp[:, ktc * 65:(ktc + 1) * 65],
                                       pslot[:, j * 512:(j + 1) * 512],
                                       start=first, stop=last)
                    ins.then_inc(pvs.h, 1)

                scores(0)
                if ng > 1:
                    scores(1)
                for i in range(ng - 2):
                    pv(i)
                    scores(i + 2)
                if ng > 1:
                    pv(ng - 2)
                pv(ng - 1)
                t.wait_ge(dv.h, dv0 + ng + 1)
                t.matmul(c.psX[:, 0:512], c.ones1f[:],
                         smalls[0:1, 512:1024], start=True, stop=True
                         ).then_inc(bc.h, 1)
            sc.n = sc0 + ng
            pvs.n = pv0 + ng
            bc.n = bc0 + 1

            @c.block.scalar
            def _(s, qtb=qtb, ac0=ac0, sc0=sc0, pv0=pv0, groups=groups, ng=ng):
                for i in range(ng):
                    s.wait_ge(sc.h, sc0 + i + 1)
                    if i >= 2:
                        s.wait_ge(pvs.h, pv0 + i - 1)
                    slot = c.psS1 if i % 2 == 0 else c.psS2
                    w = 512 * len(groups[i])
                    s.activation(pt[:, (i % 2) * 1024:(i % 2) * 1024 + w],
                                 slot[:, 0:w], AF.Exp, bias=-EXPS
                                 ).then_inc(ac.h, 1)
                s.wait_ge(pvs.h, pv0 + ng)
                s.copy(stage0[0:65, :], c.psO[0:65, 0:512]).then_inc(ac.h, 1)
            ac.n = ac0 + ng + 1

            @c.block.vector
            def _(v, head=head, hb=hb, qtb=qtb, ac0=ac0, dv0=dv0, bc0=bc0,
                  groups=groups, ng=ng):
                st_ = c.chain(v)
                for i in range(ng):
                    v.wait_ge(ac.h, ac0 + i + 1)
                    for j, ktc in enumerate(groups[i]):
                        if classes[(qtb, ktc)] == "partial":
                            pi = pidx[(qtb, ktc)]
                            seg = pt[:, (i % 2) * 1024 + j * 512:
                                     (i % 2) * 1024 + j * 512 + 512]
                            st_(v.tensor_tensor(
                                ptm[:, 0:512], seg,
                                maskb[:, pi * 512:(pi + 1) * 512], ALU.mult))
                            st_(v.tensor_copy(seg, ptm[:, 0:512]))
                    ins = v.memset(c.barcell[:, 3:4], 0.0)
                    dv.inc(ins)
                    v.wait_ge(dv.h, dv.n)
                v.wait_ge(ac.h, ac0 + ng + 1)
                ins = v.reciprocal(smalls[0:1, 512:1024], stage0[64:65, 0:512])
                dv.inc(ins)
                v.wait_ge(bc.h, bc0 + 1)
                st_(v.tensor_tensor(stage1[0:64, :], stage0[0:64, :],
                                    c.psX[0:64, 0:512], ALU.mult))
                ins = v.tensor_scalar(
                    ao[hb:hb + 64,
                       (head // 2) * T + qtb * 512:
                       (head // 2) * T + qtb * 512 + 512],
                    stage1[0:64, :], 1.0,
                    bv_t[hb:hb + 64, head // 2:head // 2 + 1],
                    ALU.mult, ALU.add)
                dv.inc(ins)
            c.barrier()

    pe0, ac0, dv0, st0, ld0 = pe.n, ac.n, dv.n, st.n, ld.n
    lds0 = lds.n
    ntile = 32

    def tile_oc_qtb(i):
        return i % 8, i // 8

    @c.block.sync
    def _(sync, dv0=dv0, lds0=lds0):
        for i in range(ntile):
            oc, qtb = tile_oc_qtb(i)
            if i >= 2:
                sync.wait_ge(dv.h, dv0 + i - 1)
            c.qdma(sync, lds, (stage1 if i % 2 == 0 else stage2),
                   xs3[:, oc, qtb * 512:(qtb + 1) * 512])

    @c.block.tensor
    def _(t, pe0=pe0, dv0=dv0):
        for i in range(ntile):
            oc, qtb = tile_oc_qtb(i)
            slot = c.psS1 if i % 2 == 0 else c.psS2
            if i >= 2:
                t.wait_ge(dv.h, dv0 + i - 1)
            for fc in range(FC):
                ins = t.matmul(
                    slot[:, 0:512],
                    wob[:, fc * 1024 + oc * 128: fc * 1024 + oc * 128 + 128],
                    ao[:, fc * T + qtb * 512: fc * T + qtb * 512 + 512],
                    start=(fc == 0), stop=(fc == FC - 1))
            ins.then_inc(pe.h, 1)
    pe.n = pe0 + ntile

    @c.block.vector
    def _(v, pe0=pe0, dv0=dv0, st0=st0, lds0=lds0):
        st_ = c.chain(v)
        for i in range(ntile):
            oc, qtb = tile_oc_qtb(i)
            slot = c.psS1 if i % 2 == 0 else c.psS2
            v.wait_ge(pe.h, pe0 + i + 1)
            v.wait_ge(lds.h, lds0 + 16 * (i + 1))
            if i >= 1:
                v.wait_ge(st.h, st0 + 16 * i)
            st_(v.tensor_scalar(stage0[:, 0:512], slot[:, 0:512],
                                modT[:, 16 + oc:16 + oc + 1],
                                gab1[:, oc:oc + 1],
                                ALU.mult, ALU.add))
            v.tensor_tensor(stage3[:],
                            (stage1 if i % 2 == 0 else stage2),
                            stage0[:, 0:512], ALU.add).then_inc(dv.h, 1)
    dv.n = dv0 + ntile

    @c.block.gpsimd
    def _(g, dv0=dv0, st0=st0):
        for i in range(ntile):
            oc, qtb = tile_oc_qtb(i)
            g.wait_ge(dv.h, dv0 + i + 1)
            c.qdma(g, st, xs3[:, oc, qtb * 512:(qtb + 1) * 512], stage3[:])
    c.barrier()


LA_CONSTS = [L0_QK_SCALE, -EXPS, 1e-5, -0.5, L1_QK_SCALE]
A_MAP = dict(CTX=0, WQ=24576, WK=32768, WV=38912, H=45056, AO=61440,
             QH=77824, KH=79872, VH=83968, P=88064)


def build_launchA(upto="attn1", dbg=()):
    nc = bass.Bass("TRN2", target_bir_lowering=False)
    dram = {}

    def din(name, shape):
        dram[name] = nc.dram_tensor(name, shape, F32, kind="ExternalInput").ap()
        return dram[name]

    def dout(name, shape):
        dram[name] = nc.dram_tensor(name, shape, F32, kind="ExternalOutput").ap()
        return dram[name]

    x_d = din("x", [D, T])
    ctx_d = din("ctx", [CTX_D, CTX_M])
    wq_d = din("l0wq", [D, D])
    wk_d = din("l0wk", [CTX_D, D])
    wv_d = din("l0wv", [CTX_D, D])
    wout_d = din("l0wo", [D, D])
    bout_d = din("l0bout", [128, 8])
    c0_d = din("c0", [128, 8])
    wmod0_d = din("wmod0", [D, 6 * D])
    x1_d = dout("x1", [D, T])
    wg_d = din("l0wg", [I_FFN, D])
    wu_d = din("l0wu", [I_FFN, D])
    wd_d = din("l0wd", [D, I_FFN])
    _, _partials = l1_tile_classes()
    NPAR = len(_partials)
    l1wq_d = din("l1wq", [D, D])
    l1wk_d = din("l1wk", [D, D])
    l1wv_d = din("l1wv", [D, D])
    l1wo_d = din("l1wo", [D, D])
    l1bq_d = din("l1bq", [128, 8])
    l1bv_d = din("l1bv", [128, 8])
    l1bo_d = din("l1bo", [128, 8])
    l1mask_d = din("l1mask", [NPAR * 128, 512])
    c1_d = din("c1", [128, 8])
    wmod1_d = din("wmod1", [D, 6 * D])
    want_hb = any(n == "dbg_hb" for n, _ in dbg)
    h0scr_d = nc.dram_tensor("h0scr", [D, T], F32,
                             kind=("ExternalOutput" if want_hb else "Internal")).ap()
    if want_hb:
        dram["h0scr"] = h0scr_d
    full = (upto == "full")
    if full:
        h1_d = dout("h1", [D, T])
        lm_d = dout("lm", [1, 1])
        modg_d = dout("modg", [128, 8])

    dbg_outs = {}
    for name, shape in dbg:
        if name == "dbg_hb":
            continue
        dbg_outs[name] = dout(name, shape)

    with ExitStack() as stack:
        c = Ctx(nc, stack, None)
        c.register_consts(LA_CONSTS)
        c.arena_bf = c.sb("arena_bf", [128, 90112], BF16)[:]
        c.arena_f32 = c.sb("arena_f32", [128, 7168], F32)[:]
        c.ln_small = c.arena_f32[0:1, 5120:6656]
        c.psS1 = c.psum("psS1", [128, 1024])[:]
        c.psS2 = c.psum("psS2", [128, 1024])[:]
        c.psO = c.psum("psO", [128, 512])[:]
        c.psX = c.psum("psX", [128, 512])[:]
        c.psSum = c.psum("psSum", [1, 512])[:]
        c.psT = c.psum("psT", [128, 1024], BF16)[:]
        c.modT0 = c.sb("modT0", [128, 64], F32)[:]
        c.modT1 = c.sb("modT1", [128, 64], F32)[:]
        c.A = A_MAP
        c.A_SCRATCH = 0
        block = stack.enter_context(nc.Block())
        c.block = block

        setup_common(c)
        emit_mod(c, "0", wmod0_d, c0_d, c.modT0)
        x_in3 = x_d.rearrange("(f p) t -> p f t", p=128)
        xs3 = x1_d.rearrange("(f p) t -> p f t", p=128)
        emit_ln(c, "a0", x_in3, c.arena_bf[:, A_MAP["H"]:A_MAP["H"] + FC * T],
                c.modT0, 48, 0)
        if upto == "ln0":
            pass
        else:
            emit_l0_attn(c, ctx_d, wq_d, wk_d, wv_d, wout_d, bout_d,
                         x_in3, xs3, c.modT0)
        h0scr3 = h0scr_d.rearrange("(f p) t -> p f t", p=128)
        if upto in ("ffn0", "lnb0", "attn1", "full"):
            emit_ln(c, "b0", xs3, None, c.modT0, 56, 24, h_f32_dram3=h0scr3)
        if upto in ("ffn0", "attn1", "full"):
            emit_l0_ffn(c, wg_d, wu_d, wd_d, xs3, h0scr3, c.modT0)
        if upto in ("attn1", "full"):
            emit_mod(c, "1", wmod1_d, c1_d, c.modT1)
            emit_ln(c, "a1", xs3, c.arena_bf[:, A_MAP["H"]:A_MAP["H"] + FC * T],
                    c.modT1, 48, 0)
            emit_l1_attn(c, l1wq_d, l1wk_d, l1wv_d, l1wo_d, l1bq_d, l1bv_d,
                         l1bo_d, l1mask_d, NPAR, xs3, c.modT1)
        if full:
            am = c.scr_small[:, 18:19]
            h13 = h1_d.rearrange("(f p) t -> p f t", p=128)
            emit_ln(c, "b1", xs3, None, c.modT1, 56, 24,
                    h_f32_dram3=h13, absmax_to=am)
            fin2 = c.sem("fin2")

            @c.block.tensor
            def _(t):
                t.transpose(c.psX[0:1, 0:128], am, c.identf[:]
                            ).then_inc(fin2.h, 1)

            @c.block.vector
            def _(v):
                v.wait_ge(fin2.h, 1)
                v.tensor_reduce(c.scr_small[0:1, 20:21], c.psX[0:1, 0:128],
                                AX.X, ALU.max).then_inc(fin2.h, 1)

            @c.block.gpsimd
            def _(g):
                g.wait_ge(fin2.h, 2)
                c.qdma(g, c.sem("finst"), lm_d, c.scr_small[0:1, 20:21])
                c.qdma(g, c.sem("finst"), modg_d, c.modT1[:, 40:48])
            c.barrier()

        fin = c.sem("fin")

        @c.block.gpsimd
        def _(g):
            for name in dbg_outs:
                if name == "dbg_h":
                    c.qdma(g, fin, dbg_outs[name],
                           c.arena_bf[:, A_MAP["H"]:A_MAP["H"] + FC * T])
                elif name == "dbg_modT":
                    c.qdma(g, fin, dbg_outs[name], c.modT0)
                elif name == "dbg_ao":
                    c.qdma(g, fin, dbg_outs[name],
                           c.arena_bf[:, A_MAP["AO"]:A_MAP["AO"] + FC * T])
                elif name == "dbg_small":
                    c.qdma(g, fin, dbg_outs[name], c.ln_small)
            g.wait_ge(fin.h, fin.n)
        c.barrier()
    return nc



def _prep_common(inp):
    out = {}
    ctx = np.asarray(inp["L0_attn_context"], np.float32).reshape(CTX_M, CTX_D)
    out["ctx"] = np.ascontiguousarray(ctx.T)
    out["l0wq"] = np.ascontiguousarray(np.asarray(inp["L0_attn_Wq"], np.float32))
    out["l0wk"] = np.ascontiguousarray(np.asarray(inp["L0_attn_Wk"], np.float32))
    out["l0wv"] = np.ascontiguousarray(np.asarray(inp["L0_attn_Wv"], np.float32))
    out["l0wo"] = np.ascontiguousarray(np.asarray(inp["L0_attn_Wout"], np.float32))
    out["l0bout"] = np.ascontiguousarray(
        np.asarray(inp["L0_attn_bout"], np.float32).reshape(8, 128).T)
    out["wmod0"] = np.ascontiguousarray(np.asarray(inp["L0_wmod"], np.float32))
    out["l0wg"] = np.ascontiguousarray(np.asarray(inp["L0_ffn_w_gate"], np.float32))
    out["l0wu"] = np.ascontiguousarray(np.asarray(inp["L0_ffn_w_up"], np.float32))
    out["l0wd"] = np.ascontiguousarray(np.asarray(inp["L0_ffn_w_down"], np.float32))
    out["l1wq"] = np.ascontiguousarray(np.asarray(inp["L1_attn_Wq"], np.float32))
    out["l1wk"] = np.ascontiguousarray(np.asarray(inp["L1_attn_Wk"], np.float32))
    out["l1wv"] = np.ascontiguousarray(np.asarray(inp["L1_attn_Wv"], np.float32))
    out["l1wo"] = np.ascontiguousarray(np.asarray(inp["L1_attn_Wo"], np.float32))
    for nm, key in (("l1bq", "L1_attn_bq"), ("l1bv", "L1_attn_bv"),
                    ("l1bo", "L1_attn_bo")):
        out[nm] = np.ascontiguousarray(
            np.asarray(inp[key], np.float32).reshape(8, 128).T)
    out["l1mask"] = l1_mask_array().reshape(-1, 512)
    out["wmod1"] = np.ascontiguousarray(np.asarray(inp["L1_wmod"], np.float32))
    return out


def _prep_core(inp, core):
    out = {}
    out["x"] = np.ascontiguousarray(
        np.asarray(inp["L0_attn_x"][core], np.float32).T)
    out["c0"] = np.ascontiguousarray(
        np.asarray(inp["L0_c"][core], np.float32).reshape(8, 128).T)
    out["c1"] = np.ascontiguousarray(
        np.asarray(inp["L1_c"][core], np.float32).reshape(8, 128).T)
    return out



def _emit_round_halfup(c, v, vtile, itile, d1, d2, W, out, clamp):
    st_ = c.chain(v)
    st_(v.tensor_copy(itile, vtile))
    st_(v.tensor_tensor(d1, vtile, itile, ALU.subtract))
    st_(v.tensor_scalar(d2, d1, 0.5, None, ALU.is_ge))
    st_(v.tensor_tensor(d2, d2, itile, ALU.add))
    st_(v.tensor_scalar(d1, d1, -0.5, None, ALU.is_lt))
    if clamp:
        st_(v.tensor_tensor(d2, d2, d1, ALU.subtract))
        return v.tensor_scalar(out, d2, -128.0, 127.0, ALU.max, ALU.min)
    return v.tensor_tensor(out, d2, d1, ALU.subtract)


def build_launchB():
    nc = bass.Bass("TRN2", target_bir_lowering=False)
    h1_d = nc.dram_tensor("h1", [D, T], F32, kind="ExternalInput").ap()
    mx_d = nc.dram_tensor("maxes", [1, 8], F32, kind="ExternalInput").ap()
    clip_d = nc.dram_tensor("clipx", [1, 1], F32, kind="ExternalInput").ap()
    wg_d = nc.dram_tensor("l1wg", [I_FFN, D], F32, kind="ExternalInput").ap()
    wu_d = nc.dram_tensor("l1wu", [I_FFN, D], F32, kind="ExternalInput").ap()
    h2_d = nc.dram_tensor("h2", [I_FFN, T], F32, kind="ExternalOutput").ap()
    lm2_d = nc.dram_tensor("lm2", [1, 1], F32, kind="ExternalOutput").ap()

    with ExitStack() as stack:
        c = Ctx(nc, stack, None)
        c.register_consts([1.0 / 127.0])
        c.arena_bf = c.sb("arena_bf", [128, 62464], BF16)[:]
        c.arena_f32 = c.sb("arena_f32", [128, 8192], F32)[:]
        c.itile = c.sb("itile", [128, 1024], mybir.dt.int32)[:]
        c.psS1 = c.psum("psS1", [128, 1024])[:]
        c.psX = c.psum("psX", [128, 512])[:]
        c.psT = c.psum("psT", [128, 1024], BF16)[:]
        c.smallf = c.sb("smallf", [1, 16], F32)[:]
        c.swg = c.sb("swg", [128, 22], F32)[:]
        c.swu = c.sb("swu", [128, 22], F32)[:]
        c.svg = c.sb("svg", [128, 22], F32)[:]
        c.svu = c.sb("svu", [128, 22], F32)[:]
        c.rsxb = c.sb("rsxb", [128, 1], F32)[:]
        c.sxb = c.sb("sxb", [128, 1], F32)[:]
        c.am = c.sb("am", [128, 1], F32)[:]
        block = stack.enter_context(nc.Block())
        c.block = block

        hqf = c.arena_bf[:, 0:FC * T]
        wqg = c.arena_bf[:, 16384:16384 + FC * I_FFN]
        wqu = c.arena_bf[:, 38912:38912 + FC * I_FFN]
        wtm = c.arena_bf[:, 61440:61440 + 1024]
        h13 = h1_d.rearrange("(f p) t -> p f t", p=128)
        h23 = h2_d.rearrange("(i p) t -> p i t", p=128)

        setup_common(c)
        ld = c.sem("ld")
        lds = c.sem("lds")
        pe = c.sem("pe")
        ac = c.sem("ac")
        dv = c.sem("dv")
        st = c.sem("st")
        tps = c.sem("tps")
        tpe = c.sem("tpe")

        @c.block.sync
        def _(sync):
            c.qdma(sync, lds, c.smallf[0:1, 0:8], mx_d)
            c.qdma(sync, lds, c.smallf[0:1, 8:9], clip_d)

        @c.block.scalar
        def _(s, ldsn=lds.n):
            s.wait_ge(lds.h, ldsn)
            s.activation(c.smallf[0:1, 9:10], c.smallf[0:1, 8:9], AF.Sigmoid
                         ).then_inc(ac.h, 1)
        ac.n += 1

        @c.block.vector
        def _(v, acn=ac.n):
            v.wait_ge(ac.h, acn)
            st_ = c.chain(v)
            st_(v.tensor_reduce(c.smallf[0:1, 10:11], c.smallf[0:1, 0:8],
                                AX.X, ALU.max))
            st_(v.tensor_tensor(c.smallf[0:1, 11:12], c.smallf[0:1, 10:11],
                                c.smallf[0:1, 9:10], ALU.mult))
            st_(v.tensor_scalar(c.smallf[0:1, 12:13], c.smallf[0:1, 11:12],
                                1.0 / 127.0, None, ALU.mult))
            ins = v.reciprocal(c.smallf[0:1, 13:14], c.smallf[0:1, 12:13])
            dv.inc(ins)

        @c.block.tensor
        def _(t, dvn=dv.n):
            t.wait_ge(dv.h, dvn)
            t.matmul(c.psX[:, 0:1], c.ones1f[:], c.smallf[0:1, 13:14],
                     start=True, stop=True)
            t.matmul(c.psX[:, 1:2], c.ones1f[:], c.smallf[0:1, 12:13],
                     start=True, stop=True).then_inc(pe.h, 1)
        pe.n += 1

        @c.block.vector
        def _(v, pen=pe.n):
            v.wait_ge(pe.h, pen)
            st_ = c.chain(v)
            st_(v.tensor_copy(c.rsxb[:], c.psX[:, 0:1]))
            ins = v.tensor_copy(c.sxb[:], c.psX[:, 1:2])
            dv.inc(ins)
        c.barrier()

        wsrc = c.arena_f32[:, 0:1024]
        vt = c.arena_f32[:, 4096:5120]
        d1 = c.arena_f32[:, 5120:6144]
        d2 = c.arena_f32[:, 6144:7168]
        for wi, (wsrc_d, wdst, sw, sv) in enumerate(
                ((wg_d, wqg, c.swg, c.svg), (wu_d, wqu, c.swu, c.svu))):
            w3 = wsrc_d.rearrange("(n p) k -> p n k", p=128)
            for ch in range(IC):
                @c.block.sync
                def _(sync, w3=w3, ch=ch, dvn=dv.n):
                    sync.wait_ge(dv.h, dvn)
                    c.qdma(sync, ld, wsrc[:, 0:1024], w3[:, ch, :])

                @c.block.vector
                def _(v, ch=ch, sw=sw, ldn=ld.n, tpen=tpe.n):
                    v.wait_ge(ld.h, ldn)
                    v.wait_ge(tpe.h, tpen)
                    st_ = c.chain(v)
                    st_(v.tensor_reduce(sw[:, ch:ch + 1], wsrc[:, 0:1024],
                                        AX.X, ALU.max, apply_absolute_value=True))
                    st_(v.tensor_scalar(sw[:, ch:ch + 1], sw[:, ch:ch + 1],
                                        1.0 / 127.0, None, ALU.mult))
                    st_(v.reciprocal(d1[:, 0:1], sw[:, ch:ch + 1]))
                    st_(v.tensor_tensor(vt[:, 0:1024], wsrc[:, 0:1024],
                                        d1[:, 0:1].broadcast_to((128, 1024)),
                                        ALU.mult))
                    ins = _emit_round_halfup(c, v, vt[:, 0:1024],
                                             c.itile[:, 0:1024],
                                             d1[:, 0:1024], d2[:, 0:1024],
                                             1024, wtm[:, 0:1024], False)
                    ins.then_inc(dv.h, 1)
                dv.n += 1

                @c.block.tensor
                def _(t, dvn=dv.n, tps0=tps.n):
                    t.wait_ge(dv.h, dvn)
                    for g4 in range(2):
                        if tps0 + g4 >= 1:
                            t.wait_ge(tpe.h, tps0 + g4)
                        for j in range(4):
                            trc = g4 * 4 + j
                            ins = t.transpose(c.psT[:, j * 128:(j + 1) * 128],
                                              wtm[:, trc * 128:(trc + 1) * 128],
                                              c.identb[:])
                        ins.then_inc(tps.h, 1)

                @c.block.scalar
                def _(s, ch=ch, wdst=wdst, tps0=tps.n):
                    for g4 in range(2):
                        s.wait_ge(tps.h, tps0 + g4 + 1)
                        for j in range(4):
                            trc = g4 * 4 + j
                            ins = s.copy(
                                wdst[:, trc * I_FFN + ch * 128:
                                     trc * I_FFN + ch * 128 + 128],
                                c.psT[:, j * 128:(j + 1) * 128])
                        ins.then_inc(tpe.h, 1)
                tps.n += 2
                tpe.n += 2
        c.barrier()

        @c.block.vector
        def _(v):
            st_ = c.chain(v)
            st_(v.tensor_tensor(c.svg[:], c.swg[:],
                                c.sxb[:].broadcast_to((128, 22)), ALU.mult))
            st_(v.tensor_tensor(c.svu[:], c.swu[:],
                                c.sxb[:].broadcast_to((128, 22)), ALU.mult))
            st_(v.memset(c.am[:], 0.0))
        c.barrier()

        xf = c.arena_f32[:, 0:4096]
        for g in range(4):
            @c.block.sync
            def _(sync, g=g, dvn=dv.n):
                if g >= 1:
                    sync.wait_ge(dv.h, dvn)
                c.qdma(sync, ld, xf.rearrange("p (f t) -> p f t", f=8),
                       h13[:, :, g * 512:(g + 1) * 512])

            @c.block.vector
            def _(v, g=g, ldn=ld.n):
                v.wait_ge(ld.h, ldn)
                st_ = c.chain(v)
                for half in range(4):
                    seg = xf[:, half * 1024:(half + 1) * 1024]
                    st_(v.tensor_tensor(vt[:, 0:1024], seg,
                                        c.rsxb[:].broadcast_to((128, 1024)),
                                        ALU.mult))
                    ins = _emit_round_halfup(
                        c, v, vt[:, 0:1024], c.itile[:, 0:1024],
                        d1[:, 0:1024], d2[:, 0:1024], 1024,
                        wtm[:, 0:1024], True)
                    st_(ins)
                    for k in range(2):
                        fc = half * 2 + k
                        ins = v.tensor_copy(
                            hqf[:, fc * T + g * 512: fc * T + g * 512 + 512],
                            wtm[:, k * 512:(k + 1) * 512])
                        if not (half == 3 and k == 1):
                            st_(ins)
                ins.then_inc(dv.h, 1)
            dv.n += 1
            c.barrier()

        h2a = c.arena_f32[:, 0:512]
        h2b = c.arena_f32[:, 512:1024]
        sgt = c.arena_f32[:, 1024:1536]
        for ic in range(IC):
            for tb in range(4):
                idx = ic * 4 + tb

                @c.block.tensor
                def _(t, ic=ic, tb=tb, idx=idx, dvn=dv.n):
                    if idx >= 1:
                        t.wait_ge(dv.h, dvn)
                    for fc in range(FC):
                        t.matmul(c.psS1[:, 0:512],
                                 wqg[:, fc * I_FFN + ic * 128:
                                     fc * I_FFN + ic * 128 + 128],
                                 hqf[:, fc * T + tb * 512: fc * T + tb * 512 + 512],
                                 start=(fc == 0), stop=(fc == FC - 1))
                    for fc in range(FC):
                        ins = t.matmul(c.psS1[:, 512:1024],
                                       wqu[:, fc * I_FFN + ic * 128:
                                           fc * I_FFN + ic * 128 + 128],
                                       hqf[:, fc * T + tb * 512:
                                           fc * T + tb * 512 + 512],
                                       start=(fc == 0), stop=(fc == FC - 1))
                    ins.then_inc(pe.h, 1)
                pe.n += 1

                @c.block.scalar
                def _(s, ic=ic, pen=pe.n):
                    s.wait_ge(pe.h, pen)
                    s.activation(sgt, c.psS1[:, 0:512], AF.Silu,
                                 scale=c.svg[:, ic:ic + 1]).then_inc(ac.h, 1)
                ac.n += 1

                @c.block.vector
                def _(v, ic=ic, tb=tb, idx=idx, acn=ac.n, stn=st.n):
                    v.wait_ge(ac.h, acn)
                    if idx >= 2:
                        v.wait_ge(st.h, stn - 16)
                    tile = h2a if idx % 2 == 0 else h2b
                    st_ = c.chain(v)
                    st_(v.scalar_tensor_tensor(tile, c.psS1[:, 512:1024],
                                               c.svu[:, ic:ic + 1], sgt,
                                               ALU.mult, ALU.mult))
                    st_(v.tensor_reduce(d2[:, 0:1], tile, AX.X, ALU.max,
                                        apply_absolute_value=True))
                    v.tensor_tensor(c.am[:], c.am[:], d2[:, 0:1], ALU.max
                                    ).then_inc(dv.h, 1)
                dv.n += 1

                @c.block.gpsimd
                def _(g2, ic=ic, tb=tb, idx=idx, dvn=dv.n):
                    g2.wait_ge(dv.h, dvn)
                    tile = h2a if idx % 2 == 0 else h2b
                    c.qdma(g2, st, h23[:, ic, tb * 512:(tb + 1) * 512], tile)
        c.barrier()

        fin2 = c.sem("fin2")

        @c.block.tensor
        def _(t):
            t.transpose(c.psX[0:1, 0:128], c.am[:], c.identf[:]
                        ).then_inc(fin2.h, 1)

        @c.block.vector
        def _(v):
            v.wait_ge(fin2.h, 1)
            v.tensor_reduce(c.smallf[0:1, 14:15], c.psX[0:1, 0:128],
                            AX.X, ALU.max).then_inc(fin2.h, 1)

        @c.block.gpsimd
        def _(g):
            g.wait_ge(fin2.h, 2)
            c.qdma(g, c.sem("finst"), lm2_d, c.smallf[0:1, 14:15])
        c.barrier()
    return nc


def build_launchC():
    nc = bass.Bass("TRN2", target_bir_lowering=False)
    h2_d = nc.dram_tensor("h2", [I_FFN, T], F32, kind="ExternalInput").ap()
    mx_d = nc.dram_tensor("maxes2", [1, 8], F32, kind="ExternalInput").ap()
    clip_d = nc.dram_tensor("clipd", [1, 1], F32, kind="ExternalInput").ap()
    wd_d = nc.dram_tensor("l1wd", [D, I_FFN], F32, kind="ExternalInput").ap()
    x1_d = nc.dram_tensor("x1", [D, T], F32, kind="ExternalInput").ap()
    modg_d = nc.dram_tensor("modg", [128, 8], F32, kind="ExternalInput").ap()
    out_d = nc.dram_tensor("out", [D, T], F32, kind="ExternalOutput").ap()

    with ExitStack() as stack:
        c = Ctx(nc, stack, None)
        c.register_consts([1.0 / 127.0])
        c.arena_bf = c.sb("arena_bf", [128, 70400], BF16)[:]
        c.arena_f32 = c.sb("arena_f32", [128, 11776], F32)[:]
        c.itile = c.sb("itile", [128, 2816], mybir.dt.int32)[:]
        c.psS1 = c.psum("psS1", [128, 1024])[:]
        c.psX = c.psum("psX", [128, 512])[:]
        c.psT = c.psum("psT", [128, 1024], BF16)[:]
        c.smallf = c.sb("smallf", [1, 16], F32)[:]
        c.swd = c.sb("swd", [128, 8], F32)[:]
        c.coef = c.sb("coef", [128, 8], F32)[:]
        c.rsdb = c.sb("rsdb", [128, 1], F32)[:]
        c.sdb = c.sb("sdb", [128, 1], F32)[:]
        c.modg = c.sb("modgs", [128, 8], F32)[:]
        block = stack.enter_context(nc.Block())
        c.block = block

        h2qi = c.arena_bf[:, 0:IC * T]
        wdq = c.arena_bf[:, IC * T:IC * T + IC * 1024]
        wtm = c.arena_bf[:, IC * T + IC * 1024:IC * T + IC * 1024 + I_FFN]
        h23 = h2_d.rearrange("(i p) t -> p i t", p=128)
        x13 = x1_d.rearrange("(f p) t -> p f t", p=128)
        o3 = out_d.rearrange("(f p) t -> p f t", p=128)

        setup_common(c)
        ld = c.sem("ld")
        lds = c.sem("lds")
        pe = c.sem("pe")
        ac = c.sem("ac")
        dv = c.sem("dv")
        st = c.sem("st")
        tps = c.sem("tps")
        tpe = c.sem("tpe")

        @c.block.sync
        def _(sync):
            c.qdma(sync, lds, c.smallf[0:1, 0:8], mx_d)
            c.qdma(sync, lds, c.smallf[0:1, 8:9], clip_d)
            c.qdma(sync, lds, c.modg[:], modg_d)

        @c.block.scalar
        def _(s, ldsn=lds.n):
            s.wait_ge(lds.h, ldsn)
            s.activation(c.smallf[0:1, 9:10], c.smallf[0:1, 8:9], AF.Sigmoid
                         ).then_inc(ac.h, 1)
        ac.n += 1

        @c.block.vector
        def _(v, acn=ac.n):
            v.wait_ge(ac.h, acn)
            st_ = c.chain(v)
            st_(v.tensor_reduce(c.smallf[0:1, 10:11], c.smallf[0:1, 0:8],
                                AX.X, ALU.max))
            st_(v.tensor_tensor(c.smallf[0:1, 11:12], c.smallf[0:1, 10:11],
                                c.smallf[0:1, 9:10], ALU.mult))
            st_(v.tensor_scalar(c.smallf[0:1, 12:13], c.smallf[0:1, 11:12],
                                1.0 / 127.0, None, ALU.mult))
            ins = v.reciprocal(c.smallf[0:1, 13:14], c.smallf[0:1, 12:13])
            dv.inc(ins)

        @c.block.tensor
        def _(t, dvn=dv.n):
            t.wait_ge(dv.h, dvn)
            t.matmul(c.psX[:, 0:1], c.ones1f[:], c.smallf[0:1, 13:14],
                     start=True, stop=True)
            t.matmul(c.psX[:, 1:2], c.ones1f[:], c.smallf[0:1, 12:13],
                     start=True, stop=True).then_inc(pe.h, 1)
        pe.n += 1

        @c.block.vector
        def _(v, pen=pe.n):
            v.wait_ge(pe.h, pen)
            st_ = c.chain(v)
            st_(v.tensor_copy(c.rsdb[:], c.psX[:, 0:1]))
            ins = v.tensor_copy(c.sdb[:], c.psX[:, 1:2])
            dv.inc(ins)
        c.barrier()

        wsrc = c.arena_f32[:, 0:2816]
        vt = c.arena_f32[:, 2816:5632]
        dd1 = c.arena_f32[:, 5632:8448]
        dd2 = c.arena_f32[:, 8448:11264]
        w3 = wd_d.rearrange("(n p) k -> p n k", p=128)
        for ch in range(FC):
            @c.block.sync
            def _(sync, ch=ch, dvn=dv.n):
                sync.wait_ge(dv.h, dvn)
                c.qdma(sync, ld, wsrc[:, 0:I_FFN], w3[:, ch, :])

            @c.block.vector
            def _(v, ch=ch, ldn=ld.n, tpen=tpe.n):
                v.wait_ge(ld.h, ldn)
                v.wait_ge(tpe.h, tpen)
                st_ = c.chain(v)
                st_(v.tensor_reduce(c.swd[:, ch:ch + 1], wsrc[:, 0:I_FFN],
                                    AX.X, ALU.max, apply_absolute_value=True))
                st_(v.tensor_scalar(c.swd[:, ch:ch + 1], c.swd[:, ch:ch + 1],
                                    1.0 / 127.0, None, ALU.mult))
                st_(v.reciprocal(c.scr_small[:, 0:1], c.swd[:, ch:ch + 1]))
                st_(v.tensor_tensor(vt[:, 0:I_FFN], wsrc[:, 0:I_FFN],
                                    c.scr_small[:, 0:1].broadcast_to((128, I_FFN)),
                                    ALU.mult))
                ins = _emit_round_halfup(c, v, vt[:, 0:I_FFN],
                                         c.itile[:, 0:I_FFN],
                                         dd1[:, 0:I_FFN], dd2[:, 0:I_FFN],
                                         I_FFN, wtm[:, 0:I_FFN], False)
                ins.then_inc(dv.h, 1)
            dv.n += 1

            @c.block.tensor
            def _(t, dvn=dv.n, tps0=tps.n):
                t.wait_ge(dv.h, dvn)
                for g4 in range(6):
                    if tps0 + g4 >= 1:
                        t.wait_ge(tpe.h, tps0 + g4)
                    for j in range(min(4, 22 - g4 * 4)):
                        trc = g4 * 4 + j
                        ins = t.transpose(c.psT[:, j * 128:(j + 1) * 128],
                                          wtm[:, trc * 128:(trc + 1) * 128],
                                          c.identb[:])
                    ins.then_inc(tps.h, 1)

            @c.block.scalar
            def _(s, ch=ch, tps0=tps.n):
                for g4 in range(6):
                    s.wait_ge(tps.h, tps0 + g4 + 1)
                    for j in range(min(4, 22 - g4 * 4)):
                        trc = g4 * 4 + j
                        ins = s.copy(
                            wdq[:, trc * 1024 + ch * 128:
                                trc * 1024 + ch * 128 + 128],
                            c.psT[:, j * 128:(j + 1) * 128])
                    ins.then_inc(tpe.h, 1)
            tps.n += 6
            tpe.n += 6
        c.barrier()

        @c.block.vector
        def _(v):
            st_ = c.chain(v)
            st_(v.tensor_tensor(c.coef[:], c.swd[:],
                                c.sdb[:].broadcast_to((128, 8)), ALU.mult))
            st_(v.tensor_tensor(c.coef[:], c.coef[:], c.modg[:], ALU.mult))
        c.barrier()

        hsrc = c.arena_f32[:, 0:2048]
        vt2 = c.arena_f32[:, 2048:4096]
        for ch in range(IC):
            @c.block.sync
            def _(sync, ch=ch, dvn=dv.n):
                sync.wait_ge(dv.h, dvn)
                c.qdma(sync, ld, hsrc[:, 0:2048], h23[:, ch, :])

            @c.block.vector
            def _(v, ch=ch, ldn=ld.n):
                v.wait_ge(ld.h, ldn)
                st_ = c.chain(v)
                st_(v.tensor_tensor(vt2[:, 0:2048], hsrc[:, 0:2048],
                                    c.rsdb[:].broadcast_to((128, 2048)),
                                    ALU.mult))
                ins = _emit_round_halfup(c, v, vt2[:, 0:2048],
                                         c.itile[:, 0:2048],
                                         dd1[:, 0:2048], dd2[:, 0:2048],
                                         2048, h2qi[:, ch * T:(ch + 1) * T],
                                         True)
                ins.then_inc(dv.h, 1)
            dv.n += 1
            c.barrier()

        xw = c.arena_f32[:, 0:512]
        esg = c.arena_f32[:, 512:1024]
        xo = c.arena_f32[:, 1024:1536]
        for oc in range(FC):
            for tb in range(4):
                idx = oc * 4 + tb

                @c.block.tensor
                def _(t, oc=oc, tb=tb, idx=idx, dvn=dv.n):
                    if idx >= 2:
                        t.wait_ge(dv.h, dvn - 1)
                    slot = c.psS1[:, 0:512] if idx % 2 == 0 else c.psS1[:, 512:1024]
                    for ic in range(IC):
                        ins = t.matmul(slot,
                                       wdq[:, ic * 1024 + oc * 128:
                                           ic * 1024 + oc * 128 + 128],
                                       h2qi[:, ic * T + tb * 512:
                                            ic * T + tb * 512 + 512],
                                       start=(ic == 0), stop=(ic == IC - 1))
                    ins.then_inc(pe.h, 1)
                pe.n += 1

                @c.block.sync
                def _(sync, oc=oc, tb=tb, idx=idx, dvn=dv.n):
                    if idx >= 1:
                        sync.wait_ge(dv.h, dvn)
                    c.qdma(sync, lds, xw, x13[:, oc, tb * 512:(tb + 1) * 512])

                @c.block.vector
                def _(v, oc=oc, tb=tb, idx=idx, pen=pe.n, ldsn=lds.n, stn=st.n):
                    st_ = c.chain(v)
                    v.wait_ge(pe.h, pen)
                    v.wait_ge(lds.h, ldsn)
                    v.wait_ge(st.h, stn)
                    slot = c.psS1[:, 0:512] if idx % 2 == 0 else c.psS1[:, 512:1024]
                    st_(v.tensor_scalar(esg, slot, c.coef[:, oc:oc + 1], None,
                                        ALU.mult))
                    v.tensor_tensor(xo, xw, esg, ALU.add).then_inc(dv.h, 1)
                dv.n += 1

                @c.block.gpsimd
                def _(g, oc=oc, tb=tb, dvn=dv.n):
                    g.wait_ge(dv.h, dvn)
                    c.qdma(g, st, o3[:, oc, tb * 512:(tb + 1) * 512], xo)
        c.barrier()
    return nc



def kernel(**inp):
    common = _prep_common(inp)
    in_maps = []
    for core in range(8):
        m = dict(common)
        m.update(_prep_core(inp, core))
        in_maps.append(m)

    ncA = build_launchA(upto="full")
    resA = run_bass_kernel_spmd(ncA, in_maps, core_ids=list(range(8)))

    maxes = np.ascontiguousarray(
        np.stack([np.asarray(resA.results[cr]["lm"]).ravel()[0]
                  for cr in range(8)]).reshape(1, 8).astype(np.float32))
    clipx = np.asarray(inp["L1_ffn_clip_x"], np.float32).reshape(1, 1)
    wg1 = np.ascontiguousarray(np.asarray(inp["L1_ffn_w_gate"], np.float32))
    wu1 = np.ascontiguousarray(np.asarray(inp["L1_ffn_w_up"], np.float32))
    ncB = build_launchB()
    in_mapsB = [dict(h1=np.asarray(resA.results[cr]["h1"]),
                     maxes=maxes, clipx=clipx, l1wg=wg1, l1wu=wu1)
                for cr in range(8)]
    resB = run_bass_kernel_spmd(ncB, in_mapsB, core_ids=list(range(8)))

    maxes2 = np.ascontiguousarray(
        np.stack([np.asarray(resB.results[cr]["lm2"]).ravel()[0]
                  for cr in range(8)]).reshape(1, 8).astype(np.float32))
    clipd = np.asarray(inp["L1_ffn_clip_down"], np.float32).reshape(1, 1)
    wd1 = np.ascontiguousarray(np.asarray(inp["L1_ffn_w_down"], np.float32))
    ncC = build_launchC()
    in_mapsC = [dict(h2=np.asarray(resB.results[cr]["h2"]),
                     maxes2=maxes2, clipd=clipd, l1wd=wd1,
                     x1=np.asarray(resA.results[cr]["x1"]),
                     modg=np.asarray(resA.results[cr]["modg"]))
                for cr in range(8)]
    resC = run_bass_kernel_spmd(ncC, in_mapsC, core_ids=list(range(8)))

    out = np.stack([np.ascontiguousarray(
        np.asarray(resC.results[cr]["out"]).T) for cr in range(8)])
    return out.astype(np.float32)


# version 07
# speedup vs baseline: 1.9966x; 1.0413x over previous
import numpy as np
from contextlib import ExitStack

import concourse.bass as bass
import concourse.mybir as mybir
from concourse.bass_utils import run_bass_kernel_spmd

F32 = mybir.dt.float32
BF16 = mybir.dt.bfloat16
AF = mybir.ActivationFunctionType
ALU = mybir.AluOpType
AX = mybir.AxisListType

D = 1024
T = 2048
FC = D // 128
CTX_M = 4096
CTX_D = 768
CC = CTX_D // 128
L0_HEADS = 8
L1_HEADS = 16
L1_HD = 64
I_FFN = 2816
IC = I_FFN // 128
EXPS = 30.0
L0_QK_SCALE = 1.5 * float(np.float32(128.0 ** -0.5))
L1_QK_SCALE = float(np.float32(L1_HD ** -0.5))
FP4_THR = [0.25, 0.75, 1.25, 1.75, 2.5, 3.5, 5.0]

CHUNK_FRAMES = 100
LOOKAHEAD = 50


def _l1_tmax(q):
    return max((q // CHUNK_FRAMES) * CHUNK_FRAMES + CHUNK_FRAMES - 1, q + LOOKAHEAD)


def l1_tile_classes():
    classes = {}
    partials = []
    for qtb in range(T // 512):
        q0 = qtb * 512
        for ktc in range(T // 128):
            k0 = ktc * 128
            if k0 + 127 <= _l1_tmax(q0):
                classes[(qtb, ktc)] = "full"
            elif k0 > _l1_tmax(q0 + 511):
                classes[(qtb, ktc)] = "zero"
            else:
                classes[(qtb, ktc)] = "partial"
                partials.append((qtb, ktc))
    return classes, partials


def l1_mask_array():
    _, partials = l1_tile_classes()
    q = np.arange(T)
    tmax = np.maximum((q // CHUNK_FRAMES) * CHUNK_FRAMES + CHUNK_FRAMES - 1,
                      q + LOOKAHEAD)
    out = np.zeros((len(partials), 128, 512), np.float32)
    for i, (qtb, ktc) in enumerate(partials):
        ks = ktc * 128 + np.arange(128)
        qs = qtb * 512 + np.arange(512)
        out[i] = (ks[:, None] <= tmax[qs][None, :]).astype(np.float32)
    return out



class Sem:
    def __init__(self, stack, nc, name):
        self.h = stack.enter_context(nc.semaphore(name))
        self.n = 0

    def inc(self, ins, k=1):
        ins.then_inc(self.h, k)
        self.n += k
        return self.n


class Ctx:
    def __init__(self, nc, stack, block):
        self.nc = nc
        self.stack = stack
        self.block = block
        self._sems = {}
        self._cvals = []

    def sem(self, name):
        if name not in self._sems:
            self._sems[name] = Sem(self.stack, self.nc, name)
        return self._sems[name]

    def sb(self, name, shape, dtype):
        return self.stack.enter_context(self.nc.sbuf_tensor(name, shape, dtype))

    def psum(self, name, shape, dtype=F32):
        return self.stack.enter_context(self.nc.psum_tensor(name, shape, dtype))

    def register_consts(self, vals):
        vals = sorted(set(float(v) for v in vals) - {0.0, 1.0})
        self._cvals = vals
        self.consts = self.sb("consts", [128, max(len(vals), 1)], F32)
        for j, v in enumerate(vals):
            self.nc.const_aps.register(F32, v, self.consts[:, j:j + 1])

    def qdma(self, q, sem, out, in_):
        if sem.n > 0:
            q.wait_ge(sem.h, sem.n)
        ins = q.dma_start(out=out, in_=in_)
        sem.inc(ins, 16)
        return ins

    def chain(self, eng, name="chn"):
        sem = self.sem(name)

        def step(ins):
            sem.inc(ins)
            eng.wait_ge(sem.h, sem.n)
            return ins
        return step

    def barrier(self):
        b = self.sem("bar")
        pending = [(sm.h, sm.n) for nm, sm in self._sems.items()
                   if nm != "bar" and sm.n > 0]
        engines = ("sync", "tensor", "vector", "scalar", "gpsimd")
        target = b.n + len(engines)
        for name in engines:
            dec = getattr(self.block, name)

            def _body(e, name=name, target=target):
                if name == "gpsimd":
                    for h, n in pending:
                        e.wait_ge(h, n)
                    e.memset(self.barcell[:, 2:3], 0.0).then_inc(b.h, 1)
                elif name == "vector":
                    e.memset(self.barcell[:, 0:1], 0.0).then_inc(b.h, 1)
                elif name == "scalar":
                    e.memzero(self.barcell[:, 1:2]).then_inc(b.h, 1)
                else:
                    e.nop().then_inc(b.h, 1)
                e.wait_ge(b.h, target)
            dec(_body)
        b.n = target


def setup_common(c: Ctx):
    c.onesb = c.sb("onesb", [128, 1], BF16)
    c.ones1f = c.sb("ones1f", [1, 128], F32)
    c.identb = c.sb("identb", [128, 128], BF16)
    c.identf = c.sb("identf", [128, 128], F32)
    c.scr_small = c.sb("scr_small", [128, 64], F32)
    c.barcell = c.sb("barcell", [1, 4], F32)

    @c.block.vector
    def _(v):
        if c._cvals:
            for j, val in enumerate(c._cvals):
                v.memset(c.consts[:, j:j + 1], val)
        v.memset(c.onesb[:], 1.0)
        v.memset(c.ones1f[:], 1.0)

    @c.block.gpsimd
    def _(g):
        st_ = c.chain(g, "chg")
        for ident in (c.identb, c.identf):
            st_(g.memset(ident[:], 0.0))
            st_(g.affine_select(out=ident[:], in_=ident[:],
                                compare_op=ALU.not_equal, fill=1.0,
                                base=0, pattern=[[-1, 128]],
                                channel_multiplier=1))
    c.barrier()



def emit_mod(c: Ctx, tag, wmod_d, c_d, modT):
    ld = c.sem(f"md{tag}")
    pe = c.sem(f"mp{tag}")
    ac = c.sem(f"ma{tag}")

    crow = c.sb(f"crow{tag}", [128, 8], F32)
    siluf = c.sb(f"siluf{tag}", [128, 8], F32)
    wmf = c.arena_f32[:, 0:6144]
    wm3 = wmod_d.rearrange("(f p) j -> p f j", p=128)

    @c.block.sync
    def _(sync):
        c.qdma(sync, ld, crow[:], c_d)

    @c.block.scalar
    def _(s, ldn=ld.n):
        s.wait_ge(ld.h, ldn)
        s.activation(siluf[:], crow[:], AF.Silu).then_inc(ac.h, 1)
    ac.n += 1

    pe0, ac0 = pe.n, ac.n
    for piece in range(8):
        @c.block.sync
        def _(sync, piece=piece, pe0=pe0):
            if piece >= 1:
                sync.wait_ge(pe.h, pe0 + piece)
            c.qdma(sync, ld, wmf.rearrange("p (f j) -> p f j", f=8),
                   wm3[:, :, piece * 768:(piece + 1) * 768])

        @c.block.tensor
        def _(t, piece=piece, ldn=ld.n, ac0=ac0):
            t.wait_ge(ld.h, ldn)
            t.wait_ge(ac.h, ac0)
            for jloc in range(6):
                jc = piece * 6 + jloc
                for fc in range(FC):
                    ins = t.matmul(
                        c.psX[:, jc:jc + 1],
                        wmf[:, fc * 768 + jloc * 128: fc * 768 + (jloc + 1) * 128],
                        siluf[:, fc:fc + 1],
                        start=(fc == 0), stop=(fc == FC - 1))
            ins.then_inc(pe.h, 1)
    pe.n = pe0 + 8

    @c.block.vector
    def _(v, pen=pe.n):
        v.wait_ge(pe.h, pen)
        st_ = c.chain(v)
        st_(v.tensor_copy(modT[:, 0:48], c.psX[:, 0:48]))
        for col in (16, 40):
            g = modT[:, col:col + 8]
            sgn = c.scr_small[:, 0:8]
            av = c.scr_small[:, 8:16]
            st_(v.tensor_scalar(sgn, g, 0.0, None, ALU.is_ge))
            st_(v.tensor_scalar(sgn, sgn, 2.0, -1.0, ALU.mult, ALU.add))
            st_(v.tensor_scalar(av, g, -1.0, None, ALU.mult))
            st_(v.tensor_tensor(av, av, g, ALU.max))
            st_(v.tensor_scalar(av, av, 0.25, None, ALU.max))
            st_(v.tensor_tensor(g, sgn, av, ALU.mult))
        st_(v.tensor_scalar(modT[:, 48:56], modT[:, 8:16], 1.0, None, ALU.add))
        st_(v.tensor_scalar(modT[:, 56:64], modT[:, 32:40], 1.0, None, ALU.add))
    c.barrier()



def emit_ln(c: Ctx, tag, x_src3, h_dst, modT, onec_col, s_col,
            h_f32_dram3=None, absmax_to=None):
    ld = c.sem(f"ln_ld{tag}")
    pe = c.sem(f"ln_pe{tag}")
    dv = c.sem(f"ln_dv{tag}")
    ac = c.sem(f"ln_ac{tag}")
    st = c.sem(f"ln_st{tag}")

    xf = c.arena_f32[:, 0:4096]
    hf = xf
    xbb = c.arena_bf[:, c.A_SCRATCH:c.A_SCRATCH + 4096]
    xsq = c.arena_bf[:, c.A_SCRATCH + 4096:c.A_SCRATCH + 8192]
    small = c.ln_small

    if absmax_to is not None:
        @c.block.vector
        def _(v):
            v.memset(absmax_to, 0.0)

    for g in range(4):
        @c.block.sync
        def _(sync, g=g):
            c.qdma(sync, ld, xf.rearrange("p (f t) -> p f t", f=8),
                   x_src3[:, :, g * 512:(g + 1) * 512])

        @c.block.vector
        def _(v, ldn=ld.n):
            v.wait_ge(ld.h, ldn)
            ins = v.tensor_copy(xbb, xf)
            dv.inc(ins)

        @c.block.scalar
        def _(s, dvn=dv.n):
            s.wait_ge(dv.h, dvn)
            ins = s.activation(xsq, xbb, AF.Square)
            ac.inc(ins)

        @c.block.tensor
        def _(t, acn=ac.n):
            t.wait_ge(ac.h, acn)
            for fc in range(FC):
                t.matmul(c.psS1[0:1, 0:512], c.onesb[:],
                         xbb[:, fc * 512:(fc + 1) * 512],
                         start=(fc == 0), stop=(fc == FC - 1))
            for fc in range(FC):
                ins = t.matmul(c.psS1[0:1, 512:1024], c.onesb[:],
                               xsq[:, fc * 512:(fc + 1) * 512],
                               start=(fc == 0), stop=(fc == FC - 1))
            pe.inc(ins)

        @c.block.vector
        def _(v, pen=pe.n):
            v.wait_ge(pe.h, pen)
            st_ = c.chain(v)
            m, varr, lv = (small[0:1, 0:512], small[0:1, 512:1024],
                           small[0:1, 1024:1536])
            st_(v.tensor_scalar(m, c.psS1[0:1, 0:512], 1.0 / D, None, ALU.mult))
            st_(v.tensor_scalar(lv, c.psS1[0:1, 512:1024], 1.0 / D, None, ALU.mult))
            st_(v.tensor_tensor(varr, m, m, ALU.mult))
            ins = v.tensor_tensor(varr, lv, varr, ALU.subtract)
            dv.inc(ins)

        @c.block.scalar
        def _(s, dvn=dv.n):
            s.wait_ge(dv.h, dvn)
            st_ = c.chain(s, "chs")
            varr, lv = small[0:1, 512:1024], small[0:1, 1024:1536]
            st_(s.activation(lv, varr, AF.Ln, bias=1e-5))
            ins = s.activation(varr, lv, AF.Exp, scale=-0.5)
            ac.inc(ins)

        @c.block.tensor
        def _(t, acn=ac.n):
            t.wait_ge(ac.h, acn)
            t.matmul(c.psX[:, 0:512], c.ones1f[:], small[0:1, 0:512],
                     start=True, stop=True)
            ins = t.matmul(c.psO[:, 0:512], c.ones1f[:], small[0:1, 512:1024],
                           start=True, stop=True)
            pe.inc(ins)

        @c.block.vector
        def _(v, pen=pe.n):
            v.wait_ge(pe.h, pen)
            st_ = c.chain(v)
            for fc in range(FC):
                hseg = hf[:, fc * 512:(fc + 1) * 512]
                st_(v.tensor_tensor(hseg, xf[:, fc * 512:(fc + 1) * 512],
                                    c.psX[:, 0:512], ALU.subtract))
                if fc < FC - 1:
                    st_(v.tensor_tensor(hseg, hseg, c.psO[:, 0:512], ALU.mult))
                else:
                    ins = v.tensor_tensor(hseg, hseg, c.psO[:, 0:512], ALU.mult)
            dv.inc(ins)

        @c.block.vector
        def _(v, g=g, dvn=dv.n):
            st_ = c.chain(v)
            for fc in range(FC):
                if h_dst is not None:
                    ins = v.tensor_scalar(
                        h_dst[:, fc * T + g * 512: fc * T + g * 512 + 512],
                        hf[:, fc * 512:(fc + 1) * 512],
                        modT[:, onec_col + fc:onec_col + fc + 1],
                        modT[:, s_col + fc:s_col + fc + 1],
                        ALU.mult, ALU.add)
                    if h_f32_dram3 is None and fc < FC - 1:
                        st_(ins)
                if h_f32_dram3 is not None:
                    hstage = c.arena_f32[:, 4096 + (fc % 2) * 512:
                                         4096 + (fc % 2) * 512 + 512]
                    tgt = st.n + 16 * (fc - 1)
                    if tgt > 0:
                        v.wait_ge(st.h, tgt)
                    st_(v.tensor_tensor(
                        hstage, hf[:, fc * 512:(fc + 1) * 512],
                        modT[:, onec_col + fc:onec_col + fc + 1]
                        .broadcast_to((128, 512)), ALU.mult))
                    ins = v.tensor_tensor(
                        hstage, hstage,
                        modT[:, s_col + fc:s_col + fc + 1]
                        .broadcast_to((128, 512)), ALU.add)
                    ins.then_inc(dv.h, 1)
                    if absmax_to is not None:
                        v.wait_ge(dv.h, dvn + fc + 1)
                        tm = c.scr_small[:, 16:17]
                        st_(v.tensor_reduce(tm, hstage, AX.X, ALU.max,
                                            apply_absolute_value=True))
                        st_(v.tensor_tensor(absmax_to, absmax_to, tm, ALU.max))
            if h_f32_dram3 is None:
                ac.inc(ins)

        if h_f32_dram3 is not None:
            dv.n += FC

            @c.block.gpsimd
            def _(gq, g=g, dv0=dv.n - FC, st0=st.n):
                for fc in range(FC):
                    hstage = c.arena_f32[:, 4096 + (fc % 2) * 512:
                                         4096 + (fc % 2) * 512 + 512]
                    gq.wait_ge(dv.h, dv0 + fc + 1)
                    c.qdma(gq, st, h_f32_dram3[:, fc, g * 512:(g + 1) * 512],
                           hstage)



        c.barrier()

    if h_f32_dram3 is not None:
        @c.block.gpsimd
        def _(gq):
            gq.wait_ge(st.h, st.n)
        c.barrier()

    if h_f32_dram3 is not None:
        @c.block.gpsimd
        def _(gq, stn=st.n):
            gq.wait_ge(st.h, stn)
        c.barrier()



def emit_l0_attn(c: Ctx, ctx_d, wq_d, wk_d, wv_d, wout_d, bout_d, x_in3, xs3,
                 modT):
    A = c.A
    ld = c.sem("a2ld")
    pe = c.sem("a2pe")
    ac = c.sem("a2ac")
    dv = c.sem("a2dv")
    st = c.sem("a2st")

    ctxb = c.arena_bf[:, A["CTX"]:A["CTX"] + CC * CTX_M]
    wqb = c.arena_bf[:, A["WQ"]:A["WQ"] + FC * 1024]
    wkb = c.arena_bf[:, A["WK"]:A["WK"] + CC * 1024]
    wvb = c.arena_bf[:, A["WV"]:A["WV"] + CC * 1024]
    h_bf = c.arena_bf[:, A["H"]:A["H"] + FC * T]
    ao = c.arena_bf[:, A["AO"]:A["AO"] + FC * T]
    qh = c.arena_bf[:, A["QH"]:A["QH"] + T]
    kh = c.arena_bf[:, A["KH"]:A["KH"] + CTX_M]
    vh = c.arena_bf[:, A["VH"]:A["VH"] + CTX_M]
    pt = c.arena_bf[:, A["P"]:A["P"] + 2048]
    stage0 = c.arena_f32[:, 0:512]
    stage1 = c.arena_f32[:, 512:1024]
    stage2 = c.arena_f32[:, 1024:1536]
    stage3 = c.arena_f32[:, 1536:2048]
    smalls = c.ln_small

    @c.block.gpsimd
    def _(g):
        c.qdma(g, ld, ctxb.rearrange("p (c m) -> p c m", c=CC),
               ctx_d.rearrange("(c p) m -> p c m", p=128))
        c.qdma(g, ld, wqb.rearrange("p (f o) -> p f o", f=FC),
               wq_d.rearrange("(f p) o -> p f o", p=128))
        c.qdma(g, ld, wkb.rearrange("p (f o) -> p f o", f=CC),
               wk_d.rearrange("(f p) o -> p f o", p=128))
        c.qdma(g, ld, wvb.rearrange("p (f o) -> p f o", f=CC),
               wv_d.rearrange("(f p) o -> p f o", p=128))

    @c.block.tensor
    def _(t, ldn=ld.n):
        t.wait_ge(ld.h, ldn)
    c.barrier()

    for head in range(L0_HEADS):
        pe0, ac0 = pe.n, ac.n
        slots = [c.psX, c.psO]

        @c.block.tensor
        def _(t, head=head, pe0=pe0, ac0=ac0):
            gi = 0
            for qtb in range(4):
                if gi >= 2:
                    t.wait_ge(ac.h, ac0 + gi - 1)
                for fc in range(FC):
                    ins = t.matmul(
                        slots[gi % 2][:, 0:512],
                        wqb[:, fc * 1024 + head * 128: fc * 1024 + head * 128 + 128],
                        h_bf[:, fc * T + qtb * 512: fc * T + qtb * 512 + 512],
                        start=(fc == 0), stop=(fc == FC - 1))
                ins.then_inc(pe.h, 1)
                gi += 1
            for kb in range(8):
                t.wait_ge(ac.h, ac0 + gi - 1)
                for cc in range(CC):
                    ins = t.matmul(
                        slots[gi % 2][:, 0:512],
                        wkb[:, cc * 1024 + head * 128: cc * 1024 + head * 128 + 128],
                        ctxb[:, cc * CTX_M + kb * 512: cc * CTX_M + kb * 512 + 512],
                        start=(cc == 0), stop=(cc == CC - 1))
                ins.then_inc(pe.h, 1)
                gi += 1
            for vg in range(8):
                t.wait_ge(ac.h, ac0 + gi - 1)
                for j in range(4):
                    ktc = vg * 4 + j
                    for cc in range(CC):
                        ins = t.matmul(
                            slots[gi % 2][:, j * 128:(j + 1) * 128],
                            ctxb[:, cc * CTX_M + ktc * 128: cc * CTX_M + ktc * 128 + 128],
                            wvb[:, cc * 1024 + head * 128: cc * 1024 + head * 128 + 128],
                            start=(cc == 0), stop=(cc == CC - 1))
                ins.then_inc(pe.h, 1)
                gi += 1
        pe.n = pe0 + 20

        @c.block.scalar
        def _(s, pe0=pe0):
            gi = 0
            for qtb in range(4):
                s.wait_ge(pe.h, pe0 + gi + 1)
                s.copy(qh[:, qtb * 512:(qtb + 1) * 512],
                       slots[gi % 2][:, 0:512]).then_inc(ac.h, 1)
                gi += 1
            for kb in range(8):
                s.wait_ge(pe.h, pe0 + gi + 1)
                s.copy(kh[:, kb * 512:(kb + 1) * 512],
                       slots[gi % 2][:, 0:512]).then_inc(ac.h, 1)
                gi += 1
            for vg in range(8):
                s.wait_ge(pe.h, pe0 + gi + 1)
                s.copy(vh[:, vg * 512:(vg + 1) * 512],
                       slots[gi % 2][:, 0:512]).then_inc(ac.h, 1)
                gi += 1
        ac.n = ac0 + 20
        c.barrier()

        for qtb in range(4):
            pe0, ac0, dv0 = pe.n, ac.n, dv.n

            @c.block.tensor
            def _(t, head=head, qtb=qtb, pe0=pe0, ac0=ac0, dv0=dv0):
                def scores(i):
                    slot = c.psS1 if i % 2 == 0 else c.psS2
                    if i >= 2:
                        t.wait_ge(ac.h, ac0 + i - 1)
                    for j in range(2):
                        ktc = i * 2 + j
                        ins = t.matmul(slot[:, j * 512:(j + 1) * 512],
                                       kh[:, ktc * 128:(ktc + 1) * 128],
                                       qh[:, qtb * 512:(qtb + 1) * 512],
                                       start=True, stop=True)
                    ins.then_inc(pe.h, 1)

                def pv(i):
                    t.wait_ge(ac.h, ac0 + i + 1)
                    pslot = pt[:, (i % 2) * 1024:(i % 2) * 1024 + 1024]
                    for j in range(2):
                        ktc = i * 2 + j
                        t.matmul(c.psO[:, 0:512],
                                 vh[:, ktc * 128:(ktc + 1) * 128],
                                 pslot[:, j * 512:(j + 1) * 512],
                                 start=(ktc == 0), stop=(ktc == 31))
                        ins = t.matmul(c.psSum[0:1, 0:512], c.onesb[:],
                                       pslot[:, j * 512:(j + 1) * 512],
                                       start=(ktc == 0), stop=(ktc == 31))
                    ins.then_inc(pe.h, 1)

                scores(0)
                scores(1)
                for i in range(14):
                    pv(i)
                    scores(i + 2)
                pv(14)
                pv(15)
                t.wait_ge(dv.h, dv0 + 1)
                t.matmul(c.psX[:, 0:512], c.ones1f[:],
                         smalls[0:1, 512:1024], start=True, stop=True
                         ).then_inc(pe.h, 1)

            pe.n = pe0 + 33

            @c.block.scalar
            def _(s, head=head, qtb=qtb, pe0=pe0, ac0=ac0):
                for i in range(16):
                    s.wait_ge(pe.h, pe0 + (i + 1 if i < 2 else 2 * i))
                    slot = c.psS1 if i % 2 == 0 else c.psS2
                    s.activation(pt[:, (i % 2) * 1024:(i % 2) * 1024 + 1024],
                                 slot[:, 0:1024], AF.Exp,
                                 scale=L0_QK_SCALE, bias=-EXPS
                                 ).then_inc(ac.h, 1)
                s.wait_ge(pe.h, pe0 + 32)
                s.copy(smalls[0:1, 0:512], c.psSum[0:1, 0:512]).then_inc(ac.h, 1)
                s.copy(stage0[:], c.psO[:, 0:512]).then_inc(ac.h, 1)
            ac.n = ac0 + 18

            @c.block.vector
            def _(v, head=head, qtb=qtb, ac0=ac0, pe0=pe0):
                v.wait_ge(ac.h, ac0 + 17)
                v.reciprocal(smalls[0:1, 512:1024], smalls[0:1, 0:512]
                             ).then_inc(dv.h, 1)
                v.wait_ge(ac.h, ac0 + 18)
                v.wait_ge(pe.h, pe0 + 33)
                v.tensor_tensor(
                    ao[:, head * T + qtb * 512: head * T + qtb * 512 + 512],
                    stage0[:], c.psX[:, 0:512], ALU.mult).then_inc(dv.h, 1)
            dv.n = dv0 + 2
            c.barrier()

    woutb = wqb
    bout_t = c.sb("bout_t", [128, 8], F32)
    gab = c.scr_small[:, 24:32]

    @c.block.gpsimd
    def _(g):
        c.qdma(g, ld, woutb.rearrange("p (f o) -> p f o", f=FC),
               wout_d.rearrange("(f p) o -> p f o", p=128))
        c.qdma(g, ld, bout_t[:], bout_d)

    @c.block.vector
    def _(v, ldn=ld.n):
        v.wait_ge(ld.h, ldn)
        v.tensor_tensor(gab, modT[:, 16:24], bout_t[:], ALU.mult).then_inc(dv.h, 1)
    dv.n += 1
    c.barrier()

    pe0, ac0, dv0, st0 = pe.n, ac.n, dv.n, st.n
    lds = c.sem("a2lds")
    ntile = 32

    def tile_oc_qtb(i):
        return i % 8, i // 8

    @c.block.sync
    def _(sync, dv0=dv0):
        for i in range(ntile):
            oc, qtb = tile_oc_qtb(i)
            if i >= 2:
                sync.wait_ge(dv.h, dv0 + i - 1)
            c.qdma(sync, lds, (stage1 if i % 2 == 0 else stage2),
                   x_in3[:, oc, qtb * 512:(qtb + 1) * 512])

    @c.block.tensor
    def _(t, pe0=pe0, dv0=dv0):
        for i in range(ntile):
            oc, qtb = tile_oc_qtb(i)
            slot = c.psS1 if i % 2 == 0 else c.psS2
            if i >= 2:
                t.wait_ge(dv.h, dv0 + i - 1)
            for fc in range(FC):
                ins = t.matmul(
                    slot[:, 0:512],
                    woutb[:, fc * 1024 + oc * 128: fc * 1024 + oc * 128 + 128],
                    ao[:, fc * T + qtb * 512: fc * T + qtb * 512 + 512],
                    start=(fc == 0), stop=(fc == FC - 1))
            ins.then_inc(pe.h, 1)
    pe.n = pe0 + ntile

    @c.block.vector
    def _(v, pe0=pe0, dv0=dv0, st0=st0):
        st_ = c.chain(v)
        for i in range(ntile):
            oc, qtb = tile_oc_qtb(i)
            slot = c.psS1 if i % 2 == 0 else c.psS2
            v.wait_ge(pe.h, pe0 + i + 1)
            v.wait_ge(lds.h, 16 * (i + 1))
            if i >= 1:
                v.wait_ge(st.h, st0 + 16 * i)
            st_(v.tensor_scalar(stage0[:], slot[:, 0:512],
                                modT[:, 16 + oc:16 + oc + 1],
                                gab[:, oc:oc + 1],
                                ALU.mult, ALU.add))
            v.tensor_tensor(stage3[:],
                            (stage1 if i % 2 == 0 else stage2),
                            stage0[:], ALU.add).then_inc(dv.h, 1)
    dv.n = dv0 + ntile

    @c.block.gpsimd
    def _(g, dv0=dv0, st0=st0):
        for i in range(ntile):
            oc, qtb = tile_oc_qtb(i)
            g.wait_ge(dv.h, dv0 + i + 1)
            c.qdma(g, st, xs3[:, oc, qtb * 512:(qtb + 1) * 512], stage3[:])
    c.barrier()



def _fp4_half_ops(eng, src, dst, utile, qtile, tmp, rs, hs, o, w):
    b0, wb = o // 16, w // 16
    s_ = lambda a: a[:, o:o + w]
    s3 = lambda a: a[:, o:o + w].rearrange("p (n b) -> p n b", b=16)
    r3 = lambda a: (a[:, b0:b0 + wb].rearrange("p (n o) -> p n o", o=1)
                    .broadcast_to((128, wb, 16)))
    ops = [lambda: eng.scalar_tensor_tensor(s_(utile), s_(src), -1.0,
                                            s_(src), ALU.mult, ALU.max),
           lambda: eng.tensor_tensor(s3(utile), s3(utile), r3(rs), ALU.mult),
           lambda: eng.tensor_scalar(s_(qtile), s_(utile), FP4_THR[0],
                                     None, ALU.is_gt)]
    for thr in FP4_THR[1:4]:
        ops.append(lambda thr=thr: eng.tensor_scalar(
            s_(tmp), s_(utile), thr, None, ALU.is_gt))
        ops.append(lambda: eng.tensor_tensor(s_(qtile), s_(qtile), s_(tmp),
                                             ALU.add))
    for thr, wt in ((FP4_THR[4], 2.0), (FP4_THR[5], 2.0), (FP4_THR[6], 4.0)):
        ops.append(lambda thr=thr: eng.tensor_scalar(
            s_(tmp), s_(utile), thr, None, ALU.is_gt))
        ops.append(lambda wt=wt: eng.scalar_tensor_tensor(
            s_(qtile), s_(tmp), wt, s_(qtile), ALU.mult, ALU.add))
    ops.append(lambda: eng.tensor_scalar(s_(tmp), s_(src), 0.0, None,
                                         ALU.is_ge))
    ops.append(lambda: eng.tensor_scalar(s_(tmp), s_(tmp), 2.0, -1.0,
                                         ALU.mult, ALU.add))
    ops.append(lambda: eng.tensor_tensor(s_(qtile), s_(qtile), s_(tmp),
                                         ALU.mult))
    ops.append(lambda: eng.tensor_tensor(s3(dst), s3(qtile), r3(hs),
                                         ALU.mult))
    return ops


def _interleave_chains(c, eng, ops_lists, sem_names):
    sems = [c.sem(nm) for nm in sem_names]
    idx = [0] * len(ops_lists)
    pend = [None] * len(ops_lists)
    finals = [None] * len(ops_lists)
    remaining = sum(len(o) for o in ops_lists)
    while remaining:
        for i, ops in enumerate(ops_lists):
            if idx[i] >= len(ops):
                continue
            if pend[i] is not None:
                eng.wait_ge(sems[i].h, pend[i])
            ins = ops[idx[i]]()
            idx[i] += 1
            remaining -= 1
            if idx[i] == len(ops):
                finals[i] = ins
            else:
                sems[i].inc(ins)
                pend[i] = sems[i].n
    return finals


def emit_fp4_quant_dual(c, src, dst, W, utile, qtile, tmp, qsm,
                        vwaits=(), gwaits=(), dv=None, gq=None):
    nb = W // 16
    hw = (W // 32) * 16
    bm = qsm[:, 0:nb]
    sc = qsm[:, nb:2 * nb]
    rs = qsm[:, 2 * nb:3 * nb]
    hs = qsm[:, 3 * nb:4 * nb]
    smalls_at = dv.n + 1
    gq_prev = gq.n

    NIL = 4
    qw = (W // (16 * NIL)) * 16
    cuts = [0, qw, 2 * qw, 3 * qw, W]
    ils = [c.sem(f"il{i}") for i in range(NIL)]

    @c.block.vector
    def _(v, vwaits=tuple(vwaits), il_prev=tuple(s.n for s in ils[1:])):
        st_ = c.chain(v)
        for h, n in vwaits:
            v.wait_ge(h, n)
        for s, n in zip(ils[1:], il_prev):
            if n:
                v.wait_ge(s.h, n)
        st_(v.tensor_reduce(bm, src.rearrange("p (n b) -> p n b", b=16),
                            AX.X, ALU.max, apply_absolute_value=True))
        st_(v.tensor_scalar(sc, bm, 1.0 / 6.0, 1e-12, ALU.mult, ALU.max))
        st_(v.tensor_scalar(hs, sc, 0.5, None, ALU.mult))
        st_(v.reciprocal(rs, sc))
        v.memset(c.scr_small[0:1, 23:24], 0.0).then_inc(dv.h, 1)
        opsl = [_fp4_half_ops(v, src, dst, utile, qtile, tmp, rs, hs,
                              cuts[i], cuts[i + 1] - cuts[i])
                for i in range(NIL)]
        fins = _interleave_chains(c, v, opsl, [f"il{i}" for i in range(NIL)])
        fins[0].then_inc(dv.h, 1)
        for s, f in zip(ils[1:], fins[1:]):
            s.inc(f)
    dv.n += 2

    @c.block.gpsimd
    def _(g, dv_done=dv.n, il_done=tuple(s.n for s in ils[1:])):
        g.wait_ge(dv.h, dv_done)
        for s, n in zip(ils[1:], il_done):
            g.wait_ge(s.h, n)
        g.memset(c.scr_small[0:1, 22:23], 0.0).then_inc(gq.h, 1)
    gq.n += 1



A3 = dict(WG=0, WU=22528, WD=45056, H2Q=67584, HQT=78848, H2QTM=79872,
          QS1=82688, QS2=85504, QS3=88320)


def emit_l0_ffn(c: Ctx, wg_d, wu_d, wd_d, xs3, h0scr3, modT):
    ld = c.sem("f0ld")
    pe = c.sem("f0pe")
    ac = c.sem("f0ac")
    dv = c.sem("f0dv")
    st = c.sem("f0st")

    wgf = c.arena_bf[:, A3["WG"]:A3["WG"] + FC * I_FFN]
    wuf = c.arena_bf[:, A3["WU"]:A3["WU"] + FC * I_FFN]
    wdf = c.arena_bf[:, A3["WD"]:A3["WD"] + IC * 1024]
    h2q = c.arena_bf[:, A3["H2Q"]:A3["H2Q"] + IC * 512]
    hqt = c.arena_bf[:, A3["HQT"]:A3["HQT"] + 1024]
    h2qtm = c.arena_bf[:, A3["H2QTM"]:A3["H2QTM"] + I_FFN]
    qs1 = c.arena_bf[:, A3["QS1"]:A3["QS1"] + I_FFN]
    qs2 = c.arena_bf[:, A3["QS2"]:A3["QS2"] + I_FFN]
    qs3 = c.arena_bf[:, A3["QS3"]:A3["QS3"] + 1792]
    hx = c.arena_f32[:, 0:1024]
    htm = c.arena_f32[:, 1024:2048]
    sg = c.arena_f32[:, 2048:2560]
    h2f = c.arena_f32[:, 2560:5376]
    qsm = c.arena_f32[:, 6416:6416 + 704]
    xw = c.arena_f32[:, 0:512]
    gq = c.sem("f0gq")

    tps = c.sem("f0tps")
    tpe = c.sem("f0tpe")
    wsrcs = (c.arena_f32[:, 0:2816], c.arena_f32[:, 2816:5632])
    chl = []
    for wi, (wd_src, wdst, nchunk, wwidth) in enumerate(
            ((wg_d, wgf, IC, 1024), (wu_d, wuf, IC, 1024), (wd_d, wdf, FC, I_FFN))):
        w3 = wd_src.rearrange("(n p) k -> p n k", p=128)
        for ch in range(nchunk):
            chl.append((w3, ch, wdst, wwidth, wi))
    prevw = {}
    for k, (w3, ch, wdst, wwidth, wi) in enumerate(chl):
        wsrc = wsrcs[k % 2]
        @c.block.sync
        def _(sync, w3=w3, ch=ch, wwidth=wwidth, wsrc=wsrc,
              pw=prevw.get(k % 2)):
            if pw is not None:
                sync.wait_ge(dv.h, pw[0])
                sync.wait_ge(gq.h, pw[1])
            c.qdma(sync, ld, wsrc[:, 0:wwidth], w3[:, ch, :])

        emit_fp4_quant_dual(c, wsrc[:, 0:wwidth], qs2[:, 0:wwidth], wwidth,
                            qs1[:, 0:wwidth], h2qtm[:, 0:wwidth],
                            h2q[:, 0:wwidth], qsm,
                            vwaits=[(ld.h, ld.n), (tpe.h, tpe.n)],
                            gwaits=[(ld.h, ld.n), (tpe.h, tpe.n)],
                            dv=dv, gq=gq)
        prevw[k % 2] = (dv.n, gq.n)

        ntr = wwidth // 128
        ngrp = (ntr + 3) // 4
        @c.block.tensor
        def _(t, ntr=ntr, ngrp=ngrp, dvn=dv.n, gqn=gq.n, tps0=tps.n,
              tpe0=tpe.n):
            t.wait_ge(dv.h, dvn)
            t.wait_ge(gq.h, gqn)
            for g4 in range(ngrp):
                if tps0 + g4 >= 1:
                    t.wait_ge(tpe.h, tps0 + g4)
                for j in range(min(4, ntr - g4 * 4)):
                    trc = g4 * 4 + j
                    ins = t.transpose(
                        c.psT[:, j * 128:(j + 1) * 128],
                        qs2[:, trc * 128:(trc + 1) * 128], c.identb[:])
                ins.then_inc(tps.h, 1)

        @c.block.scalar
        def _(s, ch=ch, wdst=wdst, ntr=ntr, ngrp=ngrp, wi=wi, tps0=tps.n):
            for g4 in range(ngrp):
                s.wait_ge(tps.h, tps0 + g4 + 1)
                nj = min(4, ntr - g4 * 4)
                for j in range(nj):
                    trc = g4 * 4 + j
                    stride = I_FFN if wi < 2 else 1024
                    dst = wdst[:, trc * stride + ch * 128:
                               trc * stride + ch * 128 + 128]
                    ins = s.copy(dst, c.psT[:, j * 128:(j + 1) * 128])
                ins.then_inc(tpe.h, 1)
        tps.n += ngrp
        tpe.n += ngrp
    c.barrier()

    IBW = [512, 512, 512, 512, 512, 256]
    for tb in range(4):
        for tci in range(4):
            tc = tb * 4 + tci
            @c.block.sync
            def _(sync, tc=tc, dvn=dv.n):
                sync.wait_ge(dv.h, dvn)
                c.qdma(sync, ld, hx.rearrange("p (f t) -> p f t", f=8),
                       h0scr3[:, :, tc * 128:(tc + 1) * 128])

            @c.block.tensor
            def _(t, ldn=ld.n, acn=ac.n):
                t.wait_ge(ld.h, ldn)
                t.wait_ge(ac.h, acn)
                for j in range(8):
                    slot = c.psX if j < 4 else c.psO
                    ins = t.transpose(slot[:, (j % 4) * 128:(j % 4) * 128 + 128],
                                      hx[:, j * 128:(j + 1) * 128], c.identf[:])
                ins.then_inc(pe.h, 1)
            pe.n += 1

            @c.block.scalar
            def _(s, pen=pe.n, gqn=gq.n):
                s.wait_ge(pe.h, pen)
                s.wait_ge(gq.h, gqn)
                st2 = c.chain(s, "chs")
                st2(s.copy(htm[:, 0:512], c.psX[:, 0:512]))
                s.copy(htm[:, 512:1024], c.psO[:, 0:512]).then_inc(ac.h, 1)
            ac.n += 1

            emit_fp4_quant_dual(c, htm[:, 0:1024], qs2[:, 0:1024], 1024,
                                qs1[:, 0:1024], h2qtm[:, 0:1024],
                                qs3[:, 0:1024], qsm,
                                vwaits=[(ac.h, ac.n)], gwaits=[(ac.h, ac.n)],
                                dv=dv, gq=gq)

            @c.block.tensor
            def _(t, dvn=dv.n, gqn=gq.n, tpe0=tpe.n):
                t.wait_ge(dv.h, dvn)
                t.wait_ge(gq.h, gqn)
                for g4 in range(2):
                    t.wait_ge(tpe.h, tpe0 + g4)
                    for j in range(4):
                        trc = g4 * 4 + j
                        ins = t.transpose(c.psT[:, j * 128:(j + 1) * 128],
                                          qs2[:, trc * 128:(trc + 1) * 128],
                                          c.identb[:])
                    ins.then_inc(tps.h, 1)
            tps.n += 2

            @c.block.scalar
            def _(s, tps0=tps.n - 2):
                for g4 in range(2):
                    s.wait_ge(tps.h, tps0 + g4 + 1)
                    for j in range(4):
                        trc = g4 * 4 + j
                        ins = s.copy(hqt[:, trc * 128:(trc + 1) * 128],
                                     c.psT[:, j * 128:(j + 1) * 128])
                    ins.then_inc(tpe.h, 1)
            tpe.n += 2

            off = 0
            for ib, ibw in enumerate(IBW):
                @c.block.tensor
                def _(t, ib=ib, ibw=ibw, off=off, tpen=tpe.n, dvn=dv.n):
                    t.wait_ge(tpe.h, tpen)
                    t.wait_ge(dv.h, dvn)
                    for fc in range(FC):
                        t.matmul(c.psS2[:, 0:ibw],
                                 hqt[:, fc * 128:(fc + 1) * 128],
                                 wgf[:, fc * I_FFN + off: fc * I_FFN + off + ibw],
                                 start=(fc == 0), stop=(fc == FC - 1))
                    for fc in range(FC):
                        ins = t.matmul(c.psS2[:, 512:512 + ibw],
                                       hqt[:, fc * 128:(fc + 1) * 128],
                                       wuf[:, fc * I_FFN + off: fc * I_FFN + off + ibw],
                                       start=(fc == 0), stop=(fc == FC - 1))
                    ins.then_inc(pe.h, 1)
                pe.n += 1

                @c.block.scalar
                def _(s, ibw=ibw, pen=pe.n, dvn=dv.n):
                    s.wait_ge(pe.h, pen)
                    s.wait_ge(dv.h, dvn)
                    s.activation(sg[:, 0:ibw], c.psS2[:, 0:ibw], AF.Silu
                                 ).then_inc(ac.h, 1)
                ac.n += 1

                @c.block.vector
                def _(v, ib=ib, ibw=ibw, off=off, acn=ac.n, stn=st.n):
                    v.wait_ge(ac.h, acn)
                    if ib == 0:
                        v.wait_ge(st.h, stn)
                    v.tensor_tensor(h2f[:, off:off + ibw], sg[:, 0:ibw],
                                    c.psS2[:, 512:512 + ibw], ALU.mult
                                    ).then_inc(dv.h, 1)
                dv.n += 1
                off += ibw

            emit_fp4_quant_dual(c, h2f[:, 0:I_FFN], h2qtm[:, 0:I_FFN], I_FFN,
                                qs1[:, 0:I_FFN], qs2[:, 0:I_FFN],
                                h2qtm[:, 0:I_FFN], qsm,
                                vwaits=[(dv.h, dv.n)], gwaits=[(dv.h, dv.n)],
                                dv=dv, gq=gq)

            @c.block.tensor
            def _(t, dvn=dv.n, gqn=gq.n, tpe0=tpe.n):
                t.wait_ge(dv.h, dvn)
                t.wait_ge(gq.h, gqn)
                for g4 in range(6):
                    t.wait_ge(tpe.h, tpe0 + g4)
                    for j in range(min(4, 22 - g4 * 4)):
                        trc = g4 * 4 + j
                        ins = t.transpose(c.psT[:, j * 128:(j + 1) * 128],
                                          h2qtm[:, trc * 128:(trc + 1) * 128],
                                          c.identb[:])
                    ins.then_inc(tps.h, 1)
            tps.n += 6

            @c.block.scalar
            def _(s, tci=tci, tps0=tps.n - 6):
                for g4 in range(6):
                    s.wait_ge(tps.h, tps0 + g4 + 1)
                    for j in range(min(4, 22 - g4 * 4)):
                        trc = g4 * 4 + j
                        ins = s.copy(h2q[:, trc * 512 + tci * 128:
                                        trc * 512 + tci * 128 + 128],
                                     c.psT[:, j * 128:(j + 1) * 128])
                    ins.then_inc(tpe.h, 1)
            tpe.n += 6

        for oc in range(FC):
            @c.block.tensor
            def _(t, oc=oc, tpen=tpe.n, dvn=dv.n):
                t.wait_ge(tpe.h, tpen)
                if oc >= 2:
                    t.wait_ge(dv.h, dvn - 1)
                slot = c.psS1[:, 0:512] if oc % 2 == 0 else c.psS1[:, 512:1024]
                for ic in range(IC):
                    ins = t.matmul(slot,
                                   wdf[:, ic * 1024 + oc * 128:
                                       ic * 1024 + oc * 128 + 128],
                                   h2q[:, ic * 512:(ic + 1) * 512],
                                   start=(ic == 0), stop=(ic == IC - 1))
                ins.then_inc(pe.h, 1)
            pe.n += 1

            @c.block.sync
            def _(sync, tb=tb, oc=oc, dvn=dv.n, pen=pe.n):
                if oc >= 1:
                    sync.wait_ge(dv.h, dvn)
                else:
                    sync.wait_ge(pe.h, pen)
                c.qdma(sync, ld, xw[:, 0:512],
                       xs3[:, oc, tb * 512:(tb + 1) * 512])

            @c.block.vector
            def _(v, oc=oc, tb=tb, pen=pe.n, ldn=ld.n, stn=st.n):
                st_ = c.chain(v)
                v.wait_ge(pe.h, pen)
                v.wait_ge(ld.h, ldn)
                v.wait_ge(st.h, stn)
                slot = c.psS1[:, 0:512] if oc % 2 == 0 else c.psS1[:, 512:1024]
                esg = c.arena_f32[:, 5904:6416]
                st_(v.tensor_scalar(esg, slot,
                                    modT[:, 40 + oc:40 + oc + 1], None, ALU.mult))
                v.tensor_tensor(h2f[:, 0:512], xw[:, 0:512], esg,
                                ALU.add).then_inc(dv.h, 1)
            dv.n += 1

            @c.block.gpsimd
            def _(g, oc=oc, tb=tb, dvn=dv.n):
                g.wait_ge(dv.h, dvn)
                c.qdma(g, st, xs3[:, oc, tb * 512:(tb + 1) * 512],
                       h2f[:, 0:512])
        c.barrier()
    c.barrier()



A5 = dict(WQ=0, WK=8192, WV=16384, WO=24576, MASK=32768, QH=77824, KH=79872,
          VP=81920, VP2=82960, PM=84000)


def emit_l1_attn(c: Ctx, wq_d, wk_d, wv_d, wo_d, bq_d, bv_d, bo_d, mask_d,
                 n_par, xs3, modT):
    A = c.A
    ld = c.sem("a5ld")
    pe = c.sem("a5pe")
    ac = c.sem("a5ac")
    dv = c.sem("a5dv")
    st = c.sem("a5st")

    wqb = c.arena_bf[:, A5["WQ"]:A5["WQ"] + FC * 1024]
    wkb = c.arena_bf[:, A5["WK"]:A5["WK"] + FC * 1024]
    wvb = c.arena_bf[:, A5["WV"]:A5["WV"] + FC * 1024]
    wob = c.arena_bf[:, A5["WO"]:A5["WO"] + FC * 1024]
    maskb = c.arena_bf[:, A5["MASK"]:A5["MASK"] + n_par * 512]
    h_bf = c.arena_bf[:, A["H"]:A["H"] + FC * T]
    ao = c.arena_bf[:, A["AO"]:A["AO"] + FC * T]
    qh2 = c.arena_bf[:, A5["QH"]:A5["QH"] + T]
    kh2 = c.arena_bf[:, A5["KH"]:A5["KH"] + T]
    vps = [c.arena_bf[:, A5["VP"]:A5["VP"] + 16 * 65],
           c.arena_bf[:, A5["VP2"]:A5["VP2"] + 16 * 65]]
    ptm = c.arena_bf[:, A5["PM"]:A5["PM"] + 512]
    pt = c.arena_bf[:, A["P"]:A["P"] + 2048]
    stage0 = c.arena_f32[:, 0:512]
    stage1 = c.arena_f32[:, 512:1024]
    stage2 = c.arena_f32[:, 1024:1536]
    stage3 = c.arena_f32[:, 1536:2048]
    smalls = c.ln_small

    bq_t = c.sb("bq_t", [128, 8], F32)
    bv_t = c.sb("bv_t", [128, 8], F32)
    bo_t = c.sb("bo_t", [128, 8], F32)
    bqs = c.sb("bqs", [128, 8], F32)
    gab1 = c.scr_small[:, 40:48]

    classes, partials = l1_tile_classes()
    pidx = {tt_: i for i, tt_ in enumerate(partials)}

    @c.block.gpsimd
    def _(g):
        for wd, dst in ((wq_d, wqb), (wk_d, wkb), (wv_d, wvb), (wo_d, wob)):
            c.qdma(g, ld, dst.rearrange("p (f o) -> p f o", f=FC),
                   wd.rearrange("(f p) o -> p f o", p=128))
        c.qdma(g, ld, maskb.rearrange("p (n q) -> p n q", n=n_par),
               mask_d.rearrange("(n p) q -> p n q", p=128))

    lds = c.sem("a5lds")

    @c.block.sync
    def _(sync):
        c.qdma(sync, lds, bq_t[:], bq_d)
        c.qdma(sync, lds, bv_t[:], bv_d)
        c.qdma(sync, lds, bo_t[:], bo_d)

    @c.block.vector
    def _(v, ldn=ld.n, ldsn=lds.n):
        v.wait_ge(ld.h, ldn)
        v.wait_ge(lds.h, ldsn)
        st_ = c.chain(v)
        st_(v.tensor_scalar(bqs[:], bq_t[:], L1_QK_SCALE, None, ALU.mult))
        st_(v.tensor_tensor(gab1, modT[:, 16:24], bo_t[:], ALU.mult))
        st_(v.memset(vps[0].rearrange("p (n w) -> p n w", w=65)[:, :, 64:65], 1.0))
        ins = v.memset(vps[1].rearrange("p (n w) -> p n w", w=65)[:, :, 64:65], 1.0)
        dv.inc(ins)
    c.barrier()

    for hp in range(L1_HEADS // 2):
        pe0, ac0, dv0 = pe.n, ac.n, dv.n

        @c.block.tensor
        def _(t, hp=hp, pe0=pe0, dv0=dv0):
            for qtb in range(4):
                if qtb >= 1:
                    t.wait_ge(dv.h, dv0 + 2 * qtb - 1)
                for fc in range(FC):
                    ins = t.matmul(
                        c.psX[:, 0:512],
                        wqb[:, fc * 1024 + hp * 128: fc * 1024 + hp * 128 + 128],
                        h_bf[:, fc * T + qtb * 512: fc * T + qtb * 512 + 512],
                        start=(fc == 0), stop=(fc == FC - 1))
                ins.then_inc(pe.h, 1)
                if qtb >= 1:
                    t.wait_ge(dv.h, dv0 + 2 * qtb)
                for fc in range(FC):
                    ins = t.matmul(
                        c.psO[:, 0:512],
                        wkb[:, fc * 1024 + hp * 128: fc * 1024 + hp * 128 + 128],
                        h_bf[:, fc * T + qtb * 512: fc * T + qtb * 512 + 512],
                        start=(fc == 0), stop=(fc == FC - 1))
                ins.then_inc(pe.h, 1)
            for tg in range(4):
                if tg >= 1:
                    t.wait_ge(dv.h, dv0 + 8 + tg)
                for j in range(4):
                    tc = tg * 4 + j
                    for fc in range(FC):
                        ins = t.matmul(
                            c.psS1[:, j * 256:j * 256 + 128],
                            h_bf[:, fc * T + tc * 128: fc * T + tc * 128 + 128],
                            wvb[:, fc * 1024 + hp * 128: fc * 1024 + hp * 128 + 128],
                            start=(fc == 0), stop=(fc == FC - 1))
                ins.then_inc(pe.h, 1)

        @c.block.vector
        def _(v, hp=hp, pe0=pe0):
            st_ = c.chain(v)
            for qtb in range(4):
                v.wait_ge(pe.h, pe0 + 2 * qtb + 1)
                v.tensor_scalar(qh2[:, qtb * 512:(qtb + 1) * 512],
                                c.psX[:, 0:512], L1_QK_SCALE,
                                bqs[:, hp:hp + 1],
                                ALU.mult, ALU.add).then_inc(dv.h, 1)
                v.wait_ge(pe.h, pe0 + 2 * qtb + 2)
                v.tensor_copy(kh2[:, qtb * 512:(qtb + 1) * 512],
                              c.psO[:, 0:512]).then_inc(dv.h, 1)
            for tg in range(4):
                v.wait_ge(pe.h, pe0 + 8 + tg + 1)
                for j in range(4):
                    tc = tg * 4 + j
                    st_(v.tensor_copy(vps[0][:, tc * 65:tc * 65 + 64],
                                      c.psS1[:, j * 256:j * 256 + 64]))
                    ins = v.tensor_copy(vps[1][:, tc * 65:tc * 65 + 64],
                                        c.psS1[:, j * 256 + 64:j * 256 + 128])
                ins.then_inc(dv.h, 1)
        pe.n += 12
        dv.n += 12
        c.barrier()

        sc = c.sem("a5sc")
        pvs = c.sem("a5pv")
        bc = c.sem("a5bc")
        for sub in range(2):
          head = 2 * hp + sub
          hb = sub * 64
          vpp = vps[sub]
          for qtb in range(4):
            allowed = [k for k in range(16) if classes[(qtb, k)] != "zero"]
            groups = [allowed[i:i + 2] for i in range(0, len(allowed), 2)]
            ng = len(groups)
            ac0, dv0 = ac.n, dv.n
            sc0, pv0, bc0 = sc.n, pvs.n, bc.n

            @c.block.tensor
            def _(t, head=head, hb=hb, vpp=vpp, qtb=qtb, ac0=ac0, dv0=dv0,
                  sc0=sc0, pv0=pv0, groups=groups, ng=ng):
                def scores(i):
                    slot = c.psS1 if i % 2 == 0 else c.psS2
                    if i >= 2:
                        t.wait_ge(ac.h, ac0 + i - 1)
                    for j, ktc in enumerate(groups[i]):
                        ins = t.matmul(slot[:, j * 512:(j + 1) * 512],
                                       kh2[hb:hb + 64,
                                           ktc * 128:(ktc + 1) * 128],
                                       qh2[hb:hb + 64,
                                           qtb * 512:(qtb + 1) * 512],
                                       start=True, stop=True)
                    ins.then_inc(sc.h, 1)

                def pv(i):
                    t.wait_ge(dv.h, dv0 + i + 1)
                    pslot = pt[:, (i % 2) * 1024:(i % 2) * 1024 + 1024]
                    for j, ktc in enumerate(groups[i]):
                        first = (i == 0 and j == 0)
                        last = (i == ng - 1 and j == len(groups[i]) - 1)
                        ins = t.matmul(c.psO[0:65, 0:512],
                                       vpp[:, ktc * 65:(ktc + 1) * 65],
                                       pslot[:, j * 512:(j + 1) * 512],
                                       start=first, stop=last)
                    ins.then_inc(pvs.h, 1)

                scores(0)
                if ng > 1:
                    scores(1)
                for i in range(ng - 2):
                    pv(i)
                    scores(i + 2)
                if ng > 1:
                    pv(ng - 2)
                pv(ng - 1)
                t.wait_ge(dv.h, dv0 + ng + 1)
                t.matmul(c.psX[:, 0:512], c.ones1f[:],
                         smalls[0:1, 512:1024], start=True, stop=True
                         ).then_inc(bc.h, 1)
            sc.n = sc0 + ng
            pvs.n = pv0 + ng
            bc.n = bc0 + 1

            @c.block.scalar
            def _(s, qtb=qtb, ac0=ac0, sc0=sc0, pv0=pv0, groups=groups, ng=ng):
                for i in range(ng):
                    s.wait_ge(sc.h, sc0 + i + 1)
                    if i >= 2:
                        s.wait_ge(pvs.h, pv0 + i - 1)
                    slot = c.psS1 if i % 2 == 0 else c.psS2
                    w = 512 * len(groups[i])
                    s.activation(pt[:, (i % 2) * 1024:(i % 2) * 1024 + w],
                                 slot[:, 0:w], AF.Exp, bias=-EXPS
                                 ).then_inc(ac.h, 1)
                s.wait_ge(pvs.h, pv0 + ng)
                s.copy(stage0[0:65, :], c.psO[0:65, 0:512]).then_inc(ac.h, 1)
            ac.n = ac0 + ng + 1

            @c.block.vector
            def _(v, head=head, hb=hb, qtb=qtb, ac0=ac0, dv0=dv0, bc0=bc0,
                  groups=groups, ng=ng):
                st_ = c.chain(v)
                for i in range(ng):
                    v.wait_ge(ac.h, ac0 + i + 1)
                    for j, ktc in enumerate(groups[i]):
                        if classes[(qtb, ktc)] == "partial":
                            pi = pidx[(qtb, ktc)]
                            seg = pt[:, (i % 2) * 1024 + j * 512:
                                     (i % 2) * 1024 + j * 512 + 512]
                            st_(v.tensor_tensor(
                                ptm[:, 0:512], seg,
                                maskb[:, pi * 512:(pi + 1) * 512], ALU.mult))
                            st_(v.tensor_copy(seg, ptm[:, 0:512]))
                    ins = v.memset(c.barcell[:, 3:4], 0.0)
                    dv.inc(ins)
                    v.wait_ge(dv.h, dv.n)
                v.wait_ge(ac.h, ac0 + ng + 1)
                ins = v.reciprocal(smalls[0:1, 512:1024], stage0[64:65, 0:512])
                dv.inc(ins)
                v.wait_ge(bc.h, bc0 + 1)
                st_(v.tensor_tensor(stage1[0:64, :], stage0[0:64, :],
                                    c.psX[0:64, 0:512], ALU.mult))
                ins = v.tensor_scalar(
                    ao[hb:hb + 64,
                       (head // 2) * T + qtb * 512:
                       (head // 2) * T + qtb * 512 + 512],
                    stage1[0:64, :], 1.0,
                    bv_t[hb:hb + 64, head // 2:head // 2 + 1],
                    ALU.mult, ALU.add)
                dv.inc(ins)
            c.barrier()

    pe0, ac0, dv0, st0, ld0 = pe.n, ac.n, dv.n, st.n, ld.n
    lds0 = lds.n
    ntile = 32

    def tile_oc_qtb(i):
        return i % 8, i // 8

    @c.block.sync
    def _(sync, dv0=dv0, lds0=lds0):
        for i in range(ntile):
            oc, qtb = tile_oc_qtb(i)
            if i >= 2:
                sync.wait_ge(dv.h, dv0 + i - 1)
            c.qdma(sync, lds, (stage1 if i % 2 == 0 else stage2),
                   xs3[:, oc, qtb * 512:(qtb + 1) * 512])

    @c.block.tensor
    def _(t, pe0=pe0, dv0=dv0):
        for i in range(ntile):
            oc, qtb = tile_oc_qtb(i)
            slot = c.psS1 if i % 2 == 0 else c.psS2
            if i >= 2:
                t.wait_ge(dv.h, dv0 + i - 1)
            for fc in range(FC):
                ins = t.matmul(
                    slot[:, 0:512],
                    wob[:, fc * 1024 + oc * 128: fc * 1024 + oc * 128 + 128],
                    ao[:, fc * T + qtb * 512: fc * T + qtb * 512 + 512],
                    start=(fc == 0), stop=(fc == FC - 1))
            ins.then_inc(pe.h, 1)
    pe.n = pe0 + ntile

    @c.block.vector
    def _(v, pe0=pe0, dv0=dv0, st0=st0, lds0=lds0):
        st_ = c.chain(v)
        for i in range(ntile):
            oc, qtb = tile_oc_qtb(i)
            slot = c.psS1 if i % 2 == 0 else c.psS2
            v.wait_ge(pe.h, pe0 + i + 1)
            v.wait_ge(lds.h, lds0 + 16 * (i + 1))
            if i >= 1:
                v.wait_ge(st.h, st0 + 16 * i)
            st_(v.tensor_scalar(stage0[:, 0:512], slot[:, 0:512],
                                modT[:, 16 + oc:16 + oc + 1],
                                gab1[:, oc:oc + 1],
                                ALU.mult, ALU.add))
            v.tensor_tensor(stage3[:],
                            (stage1 if i % 2 == 0 else stage2),
                            stage0[:, 0:512], ALU.add).then_inc(dv.h, 1)
    dv.n = dv0 + ntile

    @c.block.gpsimd
    def _(g, dv0=dv0, st0=st0):
        for i in range(ntile):
            oc, qtb = tile_oc_qtb(i)
            g.wait_ge(dv.h, dv0 + i + 1)
            c.qdma(g, st, xs3[:, oc, qtb * 512:(qtb + 1) * 512], stage3[:])
    c.barrier()


LA_CONSTS = [L0_QK_SCALE, -EXPS, 1e-5, -0.5, L1_QK_SCALE]
A_MAP = dict(CTX=0, WQ=24576, WK=32768, WV=38912, H=45056, AO=61440,
             QH=77824, KH=79872, VH=83968, P=88064)


def build_launchA(upto="attn1", dbg=()):
    nc = bass.Bass("TRN2", target_bir_lowering=False)
    dram = {}

    def din(name, shape):
        dram[name] = nc.dram_tensor(name, shape, F32, kind="ExternalInput").ap()
        return dram[name]

    def dout(name, shape):
        dram[name] = nc.dram_tensor(name, shape, F32, kind="ExternalOutput").ap()
        return dram[name]

    x_d = din("x", [D, T])
    ctx_d = din("ctx", [CTX_D, CTX_M])
    wq_d = din("l0wq", [D, D])
    wk_d = din("l0wk", [CTX_D, D])
    wv_d = din("l0wv", [CTX_D, D])
    wout_d = din("l0wo", [D, D])
    bout_d = din("l0bout", [128, 8])
    c0_d = din("c0", [128, 8])
    wmod0_d = din("wmod0", [D, 6 * D])
    x1_d = dout("x1", [D, T])
    wg_d = din("l0wg", [I_FFN, D])
    wu_d = din("l0wu", [I_FFN, D])
    wd_d = din("l0wd", [D, I_FFN])
    _, _partials = l1_tile_classes()
    NPAR = len(_partials)
    l1wq_d = din("l1wq", [D, D])
    l1wk_d = din("l1wk", [D, D])
    l1wv_d = din("l1wv", [D, D])
    l1wo_d = din("l1wo", [D, D])
    l1bq_d = din("l1bq", [128, 8])
    l1bv_d = din("l1bv", [128, 8])
    l1bo_d = din("l1bo", [128, 8])
    l1mask_d = din("l1mask", [NPAR * 128, 512])
    c1_d = din("c1", [128, 8])
    wmod1_d = din("wmod1", [D, 6 * D])
    want_hb = any(n == "dbg_hb" for n, _ in dbg)
    h0scr_d = nc.dram_tensor("h0scr", [D, T], F32,
                             kind=("ExternalOutput" if want_hb else "Internal")).ap()
    if want_hb:
        dram["h0scr"] = h0scr_d
    full = (upto == "full")
    if full:
        h1_d = dout("h1", [D, T])
        lm_d = dout("lm", [1, 1])
        modg_d = dout("modg", [128, 8])

    dbg_outs = {}
    for name, shape in dbg:
        if name == "dbg_hb":
            continue
        dbg_outs[name] = dout(name, shape)

    with ExitStack() as stack:
        c = Ctx(nc, stack, None)
        c.register_consts(LA_CONSTS)
        c.arena_bf = c.sb("arena_bf", [128, 90112], BF16)[:]
        c.arena_f32 = c.sb("arena_f32", [128, 7168], F32)[:]
        c.ln_small = c.arena_f32[0:1, 5120:6656]
        c.psS1 = c.psum("psS1", [128, 1024])[:]
        c.psS2 = c.psum("psS2", [128, 1024])[:]
        c.psO = c.psum("psO", [128, 512])[:]
        c.psX = c.psum("psX", [128, 512])[:]
        c.psSum = c.psum("psSum", [1, 512])[:]
        c.psT = c.psum("psT", [128, 1024], BF16)[:]
        c.modT0 = c.sb("modT0", [128, 64], F32)[:]
        c.modT1 = c.sb("modT1", [128, 64], F32)[:]
        c.A = A_MAP
        c.A_SCRATCH = 0
        block = stack.enter_context(nc.Block())
        c.block = block

        setup_common(c)
        emit_mod(c, "0", wmod0_d, c0_d, c.modT0)
        x_in3 = x_d.rearrange("(f p) t -> p f t", p=128)
        xs3 = x1_d.rearrange("(f p) t -> p f t", p=128)
        emit_ln(c, "a0", x_in3, c.arena_bf[:, A_MAP["H"]:A_MAP["H"] + FC * T],
                c.modT0, 48, 0)
        if upto == "ln0":
            pass
        else:
            emit_l0_attn(c, ctx_d, wq_d, wk_d, wv_d, wout_d, bout_d,
                         x_in3, xs3, c.modT0)
        h0scr3 = h0scr_d.rearrange("(f p) t -> p f t", p=128)
        if upto in ("ffn0", "lnb0", "attn1", "full"):
            emit_ln(c, "b0", xs3, None, c.modT0, 56, 24, h_f32_dram3=h0scr3)
        if upto in ("ffn0", "attn1", "full"):
            emit_l0_ffn(c, wg_d, wu_d, wd_d, xs3, h0scr3, c.modT0)
        if upto in ("attn1", "full"):
            emit_mod(c, "1", wmod1_d, c1_d, c.modT1)
            emit_ln(c, "a1", xs3, c.arena_bf[:, A_MAP["H"]:A_MAP["H"] + FC * T],
                    c.modT1, 48, 0)
            emit_l1_attn(c, l1wq_d, l1wk_d, l1wv_d, l1wo_d, l1bq_d, l1bv_d,
                         l1bo_d, l1mask_d, NPAR, xs3, c.modT1)
        if full:
            am = c.scr_small[:, 18:19]
            h13 = h1_d.rearrange("(f p) t -> p f t", p=128)
            emit_ln(c, "b1", xs3, None, c.modT1, 56, 24,
                    h_f32_dram3=h13, absmax_to=am)
            fin2 = c.sem("fin2")

            @c.block.tensor
            def _(t):
                t.transpose(c.psX[0:1, 0:128], am, c.identf[:]
                            ).then_inc(fin2.h, 1)

            @c.block.vector
            def _(v):
                v.wait_ge(fin2.h, 1)
                v.tensor_reduce(c.scr_small[0:1, 20:21], c.psX[0:1, 0:128],
                                AX.X, ALU.max).then_inc(fin2.h, 1)

            @c.block.gpsimd
            def _(g):
                g.wait_ge(fin2.h, 2)
                c.qdma(g, c.sem("finst"), lm_d, c.scr_small[0:1, 20:21])
                c.qdma(g, c.sem("finst"), modg_d, c.modT1[:, 40:48])
            c.barrier()

        fin = c.sem("fin")

        @c.block.gpsimd
        def _(g):
            for name in dbg_outs:
                if name == "dbg_h":
                    c.qdma(g, fin, dbg_outs[name],
                           c.arena_bf[:, A_MAP["H"]:A_MAP["H"] + FC * T])
                elif name == "dbg_modT":
                    c.qdma(g, fin, dbg_outs[name], c.modT0)
                elif name == "dbg_ao":
                    c.qdma(g, fin, dbg_outs[name],
                           c.arena_bf[:, A_MAP["AO"]:A_MAP["AO"] + FC * T])
                elif name == "dbg_small":
                    c.qdma(g, fin, dbg_outs[name], c.ln_small)
            g.wait_ge(fin.h, fin.n)
        c.barrier()
    return nc



def _prep_common(inp):
    out = {}
    ctx = np.asarray(inp["L0_attn_context"], np.float32).reshape(CTX_M, CTX_D)
    out["ctx"] = np.ascontiguousarray(ctx.T)
    out["l0wq"] = np.ascontiguousarray(np.asarray(inp["L0_attn_Wq"], np.float32))
    out["l0wk"] = np.ascontiguousarray(np.asarray(inp["L0_attn_Wk"], np.float32))
    out["l0wv"] = np.ascontiguousarray(np.asarray(inp["L0_attn_Wv"], np.float32))
    out["l0wo"] = np.ascontiguousarray(np.asarray(inp["L0_attn_Wout"], np.float32))
    out["l0bout"] = np.ascontiguousarray(
        np.asarray(inp["L0_attn_bout"], np.float32).reshape(8, 128).T)
    out["wmod0"] = np.ascontiguousarray(np.asarray(inp["L0_wmod"], np.float32))
    out["l0wg"] = np.ascontiguousarray(np.asarray(inp["L0_ffn_w_gate"], np.float32))
    out["l0wu"] = np.ascontiguousarray(np.asarray(inp["L0_ffn_w_up"], np.float32))
    out["l0wd"] = np.ascontiguousarray(np.asarray(inp["L0_ffn_w_down"], np.float32))
    out["l1wq"] = np.ascontiguousarray(np.asarray(inp["L1_attn_Wq"], np.float32))
    out["l1wk"] = np.ascontiguousarray(np.asarray(inp["L1_attn_Wk"], np.float32))
    out["l1wv"] = np.ascontiguousarray(np.asarray(inp["L1_attn_Wv"], np.float32))
    out["l1wo"] = np.ascontiguousarray(np.asarray(inp["L1_attn_Wo"], np.float32))
    for nm, key in (("l1bq", "L1_attn_bq"), ("l1bv", "L1_attn_bv"),
                    ("l1bo", "L1_attn_bo")):
        out[nm] = np.ascontiguousarray(
            np.asarray(inp[key], np.float32).reshape(8, 128).T)
    out["l1mask"] = l1_mask_array().reshape(-1, 512)
    out["wmod1"] = np.ascontiguousarray(np.asarray(inp["L1_wmod"], np.float32))
    return out


def _prep_core(inp, core):
    out = {}
    out["x"] = np.ascontiguousarray(
        np.asarray(inp["L0_attn_x"][core], np.float32).T)
    out["c0"] = np.ascontiguousarray(
        np.asarray(inp["L0_c"][core], np.float32).reshape(8, 128).T)
    out["c1"] = np.ascontiguousarray(
        np.asarray(inp["L1_c"][core], np.float32).reshape(8, 128).T)
    return out



def _emit_round_halfup(c, v, vtile, itile, d1, d2, W, out, clamp):
    st_ = c.chain(v)
    st_(v.tensor_copy(itile, vtile))
    st_(v.tensor_tensor(d1, vtile, itile, ALU.subtract))
    st_(v.tensor_scalar(d2, d1, 0.5, None, ALU.is_ge))
    st_(v.tensor_tensor(d2, d2, itile, ALU.add))
    st_(v.tensor_scalar(d1, d1, -0.5, None, ALU.is_lt))
    if clamp:
        st_(v.tensor_tensor(d2, d2, d1, ALU.subtract))
        return v.tensor_scalar(out, d2, -128.0, 127.0, ALU.max, ALU.min)
    return v.tensor_tensor(out, d2, d1, ALU.subtract)


def build_launchB():
    nc = bass.Bass("TRN2", target_bir_lowering=False)
    h1_d = nc.dram_tensor("h1", [D, T], F32, kind="ExternalInput").ap()
    mx_d = nc.dram_tensor("maxes", [1, 8], F32, kind="ExternalInput").ap()
    clip_d = nc.dram_tensor("clipx", [1, 1], F32, kind="ExternalInput").ap()
    wg_d = nc.dram_tensor("l1wg", [I_FFN, D], F32, kind="ExternalInput").ap()
    wu_d = nc.dram_tensor("l1wu", [I_FFN, D], F32, kind="ExternalInput").ap()
    h2_d = nc.dram_tensor("h2", [I_FFN, T], F32, kind="ExternalOutput").ap()
    lm2_d = nc.dram_tensor("lm2", [1, 1], F32, kind="ExternalOutput").ap()

    with ExitStack() as stack:
        c = Ctx(nc, stack, None)
        c.register_consts([1.0 / 127.0])
        c.arena_bf = c.sb("arena_bf", [128, 62464], BF16)[:]
        c.arena_f32 = c.sb("arena_f32", [128, 8192], F32)[:]
        c.itile = c.sb("itile", [128, 1024], mybir.dt.int32)[:]
        c.psS1 = c.psum("psS1", [128, 1024])[:]
        c.psX = c.psum("psX", [128, 512])[:]
        c.psT = c.psum("psT", [128, 1024], BF16)[:]
        c.smallf = c.sb("smallf", [1, 16], F32)[:]
        c.swg = c.sb("swg", [128, 22], F32)[:]
        c.swu = c.sb("swu", [128, 22], F32)[:]
        c.svg = c.sb("svg", [128, 22], F32)[:]
        c.svu = c.sb("svu", [128, 22], F32)[:]
        c.rsxb = c.sb("rsxb", [128, 1], F32)[:]
        c.sxb = c.sb("sxb", [128, 1], F32)[:]
        c.am = c.sb("am", [128, 1], F32)[:]
        block = stack.enter_context(nc.Block())
        c.block = block

        hqf = c.arena_bf[:, 0:FC * T]
        wqg = c.arena_bf[:, 16384:16384 + FC * I_FFN]
        wqu = c.arena_bf[:, 38912:38912 + FC * I_FFN]
        wtm = c.arena_bf[:, 61440:61440 + 1024]
        h13 = h1_d.rearrange("(f p) t -> p f t", p=128)
        h23 = h2_d.rearrange("(i p) t -> p i t", p=128)

        setup_common(c)
        ld = c.sem("ld")
        lds = c.sem("lds")
        pe = c.sem("pe")
        ac = c.sem("ac")
        dv = c.sem("dv")
        st = c.sem("st")
        tps = c.sem("tps")
        tpe = c.sem("tpe")

        @c.block.sync
        def _(sync):
            c.qdma(sync, lds, c.smallf[0:1, 0:8], mx_d)
            c.qdma(sync, lds, c.smallf[0:1, 8:9], clip_d)

        @c.block.scalar
        def _(s, ldsn=lds.n):
            s.wait_ge(lds.h, ldsn)
            s.activation(c.smallf[0:1, 9:10], c.smallf[0:1, 8:9], AF.Sigmoid
                         ).then_inc(ac.h, 1)
        ac.n += 1

        @c.block.vector
        def _(v, acn=ac.n):
            v.wait_ge(ac.h, acn)
            st_ = c.chain(v)
            st_(v.tensor_reduce(c.smallf[0:1, 10:11], c.smallf[0:1, 0:8],
                                AX.X, ALU.max))
            st_(v.tensor_tensor(c.smallf[0:1, 11:12], c.smallf[0:1, 10:11],
                                c.smallf[0:1, 9:10], ALU.mult))
            st_(v.tensor_scalar(c.smallf[0:1, 12:13], c.smallf[0:1, 11:12],
                                1.0 / 127.0, None, ALU.mult))
            ins = v.reciprocal(c.smallf[0:1, 13:14], c.smallf[0:1, 12:13])
            dv.inc(ins)

        @c.block.tensor
        def _(t, dvn=dv.n):
            t.wait_ge(dv.h, dvn)
            t.matmul(c.psX[:, 0:1], c.ones1f[:], c.smallf[0:1, 13:14],
                     start=True, stop=True)
            t.matmul(c.psX[:, 1:2], c.ones1f[:], c.smallf[0:1, 12:13],
                     start=True, stop=True).then_inc(pe.h, 1)
        pe.n += 1

        @c.block.vector
        def _(v, pen=pe.n):
            v.wait_ge(pe.h, pen)
            st_ = c.chain(v)
            st_(v.tensor_copy(c.rsxb[:], c.psX[:, 0:1]))
            ins = v.tensor_copy(c.sxb[:], c.psX[:, 1:2])
            dv.inc(ins)
        c.barrier()

        wsrc = c.arena_f32[:, 0:1024]
        vt = c.arena_f32[:, 4096:5120]
        d1 = c.arena_f32[:, 5120:6144]
        d2 = c.arena_f32[:, 6144:7168]
        for wi, (wsrc_d, wdst, sw, sv) in enumerate(
                ((wg_d, wqg, c.swg, c.svg), (wu_d, wqu, c.swu, c.svu))):
            w3 = wsrc_d.rearrange("(n p) k -> p n k", p=128)
            for ch in range(IC):
                @c.block.sync
                def _(sync, w3=w3, ch=ch, dvn=dv.n):
                    sync.wait_ge(dv.h, dvn)
                    c.qdma(sync, ld, wsrc[:, 0:1024], w3[:, ch, :])

                @c.block.vector
                def _(v, ch=ch, sw=sw, ldn=ld.n, tpen=tpe.n):
                    v.wait_ge(ld.h, ldn)
                    v.wait_ge(tpe.h, tpen)
                    st_ = c.chain(v)
                    st_(v.tensor_reduce(sw[:, ch:ch + 1], wsrc[:, 0:1024],
                                        AX.X, ALU.max, apply_absolute_value=True))
                    st_(v.tensor_scalar(sw[:, ch:ch + 1], sw[:, ch:ch + 1],
                                        1.0 / 127.0, None, ALU.mult))
                    st_(v.reciprocal(d1[:, 0:1], sw[:, ch:ch + 1]))
                    st_(v.tensor_tensor(vt[:, 0:1024], wsrc[:, 0:1024],
                                        d1[:, 0:1].broadcast_to((128, 1024)),
                                        ALU.mult))
                    ins = _emit_round_halfup(c, v, vt[:, 0:1024],
                                             c.itile[:, 0:1024],
                                             d1[:, 0:1024], d2[:, 0:1024],
                                             1024, wtm[:, 0:1024], False)
                    ins.then_inc(dv.h, 1)
                dv.n += 1

                @c.block.tensor
                def _(t, dvn=dv.n, tps0=tps.n):
                    t.wait_ge(dv.h, dvn)
                    for g4 in range(2):
                        if tps0 + g4 >= 1:
                            t.wait_ge(tpe.h, tps0 + g4)
                        for j in range(4):
                            trc = g4 * 4 + j
                            ins = t.transpose(c.psT[:, j * 128:(j + 1) * 128],
                                              wtm[:, trc * 128:(trc + 1) * 128],
                                              c.identb[:])
                        ins.then_inc(tps.h, 1)

                @c.block.scalar
                def _(s, ch=ch, wdst=wdst, tps0=tps.n):
                    for g4 in range(2):
                        s.wait_ge(tps.h, tps0 + g4 + 1)
                        for j in range(4):
                            trc = g4 * 4 + j
                            ins = s.copy(
                                wdst[:, trc * I_FFN + ch * 128:
                                     trc * I_FFN + ch * 128 + 128],
                                c.psT[:, j * 128:(j + 1) * 128])
                        ins.then_inc(tpe.h, 1)
                tps.n += 2
                tpe.n += 2
        c.barrier()

        @c.block.vector
        def _(v):
            st_ = c.chain(v)
            st_(v.tensor_tensor(c.svg[:], c.swg[:],
                                c.sxb[:].broadcast_to((128, 22)), ALU.mult))
            st_(v.tensor_tensor(c.svu[:], c.swu[:],
                                c.sxb[:].broadcast_to((128, 22)), ALU.mult))
            st_(v.memset(c.am[:], 0.0))
        c.barrier()

        xf = c.arena_f32[:, 0:4096]
        for g in range(4):
            @c.block.sync
            def _(sync, g=g, dvn=dv.n):
                if g >= 1:
                    sync.wait_ge(dv.h, dvn)
                c.qdma(sync, ld, xf.rearrange("p (f t) -> p f t", f=8),
                       h13[:, :, g * 512:(g + 1) * 512])

            @c.block.vector
            def _(v, g=g, ldn=ld.n):
                v.wait_ge(ld.h, ldn)
                st_ = c.chain(v)
                for half in range(4):
                    seg = xf[:, half * 1024:(half + 1) * 1024]
                    st_(v.tensor_tensor(vt[:, 0:1024], seg,
                                        c.rsxb[:].broadcast_to((128, 1024)),
                                        ALU.mult))
                    ins = _emit_round_halfup(
                        c, v, vt[:, 0:1024], c.itile[:, 0:1024],
                        d1[:, 0:1024], d2[:, 0:1024], 1024,
                        wtm[:, 0:1024], True)
                    st_(ins)
                    for k in range(2):
                        fc = half * 2 + k
                        ins = v.tensor_copy(
                            hqf[:, fc * T + g * 512: fc * T + g * 512 + 512],
                            wtm[:, k * 512:(k + 1) * 512])
                        if not (half == 3 and k == 1):
                            st_(ins)
                ins.then_inc(dv.h, 1)
            dv.n += 1
            c.barrier()

        h2a = c.arena_f32[:, 0:512]
        h2b = c.arena_f32[:, 512:1024]
        sgt = c.arena_f32[:, 1024:1536]
        for ic in range(IC):
            for tb in range(4):
                idx = ic * 4 + tb

                @c.block.tensor
                def _(t, ic=ic, tb=tb, idx=idx, dvn=dv.n):
                    if idx >= 1:
                        t.wait_ge(dv.h, dvn)
                    for fc in range(FC):
                        t.matmul(c.psS1[:, 0:512],
                                 wqg[:, fc * I_FFN + ic * 128:
                                     fc * I_FFN + ic * 128 + 128],
                                 hqf[:, fc * T + tb * 512: fc * T + tb * 512 + 512],
                                 start=(fc == 0), stop=(fc == FC - 1))
                    for fc in range(FC):
                        ins = t.matmul(c.psS1[:, 512:1024],
                                       wqu[:, fc * I_FFN + ic * 128:
                                           fc * I_FFN + ic * 128 + 128],
                                       hqf[:, fc * T + tb * 512:
                                           fc * T + tb * 512 + 512],
                                       start=(fc == 0), stop=(fc == FC - 1))
                    ins.then_inc(pe.h, 1)
                pe.n += 1

                @c.block.scalar
                def _(s, ic=ic, pen=pe.n):
                    s.wait_ge(pe.h, pen)
                    s.activation(sgt, c.psS1[:, 0:512], AF.Silu,
                                 scale=c.svg[:, ic:ic + 1]).then_inc(ac.h, 1)
                ac.n += 1

                @c.block.vector
                def _(v, ic=ic, tb=tb, idx=idx, acn=ac.n, stn=st.n):
                    v.wait_ge(ac.h, acn)
                    if idx >= 2:
                        v.wait_ge(st.h, stn - 16)
                    tile = h2a if idx % 2 == 0 else h2b
                    st_ = c.chain(v)
                    st_(v.scalar_tensor_tensor(tile, c.psS1[:, 512:1024],
                                               c.svu[:, ic:ic + 1], sgt,
                                               ALU.mult, ALU.mult))
                    st_(v.tensor_reduce(d2[:, 0:1], tile, AX.X, ALU.max,
                                        apply_absolute_value=True))
                    v.tensor_tensor(c.am[:], c.am[:], d2[:, 0:1], ALU.max
                                    ).then_inc(dv.h, 1)
                dv.n += 1

                @c.block.gpsimd
                def _(g2, ic=ic, tb=tb, idx=idx, dvn=dv.n):
                    g2.wait_ge(dv.h, dvn)
                    tile = h2a if idx % 2 == 0 else h2b
                    c.qdma(g2, st, h23[:, ic, tb * 512:(tb + 1) * 512], tile)
        c.barrier()

        fin2 = c.sem("fin2")

        @c.block.tensor
        def _(t):
            t.transpose(c.psX[0:1, 0:128], c.am[:], c.identf[:]
                        ).then_inc(fin2.h, 1)

        @c.block.vector
        def _(v):
            v.wait_ge(fin2.h, 1)
            v.tensor_reduce(c.smallf[0:1, 14:15], c.psX[0:1, 0:128],
                            AX.X, ALU.max).then_inc(fin2.h, 1)

        @c.block.gpsimd
        def _(g):
            g.wait_ge(fin2.h, 2)
            c.qdma(g, c.sem("finst"), lm2_d, c.smallf[0:1, 14:15])
        c.barrier()
    return nc


def build_launchC():
    nc = bass.Bass("TRN2", target_bir_lowering=False)
    h2_d = nc.dram_tensor("h2", [I_FFN, T], F32, kind="ExternalInput").ap()
    mx_d = nc.dram_tensor("maxes2", [1, 8], F32, kind="ExternalInput").ap()
    clip_d = nc.dram_tensor("clipd", [1, 1], F32, kind="ExternalInput").ap()
    wd_d = nc.dram_tensor("l1wd", [D, I_FFN], F32, kind="ExternalInput").ap()
    x1_d = nc.dram_tensor("x1", [D, T], F32, kind="ExternalInput").ap()
    modg_d = nc.dram_tensor("modg", [128, 8], F32, kind="ExternalInput").ap()
    out_d = nc.dram_tensor("out", [D, T], F32, kind="ExternalOutput").ap()

    with ExitStack() as stack:
        c = Ctx(nc, stack, None)
        c.register_consts([1.0 / 127.0])
        c.arena_bf = c.sb("arena_bf", [128, 70400], BF16)[:]
        c.arena_f32 = c.sb("arena_f32", [128, 11776], F32)[:]
        c.itile = c.sb("itile", [128, 2816], mybir.dt.int32)[:]
        c.psS1 = c.psum("psS1", [128, 1024])[:]
        c.psX = c.psum("psX", [128, 512])[:]
        c.psT = c.psum("psT", [128, 1024], BF16)[:]
        c.smallf = c.sb("smallf", [1, 16], F32)[:]
        c.swd = c.sb("swd", [128, 8], F32)[:]
        c.coef = c.sb("coef", [128, 8], F32)[:]
        c.rsdb = c.sb("rsdb", [128, 1], F32)[:]
        c.sdb = c.sb("sdb", [128, 1], F32)[:]
        c.modg = c.sb("modgs", [128, 8], F32)[:]
        block = stack.enter_context(nc.Block())
        c.block = block

        h2qi = c.arena_bf[:, 0:IC * T]
        wdq = c.arena_bf[:, IC * T:IC * T + IC * 1024]
        wtm = c.arena_bf[:, IC * T + IC * 1024:IC * T + IC * 1024 + I_FFN]
        h23 = h2_d.rearrange("(i p) t -> p i t", p=128)
        x13 = x1_d.rearrange("(f p) t -> p f t", p=128)
        o3 = out_d.rearrange("(f p) t -> p f t", p=128)

        setup_common(c)
        ld = c.sem("ld")
        lds = c.sem("lds")
        pe = c.sem("pe")
        ac = c.sem("ac")
        dv = c.sem("dv")
        st = c.sem("st")
        tps = c.sem("tps")
        tpe = c.sem("tpe")

        @c.block.sync
        def _(sync):
            c.qdma(sync, lds, c.smallf[0:1, 0:8], mx_d)
            c.qdma(sync, lds, c.smallf[0:1, 8:9], clip_d)
            c.qdma(sync, lds, c.modg[:], modg_d)

        @c.block.scalar
        def _(s, ldsn=lds.n):
            s.wait_ge(lds.h, ldsn)
            s.activation(c.smallf[0:1, 9:10], c.smallf[0:1, 8:9], AF.Sigmoid
                         ).then_inc(ac.h, 1)
        ac.n += 1

        @c.block.vector
        def _(v, acn=ac.n):
            v.wait_ge(ac.h, acn)
            st_ = c.chain(v)
            st_(v.tensor_reduce(c.smallf[0:1, 10:11], c.smallf[0:1, 0:8],
                                AX.X, ALU.max))
            st_(v.tensor_tensor(c.smallf[0:1, 11:12], c.smallf[0:1, 10:11],
                                c.smallf[0:1, 9:10], ALU.mult))
            st_(v.tensor_scalar(c.smallf[0:1, 12:13], c.smallf[0:1, 11:12],
                                1.0 / 127.0, None, ALU.mult))
            ins = v.reciprocal(c.smallf[0:1, 13:14], c.smallf[0:1, 12:13])
            dv.inc(ins)

        @c.block.tensor
        def _(t, dvn=dv.n):
            t.wait_ge(dv.h, dvn)
            t.matmul(c.psX[:, 0:1], c.ones1f[:], c.smallf[0:1, 13:14],
                     start=True, stop=True)
            t.matmul(c.psX[:, 1:2], c.ones1f[:], c.smallf[0:1, 12:13],
                     start=True, stop=True).then_inc(pe.h, 1)
        pe.n += 1

        @c.block.vector
        def _(v, pen=pe.n):
            v.wait_ge(pe.h, pen)
            st_ = c.chain(v)
            st_(v.tensor_copy(c.rsdb[:], c.psX[:, 0:1]))
            ins = v.tensor_copy(c.sdb[:], c.psX[:, 1:2])
            dv.inc(ins)
        c.barrier()

        wsrc = c.arena_f32[:, 0:2816]
        vt = c.arena_f32[:, 2816:5632]
        dd1 = c.arena_f32[:, 5632:8448]
        dd2 = c.arena_f32[:, 8448:11264]
        w3 = wd_d.rearrange("(n p) k -> p n k", p=128)
        for ch in range(FC):
            @c.block.sync
            def _(sync, ch=ch, dvn=dv.n):
                sync.wait_ge(dv.h, dvn)
                c.qdma(sync, ld, wsrc[:, 0:I_FFN], w3[:, ch, :])

            @c.block.vector
            def _(v, ch=ch, ldn=ld.n, tpen=tpe.n):
                v.wait_ge(ld.h, ldn)
                v.wait_ge(tpe.h, tpen)
                st_ = c.chain(v)
                st_(v.tensor_reduce(c.swd[:, ch:ch + 1], wsrc[:, 0:I_FFN],
                                    AX.X, ALU.max, apply_absolute_value=True))
                st_(v.tensor_scalar(c.swd[:, ch:ch + 1], c.swd[:, ch:ch + 1],
                                    1.0 / 127.0, None, ALU.mult))
                st_(v.reciprocal(c.scr_small[:, 0:1], c.swd[:, ch:ch + 1]))
                st_(v.tensor_tensor(vt[:, 0:I_FFN], wsrc[:, 0:I_FFN],
                                    c.scr_small[:, 0:1].broadcast_to((128, I_FFN)),
                                    ALU.mult))
                ins = _emit_round_halfup(c, v, vt[:, 0:I_FFN],
                                         c.itile[:, 0:I_FFN],
                                         dd1[:, 0:I_FFN], dd2[:, 0:I_FFN],
                                         I_FFN, wtm[:, 0:I_FFN], False)
                ins.then_inc(dv.h, 1)
            dv.n += 1

            @c.block.tensor
            def _(t, dvn=dv.n, tps0=tps.n):
                t.wait_ge(dv.h, dvn)
                for g4 in range(6):
                    if tps0 + g4 >= 1:
                        t.wait_ge(tpe.h, tps0 + g4)
                    for j in range(min(4, 22 - g4 * 4)):
                        trc = g4 * 4 + j
                        ins = t.transpose(c.psT[:, j * 128:(j + 1) * 128],
                                          wtm[:, trc * 128:(trc + 1) * 128],
                                          c.identb[:])
                    ins.then_inc(tps.h, 1)

            @c.block.scalar
            def _(s, ch=ch, tps0=tps.n):
                for g4 in range(6):
                    s.wait_ge(tps.h, tps0 + g4 + 1)
                    for j in range(min(4, 22 - g4 * 4)):
                        trc = g4 * 4 + j
                        ins = s.copy(
                            wdq[:, trc * 1024 + ch * 128:
                                trc * 1024 + ch * 128 + 128],
                            c.psT[:, j * 128:(j + 1) * 128])
                    ins.then_inc(tpe.h, 1)
            tps.n += 6
            tpe.n += 6
        c.barrier()

        @c.block.vector
        def _(v):
            st_ = c.chain(v)
            st_(v.tensor_tensor(c.coef[:], c.swd[:],
                                c.sdb[:].broadcast_to((128, 8)), ALU.mult))
            st_(v.tensor_tensor(c.coef[:], c.coef[:], c.modg[:], ALU.mult))
        c.barrier()

        hsrc = c.arena_f32[:, 0:2048]
        vt2 = c.arena_f32[:, 2048:4096]
        for ch in range(IC):
            @c.block.sync
            def _(sync, ch=ch, dvn=dv.n):
                sync.wait_ge(dv.h, dvn)
                c.qdma(sync, ld, hsrc[:, 0:2048], h23[:, ch, :])

            @c.block.vector
            def _(v, ch=ch, ldn=ld.n):
                v.wait_ge(ld.h, ldn)
                st_ = c.chain(v)
                st_(v.tensor_tensor(vt2[:, 0:2048], hsrc[:, 0:2048],
                                    c.rsdb[:].broadcast_to((128, 2048)),
                                    ALU.mult))
                ins = _emit_round_halfup(c, v, vt2[:, 0:2048],
                                         c.itile[:, 0:2048],
                                         dd1[:, 0:2048], dd2[:, 0:2048],
                                         2048, h2qi[:, ch * T:(ch + 1) * T],
                                         True)
                ins.then_inc(dv.h, 1)
            dv.n += 1
            c.barrier()

        xw = c.arena_f32[:, 0:512]
        esg = c.arena_f32[:, 512:1024]
        xo = c.arena_f32[:, 1024:1536]
        for oc in range(FC):
            for tb in range(4):
                idx = oc * 4 + tb

                @c.block.tensor
                def _(t, oc=oc, tb=tb, idx=idx, dvn=dv.n):
                    if idx >= 2:
                        t.wait_ge(dv.h, dvn - 1)
                    slot = c.psS1[:, 0:512] if idx % 2 == 0 else c.psS1[:, 512:1024]
                    for ic in range(IC):
                        ins = t.matmul(slot,
                                       wdq[:, ic * 1024 + oc * 128:
                                           ic * 1024 + oc * 128 + 128],
                                       h2qi[:, ic * T + tb * 512:
                                            ic * T + tb * 512 + 512],
                                       start=(ic == 0), stop=(ic == IC - 1))
                    ins.then_inc(pe.h, 1)
                pe.n += 1

                @c.block.sync
                def _(sync, oc=oc, tb=tb, idx=idx, dvn=dv.n):
                    if idx >= 1:
                        sync.wait_ge(dv.h, dvn)
                    c.qdma(sync, lds, xw, x13[:, oc, tb * 512:(tb + 1) * 512])

                @c.block.vector
                def _(v, oc=oc, tb=tb, idx=idx, pen=pe.n, ldsn=lds.n, stn=st.n):
                    st_ = c.chain(v)
                    v.wait_ge(pe.h, pen)
                    v.wait_ge(lds.h, ldsn)
                    v.wait_ge(st.h, stn)
                    slot = c.psS1[:, 0:512] if idx % 2 == 0 else c.psS1[:, 512:1024]
                    st_(v.tensor_scalar(esg, slot, c.coef[:, oc:oc + 1], None,
                                        ALU.mult))
                    v.tensor_tensor(xo, xw, esg, ALU.add).then_inc(dv.h, 1)
                dv.n += 1

                @c.block.gpsimd
                def _(g, oc=oc, tb=tb, dvn=dv.n):
                    g.wait_ge(dv.h, dvn)
                    c.qdma(g, st, o3[:, oc, tb * 512:(tb + 1) * 512], xo)
        c.barrier()
    return nc



def kernel(**inp):
    common = _prep_common(inp)
    in_maps = []
    for core in range(8):
        m = dict(common)
        m.update(_prep_core(inp, core))
        in_maps.append(m)

    ncA = build_launchA(upto="full")
    resA = run_bass_kernel_spmd(ncA, in_maps, core_ids=list(range(8)))

    maxes = np.ascontiguousarray(
        np.stack([np.asarray(resA.results[cr]["lm"]).ravel()[0]
                  for cr in range(8)]).reshape(1, 8).astype(np.float32))
    clipx = np.asarray(inp["L1_ffn_clip_x"], np.float32).reshape(1, 1)
    wg1 = np.ascontiguousarray(np.asarray(inp["L1_ffn_w_gate"], np.float32))
    wu1 = np.ascontiguousarray(np.asarray(inp["L1_ffn_w_up"], np.float32))
    ncB = build_launchB()
    in_mapsB = [dict(h1=np.asarray(resA.results[cr]["h1"]),
                     maxes=maxes, clipx=clipx, l1wg=wg1, l1wu=wu1)
                for cr in range(8)]
    resB = run_bass_kernel_spmd(ncB, in_mapsB, core_ids=list(range(8)))

    maxes2 = np.ascontiguousarray(
        np.stack([np.asarray(resB.results[cr]["lm2"]).ravel()[0]
                  for cr in range(8)]).reshape(1, 8).astype(np.float32))
    clipd = np.asarray(inp["L1_ffn_clip_down"], np.float32).reshape(1, 1)
    wd1 = np.ascontiguousarray(np.asarray(inp["L1_ffn_w_down"], np.float32))
    ncC = build_launchC()
    in_mapsC = [dict(h2=np.asarray(resB.results[cr]["h2"]),
                     maxes2=maxes2, clipd=clipd, l1wd=wd1,
                     x1=np.asarray(resA.results[cr]["x1"]),
                     modg=np.asarray(resA.results[cr]["modg"]))
                for cr in range(8)]
    resC = run_bass_kernel_spmd(ncC, in_mapsC, core_ids=list(range(8)))

    out = np.stack([np.ascontiguousarray(
        np.asarray(resC.results[cr]["out"]).T) for cr in range(8)])
    return out.astype(np.float32)
